# version 01
# baseline (speedup 1.0000x reference)
import numpy as np
import ml_dtypes
from contextlib import ExitStack

import concourse.bass as bass
import concourse.mybir as mybir
from concourse.bass_utils import run_bass_kernel_spmd
from concourse.bass import IndirectOffsetOnAxis

F32 = mybir.dt.float32
BF16 = mybir.dt.bfloat16
I32 = mybir.dt.int32
AF = mybir.ActivationFunctionType
ALU = mybir.AluOpType
AX = mybir.AxisListType
BF16NP = ml_dtypes.bfloat16

NCORES = 8
B, T, D = 2, 8192, 1024
TPC = 2048
NT = TPC // 128
NBLK = 128
NH, HS = 16, 64
NE, CAP = 8, 5120
FFN_H = 4096
RK = 16
BIG = 1.0e9
NEG = -1.0e30


def own_blocks(c):
    return [(c // 4) * 64 + (c % 4) + 4 * i for i in range(NT)]


def core_rows(c):
    bl = np.array(own_blocks(c))
    return (bl[:, None] * 128 + np.arange(128)[None, :]).reshape(-1)



class S:

    def __init__(self, nc, ctx, name):
        self.h = ctx.enter_context(nc.semaphore(name))
        self.n = 0

    def inc(self, instr, k=1):
        instr.then_inc(self.h, k)
        self.n += k
        return instr

    def wait(self, eng):
        if self.n:
            eng.wait_ge(self.h, self.n)


class Env:
    def __init__(self, nc, ctx):
        self.nc, self.ctx = nc, ctx
        self.block = ctx.enter_context(nc.Block())
        self.ld = S(nc, ctx, "ld")
        self.ts = S(nc, ctx, "ts")
        self.vs = S(nc, ctx, "vs")
        self.ss = S(nc, ctx, "ss")
        self.gs = S(nc, ctx, "gs")
        self.cc = S(nc, ctx, "cc")
        self._consts = {}

    def sb(self, name, shape, dt):
        return self.ctx.enter_context(self.nc.sbuf_tensor(name, shape, dt))

    def ps(self, name, shape, dt=F32):
        return self.ctx.enter_context(self.nc.psum_tensor(name, shape, dt))

    def const_ident(self, dt):
        key = ("ident", dt)
        if key not in self._consts:
            t = self.sb(f"ident{'32' if dt == F32 else '16'}", [128, 128], dt)

            @self.block.gpsimd
            def _(g):
                self.gs.inc(g.memset(t[:], 1.0))
                self.gs.wait(g)
                self.gs.inc(g.affine_select(t[:], t[:], [[1, 128]], ALU.is_equal,
                                            0.0, channel_multiplier=-1))
            self._consts[key] = t
        return self._consts[key]

    def const_ltri(self):
        if "ltri" not in self._consts:
            t = self.sb("ltri", [128, 128], F32)

            @self.block.gpsimd
            def _(g):
                self.gs.inc(g.memset(t[:], 1.0))
                self.gs.wait(g)
                self.gs.inc(g.affine_select(t[:], t[:], [[1, 128]], ALU.is_gt,
                                            0.0, channel_multiplier=-1))
            self._consts["ltri"] = t
        return self._consts["ltri"]

    def const_iota8(self):
        if "iota8" not in self._consts:
            t = self.sb("iota8", [128, 8], F32)

            @self.block.gpsimd
            def _(g):
                self.gs.inc(g.iota(t[:], [[1, 8]], channel_multiplier=0,
                                   allow_small_or_imprecise_dtypes=True))
            self._consts["iota8"] = t
        return self._consts["iota8"]

    def const_iotap(self):
        if "iotap" not in self._consts:
            t = self.sb("iotap", [128, 1], F32)

            @self.block.gpsimd
            def _(g):
                self.gs.inc(g.iota(t[:], [[0, 1]], channel_multiplier=1,
                                   allow_small_or_imprecise_dtypes=True))
            self._consts["iotap"] = t
        return self._consts["iotap"]

    def const_ones(self, n=128):
        key = ("ones", n)
        if key not in self._consts:
            t = self.sb(f"ones{n}", [n, 1], F32)

            @self.block.gpsimd
            def _(g):
                self.gs.inc(g.memset(t[:], 1.0))
            self._consts[key] = t
        return self._consts[key]


def emit_ln_tile(env, x_ap, outs, mv, bn):
    blk = env.block

    @blk.vector
    def _(v):
        env.vs.inc(v.bn_stats(bn[:, 0:6], x_ap[:, 0:512]))
        env.vs.inc(v.bn_stats(bn[:, 6:12], x_ap[:, 512:1024]))
        env.vs.wait(v)
        env.vs.inc(v.bn_aggr(mv[:, 0:2], bn[:, 0:12]))
        env.vs.wait(v)
        env.vs.inc(v.tensor_scalar(mv[:, 2:3], mv[:, 1:2], 1e-5, None, ALU.add))

    @blk.scalar
    def _(s):
        env.vs.wait(s)
        env.ss.inc(s.activation(mv[:, 2:3], mv[:, 2:3], AF.Ln))
        env.ss.wait(s)
        env.ss.inc(s.activation(mv[:, 3:4], mv[:, 2:3], AF.Exp, scale=-0.5))

    @blk.vector
    def _(v):
        env.ss.wait(v)
        for o in outs:
            env.vs.inc(v.tensor_scalar(o, x_ap, mv[:, 0:1], mv[:, 3:4],
                                       ALU.subtract, ALU.mult))



def build_route():
    nc = bass.Bass("TRN2", target_bir_lowering=False)
    x = nc.dram_tensor("x", [TPC, D], F32, kind="ExternalInput").ap()
    wg = nc.dram_tensor("wg", [NE, D], F32, kind="ExternalInput").ap()
    xn_out = nc.dram_tensor("xn", [TPC, D], BF16, kind="ExternalOutput").ap()
    meta = nc.dram_tensor("meta", [TPC, 6], F32, kind="ExternalOutput").ap()
    tsum = nc.dram_tensor("tsum", [1, NT * NE], F32, kind="ExternalOutput").ap()
    with ExitStack() as ctx:
        env = Env(nc, ctx)
        blk = env.block
        ident32 = env.const_ident(F32)
        ltri = env.const_ltri()
        iota8 = env.const_iota8()
        ones128 = env.const_ones()

        xt = env.sb("xt", [128, 1024], F32)
        xnf = env.sb("xnf", [128, 1024], F32)
        xnb = env.sb("xnb", [128, 1024], BF16)
        xc = env.sb("xc", [128, 1024], F32)
        wgt = env.sb("wgt", [NE, 1024], F32)
        wgT = env.sb("wgT", [128, 64], F32)
        logit = env.sb("logit", [128, 8], F32)
        l2 = env.sb("l2", [128, 8], F32)
        m1 = env.sb("m1", [128, 8], F32)
        m2 = env.sb("m2", [128, 8], F32)
        twoh = env.sb("twoh", [128, 8], F32)
        csum = env.sb("csum", [128, 8], F32)
        tmp8 = env.sb("tmp8", [128, 8], F32)
        idx8 = env.sb("idx8", [128, 8], mybir.dt.uint32)
        r = env.sb("r", [128, 12], F32)
        bn = env.sb("bn", [128, 12], F32)
        mv = env.sb("mv", [128, 4], F32)
        sums = env.sb("sums", [1, 128], F32)
        metat = env.sb("metat", [128, 6], F32)

        psT = [env.ps("psT0", [128, 128], F32), env.ps("psT1", [128, 128], F32)]
        psW = env.ps("psW", [128, 64], F32)
        psL = env.ps("psL", [128, 8], F32)
        psC = env.ps("psC", [128, 8], F32)
        psS = env.ps("psS", [1, 8], F32)

        @blk.sync
        def _(sync):
            env.ld.inc(sync.dma_start(out=wgt[:], in_=wg), 16)

        @blk.tensor
        def _(t):
            env.ld.wait(t)
            env.gs.wait(t)
            for c in range(8):
                env.ts.inc(t.transpose(psW[:, c * 8:(c + 1) * 8],
                                       wgt[:, c * 128:(c + 1) * 128], ident32[0:8, 0:8]))

        @blk.vector
        def _(v):
            env.ts.wait(v)
            env.vs.inc(v.tensor_copy(wgT[:], psW[:]))

        for i in range(NT):
            @blk.sync
            def _(sync, i=i):
                env.vs.wait(sync)
                env.ld.inc(sync.dma_start(out=xt[:], in_=x[i * 128:(i + 1) * 128, :]), 16)

            @blk.vector
            def _(v, i=i):
                env.ld.wait(v)
            emit_ln_tile(env, xt[:], [xnf[:], xnb[:]], mv, bn)

            for c in range(8):
                @blk.tensor
                def _(t, c=c):
                    env.vs.wait(t)
                    env.ts.inc(t.transpose(psT[c % 2][:], xnf[:, c * 128:(c + 1) * 128],
                                           ident32[:]))

                @blk.vector
                def _(v, c=c):
                    env.ts.wait(v)
                    env.vs.inc(v.tensor_copy(xc[:, c * 128:(c + 1) * 128], psT[c % 2][:]))

            @blk.tensor
            def _(t, i=i):
                env.vs.wait(t)
                for c in range(8):
                    mm = t.matmul(psL[:], xc[:, c * 128:(c + 1) * 128],
                                  wgT[:, c * 8:(c + 1) * 8],
                                  start=(c == 0), stop=(c == 7))
                env.ts.inc(mm)

            @blk.vector
            def _(v, i=i):
                env.ts.wait(v)
                env.vs.inc(v.tensor_copy(logit[:], psL[:]))
                env.vs.wait(v)
                env.vs.inc(v.max(l2[:], logit[:]))
                env.vs.wait(v)
                env.vs.inc(v.max_index(idx8[:], l2[:], logit[:]))
                env.vs.wait(v)
                env.vs.inc(v.tensor_copy(metat[:, 0:1], idx8[:, 0:1]))
                env.vs.inc(v.tensor_copy(metat[:, 1:2], idx8[:, 1:2]))
                env.vs.wait(v)
                env.vs.inc(v.tensor_scalar(m1[:], iota8[:], metat[:, 0:1], None, ALU.is_equal))
                env.vs.inc(v.tensor_scalar(m2[:], iota8[:], metat[:, 1:2], None, ALU.is_equal))
                env.vs.inc(v.tensor_sub(r[:, 2:3], l2[:, 1:2], l2[:, 0:1]))

            @blk.scalar
            def _(s, i=i):
                env.vs.wait(s)
                env.ss.inc(s.activation(r[:, 3:4], r[:, 2:3], AF.Exp))

            @blk.vector
            def _(v, i=i):
                env.ss.wait(v)
                env.vs.inc(v.tensor_scalar(r[:, 4:5], r[:, 3:4], 1.0, None, ALU.add))
                env.vs.wait(v)
                env.vs.inc(v.reciprocal(metat[:, 2:3], r[:, 4:5]))
                env.vs.wait(v)
                env.vs.inc(v.tensor_mul(metat[:, 3:4], r[:, 3:4], metat[:, 2:3]))
                env.vs.inc(v.tensor_add(twoh[:], m1[:], m2[:]))

            @blk.tensor
            def _(t, i=i):
                env.vs.wait(t)
                env.ts.inc(t.matmul(psC[:], ltri[:], twoh[:], start=True, stop=True))
                env.ts.inc(t.matmul(psS[:], ones128[:], twoh[:], start=True, stop=True))

            @blk.vector
            def _(v, i=i):
                env.ts.wait(v)
                env.vs.inc(v.tensor_copy(csum[:], psC[:]))
                env.vs.inc(v.tensor_copy(sums[0:1, i * 8:(i + 1) * 8], psS[:]))
                env.vs.wait(v)
                env.vs.inc(v.tensor_mul(tmp8[:], m1[:], csum[:]))
                env.vs.wait(v)
                env.vs.inc(v.reduce_sum(metat[:, 4:5], tmp8[:], axis=AX.X))
                env.vs.inc(v.tensor_mul(tmp8[:], m2[:], csum[:]))
                env.vs.wait(v)
                env.vs.inc(v.reduce_sum(metat[:, 5:6], tmp8[:], axis=AX.X))

            @blk.sync
            def _(sync, i=i):
                env.vs.wait(sync)
                env.ld.inc(sync.dma_start(out=xn_out[i * 128:(i + 1) * 128, :], in_=xnb[:]), 16)
                env.ld.inc(sync.dma_start(out=meta[i * 128:(i + 1) * 128, :], in_=metat[:]), 16)

        @blk.sync
        def _(sync):
            env.vs.wait(sync)
            env.ld.inc(sync.dma_start(out=tsum, in_=sums[0:1, :]), 16)
            env.ld.wait(sync)
    return nc



def build_moe_ffn():
    nc = bass.Bass("TRN2", target_bir_lowering=False)
    xn_all = nc.dram_tensor("xn_all", [B * T, D], BF16, kind="ExternalInput").ap()
    meta = nc.dram_tensor("meta", [B * T, 6], F32, kind="ExternalInput").ap()
    tsg = nc.dram_tensor("tsg", [NBLK, NE], F32, kind="ExternalInput").ap()
    w1 = nc.dram_tensor("w1", [D, 2 * 2048], F32, kind="ExternalInput").ap()
    w2 = nc.dram_tensor("w2", [2048, D], F32, kind="ExternalInput").ap()
    cid = nc.dram_tensor("cid", [1, 1], F32, kind="ExternalInput").ap()
    contrib = nc.dram_tensor("contrib", [CAP, D], F32, kind="ExternalOutput").ap()
    sall = nc.dram_tensor("sall", [B * T, 2], F32, kind="ExternalOutput").ap()

    offs_dram = nc.dram_tensor("offs_dram", [NBLK, NE], F32)
    tdram = nc.dram_tensor("tdram", [CAP + 128, 2], F32)

    with ExitStack() as ctx:
        env = Env(nc, ctx)
        blk = env.block
        ltri = env.const_ltri()
        iota8 = env.const_iota8()
        iotap = env.const_iotap()
        ident16 = env.const_ident(BF16)

        tsg_sb = env.sb("tsg_sb", [128, 8], F32)
        offs_sb = env.sb("offs_sb", [128, 8], F32)
        OFFB = env.sb("OFFB", [128, 1024], F32)
        cidb = env.sb("cidb", [128, 1], F32)
        mt = [env.sb(f"mt{k}", [128, 6], F32) for k in range(2)]
        st = [env.sb(f"st{k}", [128, 2], F32) for k in range(2)]
        r = env.sb("rm", [128, 16], F32)
        msk = env.sb("msk", [128, 8], F32)
        vals = env.sb("vals", [128, 2], F32)
        scidx = env.sb("scidx", [128, 1], I32)
        init80 = env.sb("init80", [128, 82], F32)
        tg = env.sb("tg", [128, 80], F32)
        gidx = env.sb("gidx", [128, 4], I32)
        rg = env.sb("rg", [128, 4], F32)
        gx = [env.sb(f"gx{a}", [128, 1024], BF16) for a in range(4)]
        xfm = [env.sb(f"xfm{cc}", [128, 512], BF16) for cc in range(8)]
        w1b = [env.sb(f"w1b{cc}", [128, 4096], BF16) for cc in range(8)]
        w2b = [env.sb(f"w2b{ii}", [128, 1024], BF16) for ii in range(16)]
        haf = [env.sb(f"haf{ii}", [128, 512], BF16) for ii in range(16)]
        gel = env.sb("gel", [128, 512], F32)
        ct = env.sb("ct", [128, 1024], F32)

        psO = env.ps("psO", [128, 8], F32)
        psT = [env.ps("psT0", [128, 1024], BF16), env.ps("psT1", [128, 1024], BF16)]
        psA = env.ps("psA", [128, 512], F32)
        psB = env.ps("psB", [128, 512], F32)
        psC = env.ps("psC", [128, 512], F32)
        psD = env.ps("psD", [128, 512], F32)

        @blk.sync
        def _(sync):
            env.ld.inc(sync.dma_start(out=tsg_sb[:], in_=tsg), 16)
            env.ld.inc(sync.dma_start(
                out=cidb[:], in_=cid.unsqueeze(0).broadcast_to([128, 1, 1]).rearrange("p a b -> p (a b)")), 16)

        @blk.gpsimd
        def _(g):
            for c8 in range(8):
                env.ld.inc(g.dma_start(out=w1b[c8][:], in_=w1[c8 * 128:(c8 + 1) * 128, :]), 16)
            for i16 in range(16):
                env.ld.inc(g.dma_start(out=w2b[i16][:], in_=w2[i16 * 128:(i16 + 1) * 128, :]), 16)

        @blk.tensor
        def _(t):
            env.ld.wait(t)
            env.gs.wait(t)
            env.ts.inc(t.matmul(psO[:], ltri[:], tsg_sb[:], start=True, stop=True))

        @blk.vector
        def _(v):
            env.ts.wait(v)
            env.vs.inc(v.tensor_copy(offs_sb[:], psO[:]))
            env.vs.inc(v.memset(init80[:], BIG))
            env.vs.wait(v)
            env.vs.inc(v.memset(init80[:].rearrange("p (a w) -> p a w", w=2)[:, :, 1:2], 0.0))

        @blk.sync
        def _(sync):
            env.vs.wait(sync)
            env.ld.inc(sync.dma_start(out=offs_dram.ap(), in_=offs_sb[:]), 16)
            env.ld.inc(sync.dma_start(
                out=tdram.ap().rearrange("(p a) w -> p (a w)", p=128), in_=init80[:]), 16)
            env.ld.wait(sync)
            env.ld.inc(sync.dma_start(
                out=OFFB[:].rearrange("p (j e) -> p j e", e=8),
                in_=offs_dram.ap().unsqueeze(0).broadcast_to([128, NBLK, NE])), 16)

        for j in range(NBLK):
            buf = j % 2

            @blk.sync
            def _(sync, j=j, buf=buf):
                env.vs.wait(sync)
                env.ld.inc(sync.dma_start(out=mt[buf][:], in_=meta[j * 128:(j + 1) * 128, :]), 16)

            for k in range(2):
                @blk.vector
                def _(v, j=j, k=k, buf=buf):
                    env.ld.wait(v)
                    ecol, pcol, gcol = mt[buf][:, k:k + 1], mt[buf][:, 4 + k:5 + k], mt[buf][:, 2 + k:3 + k]
                    env.vs.inc(v.tensor_scalar(msk[:], iota8[:], ecol, None, ALU.is_equal))
                    env.vs.wait(v)
                    env.vs.inc(v.tensor_mul(msk[:], msk[:], OFFB[:, j * 8:(j + 1) * 8]))
                    env.vs.wait(v)
                    env.vs.inc(v.reduce_sum(r[:, 0:1], msk[:], axis=AX.X))
                    env.vs.wait(v)
                    env.vs.inc(v.tensor_add(r[:, 1:2], r[:, 0:1], pcol))
                    env.vs.wait(v)
                    env.vs.inc(v.tensor_scalar(r[:, 2:3], r[:, 1:2], float(CAP), None, ALU.is_lt))
                    env.vs.wait(v)
                    env.vs.inc(v.tensor_mul(vals[:, 1:2], gcol, r[:, 2:3]))
                    env.vs.inc(v.tensor_scalar(r[:, 3:4], r[:, 2:3], -1.0, -BIG, ALU.add, ALU.mult))
                    env.vs.wait(v)
                    env.vs.inc(v.tensor_scalar(r[:, 4:5], ecol, float(CAP), None, ALU.mult))
                    env.vs.wait(v)
                    env.vs.inc(v.tensor_add(r[:, 5:6], r[:, 4:5], r[:, 1:2]))
                    env.vs.wait(v)
                    env.vs.inc(v.tensor_add(st[buf][:, k:k + 1], r[:, 5:6], r[:, 3:4]))
                    env.vs.inc(v.tensor_tensor(r[:, 6:7], ecol, cidb[:], ALU.not_equal))
                    env.vs.wait(v)
                    env.vs.inc(v.scalar_tensor_tensor(r[:, 7:8], r[:, 6:7], BIG, r[:, 1:2], ALU.mult, ALU.add))
                    env.vs.wait(v)
                    env.vs.inc(v.tensor_scalar(r[:, 7:8], r[:, 7:8], float(CAP), None, ALU.min))
                    env.vs.wait(v)
                    env.vs.inc(v.tensor_copy(scidx[:], r[:, 7:8]))
                    env.vs.inc(v.tensor_scalar(vals[:, 0:1], iotap[:], float(j * 128), None, ALU.add))

                @blk.gpsimd
                def _(g, j=j, k=k, buf=buf):
                    env.vs.wait(g)
                    env.ld.inc(g.indirect_dma_start(
                        out=tdram.ap(), out_offset=IndirectOffsetOnAxis(scidx[:, :], 0),
                        in_=vals[:, :], in_offset=None), 16)

            @blk.sync
            def _(sync, j=j, buf=buf):
                env.vs.wait(sync)
                env.ld.inc(sync.dma_start(out=sall[j * 128:(j + 1) * 128, :], in_=st[buf][:]), 16)

        @blk.sync
        def _(sync):
            env.ld.wait(sync)
            env.ld.inc(sync.dma_start(
                out=tg[:].rearrange("p (a w) -> p a w", w=2),
                in_=tdram.ap()[0:CAP, :].rearrange("(a p) w -> p a w", p=128)), 16)

        for G in range(10):
            @blk.vector
            def _(v, G=G):
                env.ld.wait(v)
                env.ts.wait(v)
                for a4 in range(4):
                    a = G * 4 + a4
                    env.vs.inc(v.tensor_scalar(rg[:, a4:a4 + 1], tg[:, 2 * a:2 * a + 1],
                                               float(B * T - 1), None, ALU.min))
                env.vs.wait(v)
                env.vs.inc(v.tensor_copy(gidx[:], rg[:, 0:4]))

            @blk.gpsimd
            def _(g, G=G):
                env.vs.wait(g)
                for a4 in range(4):
                    env.ld.inc(g.indirect_dma_start(
                        out=gx[a4][:, :], out_offset=None,
                        in_=xn_all, in_offset=IndirectOffsetOnAxis(gidx[:, a4:a4 + 1], 0)), 16)

            for c8 in range(8):
                @blk.tensor
                def _(t, G=G, c8=c8):
                    env.ld.wait(t)
                    env.vs.wait(t)
                    for a4 in range(4):
                        env.ts.inc(t.transpose(psT[c8 % 2][:, a4 * 128:(a4 + 1) * 128],
                                               gx[a4][:, c8 * 128:(c8 + 1) * 128], ident16[:]))

                @blk.vector
                def _(v, G=G, c8=c8):
                    env.ts.wait(v)
                    env.vs.inc(v.tensor_copy(xfm[c8][:], psT[c8 % 2][:, 0:512]))

            for fi in range(16):
                @blk.tensor
                def _(t, G=G, fi=fi):
                    env.vs.wait(t)
                    env.ss.wait(t)
                    for c8 in range(8):
                        ma = t.matmul(psA[:], w1b[c8][:, fi * 128:(fi + 1) * 128],
                                      xfm[c8][:], start=(c8 == 0), stop=(c8 == 7))
                        mb = t.matmul(psB[:], w1b[c8][:, 2048 + fi * 128:2048 + (fi + 1) * 128],
                                      xfm[c8][:], start=(c8 == 0), stop=(c8 == 7))
                    env.ts.inc(ma)
                    env.ts.inc(mb)

                @blk.scalar
                def _(s, G=G, fi=fi):
                    env.ts.wait(s)
                    env.ss.inc(s.activation(gel[:], psA[:], AF.Gelu))

                @blk.vector
                def _(v, G=G, fi=fi):
                    env.ss.wait(v)
                    env.vs.inc(v.tensor_mul(haf[fi][:], gel[:], psB[:]))

            for m in range(4):
                @blk.tensor
                def _(t, G=G, m=m):
                    env.vs.wait(t)
                    for i16 in range(16):
                        mc = t.matmul(psC[:], haf[i16][:, m * 128:(m + 1) * 128],
                                      w2b[i16][:, 0:512], start=(i16 == 0), stop=(i16 == 15))
                        md = t.matmul(psD[:], haf[i16][:, m * 128:(m + 1) * 128],
                                      w2b[i16][:, 512:1024], start=(i16 == 0), stop=(i16 == 15))
                    env.ts.inc(mc)
                    env.ts.inc(md)

                @blk.vector
                def _(v, G=G, m=m):
                    env.ts.wait(v)
                    env.ld.wait(v)
                    wcol = tg[:, 2 * (G * 4 + m) + 1:2 * (G * 4 + m) + 2]
                    env.vs.inc(v.tensor_scalar(ct[:, 0:512], psC[:], wcol, None, ALU.mult))
                    env.vs.inc(v.tensor_scalar(ct[:, 512:1024], psD[:], wcol, None, ALU.mult))

                @blk.sync
                def _(sync, G=G, m=m):
                    env.vs.wait(sync)
                    row0 = G * 512 + m * 128
                    env.ld.inc(sync.dma_start(out=contrib[row0:row0 + 128, :], in_=ct[:]), 16)

        @blk.sync
        def _(sync):
            env.ld.wait(sync)
    return nc



def build_moe_combine():
    nc = bass.Bass("TRN2", target_bir_lowering=False)
    x = nc.dram_tensor("x", [TPC, D], F32, kind="ExternalInput").ap()
    call = nc.dram_tensor("call", [NE * CAP + 1, D], F32, kind="ExternalInput").ap()
    sown = nc.dram_tensor("sown", [TPC, 2], F32, kind="ExternalInput").ap()
    xo = nc.dram_tensor("xo", [TPC, D], F32, kind="ExternalOutput").ap()
    with ExitStack() as ctx:
        env = Env(nc, ctx)
        blk = env.block
        xt = env.sb("xt", [128, 1024], F32)
        sot = env.sb("sot", [128, 2], F32)
        sidx = env.sb("sidx", [128, 2], I32)
        g1 = env.sb("g1", [128, 1024], F32)
        g2 = env.sb("g2", [128, 1024], F32)

        for i in range(NT):
            @blk.sync
            def _(sync, i=i):
                env.vs.wait(sync)
                env.ld.wait(sync)
                env.ld.inc(sync.dma_start(out=xt[:], in_=x[i * 128:(i + 1) * 128, :]), 16)
                env.ld.inc(sync.dma_start(out=sot[:], in_=sown[i * 128:(i + 1) * 128, :]), 16)

            @blk.vector
            def _(v, i=i):
                env.ld.wait(v)
                env.vs.inc(v.tensor_scalar(sot[:], sot[:], float(NE * CAP), None, ALU.min))
                env.vs.wait(v)
                env.vs.inc(v.tensor_copy(sidx[:], sot[:]))

            @blk.gpsimd
            def _(g, i=i):
                env.vs.wait(g)
                env.ld.inc(g.indirect_dma_start(
                    out=g1[:, :], out_offset=None, in_=call,
                    in_offset=IndirectOffsetOnAxis(sidx[:, 0:1], 0)), 16)
                env.ld.inc(g.indirect_dma_start(
                    out=g2[:, :], out_offset=None, in_=call,
                    in_offset=IndirectOffsetOnAxis(sidx[:, 1:2], 0)), 16)

            @blk.vector
            def _(v, i=i):
                env.ld.wait(v)
                env.vs.inc(v.scalar_tensor_tensor(xt[:], g1[:], 0.5, xt[:], ALU.mult, ALU.add))
                env.vs.wait(v)
                env.vs.inc(v.scalar_tensor_tensor(xt[:], g2[:], 0.5, xt[:], ALU.mult, ALU.add))

            @blk.sync
            def _(sync, i=i):
                env.vs.wait(sync)
                env.ld.inc(sync.dma_start(out=xo[i * 128:(i + 1) * 128, :], in_=xt[:]), 16)

        @blk.sync
        def _(sync):
            env.ld.wait(sync)
    return nc



def _emit_ternarize(env, w_ap, s_sb, si_sb, t_sb, out_ap, free):
    blk = env.block

    @blk.vector
    def _(v):
        env.ld.wait(v)
        env.vs.inc(v.reduce_sum(s_sb, w_ap, axis=AX.X, apply_absolute_value=True))
        env.vs.wait(v)
        env.vs.inc(v.tensor_scalar(s_sb, s_sb, 1.0 / free, 1e-5, ALU.mult, ALU.max))
        env.vs.wait(v)
        env.vs.inc(v.reciprocal(si_sb, s_sb))
        env.vs.wait(v)
        env.vs.inc(v.tensor_scalar(t_sb, w_ap, si_sb, None, ALU.mult))
        env.vs.wait(v)
        env.vs.inc(v.tensor_scalar(out_ap, t_sb, 0.5, None, ALU.is_gt))
        env.vs.inc(v.tensor_scalar(t_sb, t_sb, -0.5, None, ALU.is_lt))
        env.vs.wait(v)
        env.vs.inc(v.tensor_tensor(t_sb, out_ap, t_sb, ALU.subtract))
        env.vs.wait(v)
        env.vs.inc(v.tensor_scalar(out_ap, t_sb, s_sb, None, ALU.mult))


def build_ffn1():
    nc = bass.Bass("TRN2", target_bir_lowering=False)
    x = nc.dram_tensor("x", [TPC, D], F32, kind="ExternalInput").ap()
    wfc = nc.dram_tensor("wfc", [FFN_H, D], F32, kind="ExternalInput").ap()
    wpj = nc.dram_tensor("wpj", [D, FFN_H], F32, kind="ExternalInput").ap()
    uu = nc.dram_tensor("uu", [FFN_H, RK], F32, kind="ExternalInput").ap()
    vv = nc.dram_tensor("vv", [RK, D], F32, kind="ExternalInput").ap()
    fb = nc.dram_tensor("fb", [RK, RK], F32, kind="ExternalInput").ap()
    gate = nc.dram_tensor("gate", [1, 1], F32, kind="ExternalInput").ap()
    xo = nc.dram_tensor("xo", [TPC, D], F32, kind="ExternalOutput").ap()
    with ExitStack() as ctx:
        env = Env(nc, ctx)
        blk = env.block
        ident32 = env.const_ident(F32)
        ident16 = env.const_ident(BF16)

        wrow = env.sb("wrow", [128, FFN_H], F32)
        weff = env.sb("weff", [128, FFN_H], BF16)
        tsc = env.sb("tsc", [128, FFN_H], F32)
        s_sb = env.sb("s_sb", [128, 1], F32)
        si_sb = env.sb("si_sb", [128, 1], F32)
        wfct = [env.sb(f"wfct{c}", [128, FFN_H], BF16) for c in range(8)]
        wpjt = [env.sb(f"wpjt{i}", [128, D], BF16) for i in range(32)]
        vTb = env.sb("vTb", [128, 8 * RK], BF16)
        uT = env.sb("uT", [RK, FFN_H], BF16)
        fbT = env.sb("fbT", [RK, RK], BF16)
        fbsb = env.sb("fbsb", [RK, RK], F32)
        gb = env.sb("gb", [128, 1], F32)
        xt = env.sb("xt_f", [128, 1024], F32)
        xnb = env.sb("xnb_f", [128, 1024], BF16)
        bn = env.sb("bn_f", [128, 12], F32)
        mv = env.sb("mv_f", [128, 4], F32)
        xfm = [env.sb(f"xfmf{c}", [128, 128], BF16) for c in range(8)]
        t1s = env.sb("t1s", [RK, 128], BF16)
        t2s = env.sb("t2s", [RK, 128], BF16)
        hs = env.sb("hs", [128, 128], F32)
        hf = env.sb("hf", [128, 128], F32)
        rr = [env.sb(f"rr{c}", [128, 128], BF16) for c in range(32)]

        psT = [env.ps("psT0", [128, 1024], BF16), env.ps("psT1", [128, 1024], BF16)]
        psF = env.ps("psF", [128, 128], F32)
        psL = env.ps("psL", [128, 128], F32)
        ps1 = env.ps("ps1", [RK, 128], F32)
        psP = [env.ps("psP0", [128, 512], F32), env.ps("psP1", [128, 512], F32)]

        @blk.sync
        def _(sync):
            env.ld.inc(sync.dma_start(out=fbsb[:], in_=fb), 16)
            env.ld.inc(sync.dma_start(
                out=gb[:], in_=gate.unsqueeze(0).broadcast_to([128, 1, 1]).rearrange("p a b -> p (a b)")), 16)

        @blk.gpsimd
        def _(g):
            env.ld.inc(g.dma_start(out=wrow[:, 0:RK * 32].rearrange("p (a r) -> p a r", r=RK),
                                   in_=uu.rearrange("(a p) r -> p a r", p=128)), 16)
            env.ld.inc(g.dma_start(out=tsc[0:RK, 0:1024], in_=vv), 16)

        @blk.tensor
        def _(t):
            env.ld.wait(t)
            env.gs.wait(t)
            env.ts.inc(t.transpose(ps1[:, 0:RK], fbsb[:, :], ident32[0:RK, 0:RK]))

        @blk.vector
        def _(v):
            env.ts.wait(v)
            env.vs.inc(v.tensor_copy(fbT[:], ps1[:, 0:RK]))

        for a in range(32):
            @blk.tensor
            def _(t, a=a):
                env.vs.wait(t)
                env.ts.inc(t.transpose(ps1[0:RK, 0:128], wrow[:, a * RK:(a + 1) * RK], ident32[:]))

            @blk.vector
            def _(v, a=a):
                env.ts.wait(v)
                env.vs.inc(v.tensor_copy(uT[:, a * 128:(a + 1) * 128], ps1[0:RK, 0:128]))

        for c in range(8):
            @blk.tensor
            def _(t, c=c):
                env.vs.wait(t)
                env.ts.inc(t.transpose(psF[:, 0:RK], tsc[0:RK, c * 128:(c + 1) * 128],
                                       ident32[0:RK, 0:RK]))

            @blk.vector
            def _(v, c=c):
                env.ts.wait(v)
                env.vs.inc(v.tensor_copy(vTb[:, c * RK:(c + 1) * RK], psF[:, 0:RK]))

        for rt in range(32):
            @blk.sync
            def _(sync, rt=rt):
                env.vs.wait(sync)
                env.ld.inc(sync.dma_start(out=wrow[:, 0:1024], in_=wfc[rt * 128:(rt + 1) * 128, :]), 16)
            _emit_ternarize(env, wrow[:, 0:1024], s_sb[:], si_sb[:], tsc[:, 0:1024],
                            weff[:, 0:1024], 1024)
            for c in range(8):
                @blk.tensor
                def _(t, rt=rt, c=c):
                    env.vs.wait(t)
                    env.ts.inc(t.transpose(psT[c % 2][:, 0:128],
                                           weff[:, c * 128:(c + 1) * 128], ident16[:]))

                @blk.vector
                def _(v, rt=rt, c=c):
                    env.ts.wait(v)
                    env.vs.inc(v.tensor_copy(wfct[c][:, rt * 128:(rt + 1) * 128], psT[c % 2][:, 0:128]))

        for rt in range(8):
            @blk.sync
            def _(sync, rt=rt):
                env.vs.wait(sync)
                env.ld.inc(sync.dma_start(out=wrow[:], in_=wpj[rt * 128:(rt + 1) * 128, :]), 16)
            _emit_ternarize(env, wrow[:], s_sb[:], si_sb[:], tsc[:], weff[:], FFN_H)
            for c in range(32):
                @blk.tensor
                def _(t, rt=rt, c=c):
                    env.vs.wait(t)
                    env.ts.inc(t.transpose(psT[c % 2][:, 0:128],
                                           weff[:, c * 128:(c + 1) * 128], ident16[:]))

                @blk.vector
                def _(v, rt=rt, c=c):
                    env.ts.wait(v)
                    env.vs.inc(v.tensor_copy(wpjt[c][:, rt * 128:(rt + 1) * 128], psT[c % 2][:, 0:128]))

        for i in range(NT):
            @blk.sync
            def _(sync, i=i):
                env.vs.wait(sync)
                env.ld.wait(sync)
                env.ld.inc(sync.dma_start(out=xt[:], in_=x[i * 128:(i + 1) * 128, :]), 16)

            @blk.vector
            def _(v, i=i):
                env.ld.wait(v)
            emit_ln_tile(env, xt[:], [xnb[:]], mv, bn)

            for c in range(8):
                @blk.tensor
                def _(t, i=i, c=c):
                    env.vs.wait(t)
                    env.ts.inc(t.transpose(psT[c % 2][:, 0:128],
                                           xnb[:, c * 128:(c + 1) * 128], ident16[:]))

                @blk.vector
                def _(v, i=i, c=c):
                    env.ts.wait(v)
                    env.vs.inc(v.tensor_copy(xfm[c][:], psT[c % 2][:, 0:128]))

            @blk.tensor
            def _(t, i=i):
                env.vs.wait(t)
                for c in range(8):
                    mm = t.matmul(ps1[:, :], vTb[:, c * RK:(c + 1) * RK], xfm[c][:],
                                  start=(c == 0), stop=(c == 7))
                env.ts.inc(mm)

            @blk.vector
            def _(v, i=i):
                env.ts.wait(v)
                env.vs.inc(v.tensor_copy(t1s[:], ps1[:]))

            @blk.tensor
            def _(t, i=i):
                env.vs.wait(t)
                env.ts.inc(t.matmul(ps1[:], fbT[:], t1s[:], start=True, stop=True))

            @blk.vector
            def _(v, i=i):
                env.ts.wait(v)
                env.vs.inc(v.tensor_copy(t2s[:], ps1[:]))

            for fc_i in range(32):
                @blk.tensor
                def _(t, i=i, fc_i=fc_i):
                    env.vs.wait(t)
                    for c in range(8):
                        mm = t.matmul(psF[:], wfct[c][:, fc_i * 128:(fc_i + 1) * 128],
                                      xfm[c][:], start=(c == 0), stop=(c == 7))
                    env.ts.inc(mm)
                    env.ts.inc(t.matmul(psL[:], uT[:, fc_i * 128:(fc_i + 1) * 128], t2s[:],
                                        start=True, stop=True))

                @blk.vector
                def _(v, i=i, fc_i=fc_i):
                    env.ts.wait(v)
                    env.vs.inc(v.tensor_scalar(hs[:], psL[:], gb[:], None, ALU.mult))
                    env.vs.wait(v)
                    env.vs.inc(v.tensor_tensor(hf[:], psF[:], hs[:], ALU.add))
                    env.vs.wait(v)
                    env.vs.inc(v.tensor_scalar(hs[:], hf[:], 0.0, None, ALU.max))
                    env.vs.wait(v)
                    env.vs.inc(v.tensor_mul(rr[fc_i][:], hf[:], hs[:]))

            @blk.tensor
            def _(t, i=i):
                env.vs.wait(t)
                for half in range(2):
                    for c in range(32):
                        mm = t.matmul(psP[half][:], rr[c][:], wpjt[c][:, half * 512:(half + 1) * 512],
                                      start=(c == 0), stop=(c == 31))
                    env.ts.inc(mm)

            @blk.vector
            def _(v, i=i):
                env.ts.wait(v)
                env.vs.inc(v.scalar_tensor_tensor(xt[:, 0:512], psP[0][:], 0.5,
                                                  xt[:, 0:512], ALU.mult, ALU.add))
                env.vs.inc(v.scalar_tensor_tensor(xt[:, 512:1024], psP[1][:], 0.5,
                                                  xt[:, 512:1024], ALU.mult, ALU.add))

            @blk.sync
            def _(sync, i=i):
                env.vs.wait(sync)
                env.ld.inc(sync.dma_start(out=xo[i * 128:(i + 1) * 128, :], in_=xt[:]), 16)

        @blk.sync
        def _(sync):
            env.ld.wait(sync)
    return nc



def _build_qkv(layer):
    nc = bass.Bass("TRN2", target_bir_lowering=False)
    x = nc.dram_tensor("x", [TPC, D], F32, kind="ExternalInput").ap()
    if layer == 0:
        wat = nc.dram_tensor("wat", [3 * D, D], F32, kind="ExternalInput").ap()
    else:
        wat = nc.dram_tensor("wat", [D, 3 * D], F32, kind="ExternalInput").ap()
        bqkv = nc.dram_tensor("bqkv", [1, 3 * D], F32, kind="ExternalInput").ap()
    qo = nc.dram_tensor("qo", [NH, HS, TPC], BF16, kind="ExternalOutput").ap()
    ko = nc.dram_tensor("ko", [NH, HS, TPC], BF16, kind="ExternalOutput").ap()
    vo = nc.dram_tensor("vo", [NH, TPC, HS], BF16, kind="ExternalOutput").ap()
    with ExitStack() as ctx:
        env = Env(nc, ctx)
        blk = env.block
        ident16 = env.const_ident(BF16)

        watT = [env.sb(f"watT{c}", [128, 3 * D], BF16) for c in range(8)]
        wrow = env.sb("wrow_q", [128, 1024], BF16)
        xt = env.sb("xt_q", [128, 1024], F32)
        xnb = env.sb("xnb_q", [128, 1024], BF16)
        bn = env.sb("bn_q", [128, 12], F32)
        mv = env.sb("mv_q", [128, 4], F32)
        xfm = [env.sb(f"xfmq{c}", [128, 128], BF16) for c in range(8)]
        qks = env.sb("qks", [128, 128], BF16)
        vs_sb = env.sb("vs_sb", [128, 1024], BF16)
        if layer == 1:
            bcols = env.sb("bcols", [128, 16], F32)
            bvb = env.sb("bvb", [128, 1024], F32)
        psT = [env.ps("psTq0", [128, 1024], BF16), env.ps("psTq1", [128, 1024], BF16)]
        psQ = env.ps("psQ", [128, 128], F32)
        psV = [env.ps("psV0", [128, 512], F32), env.ps("psV1", [128, 512], F32)]

        if layer == 0:
            for rt in range(24):
                @blk.gpsimd
                def _(g, rt=rt):
                    env.vs.wait(g)
                    env.ld.inc(g.dma_start(out=wrow[:], in_=wat[rt * 128:(rt + 1) * 128, :]), 16)
                for c in range(8):
                    @blk.tensor
                    def _(t, rt=rt, c=c):
                        env.ld.wait(t)
                        env.gs.wait(t)
                        env.vs.wait(t)
                        env.ts.inc(t.transpose(psT[c % 2][:, 0:128],
                                               wrow[:, c * 128:(c + 1) * 128], ident16[:]))

                    @blk.vector
                    def _(v, rt=rt, c=c):
                        env.ts.wait(v)
                        env.vs.inc(v.tensor_copy(watT[c][:, rt * 128:(rt + 1) * 128],
                                                 psT[c % 2][:, 0:128]))
        else:
            @blk.gpsimd
            def _(g):
                for c in range(8):
                    env.ld.inc(g.dma_start(out=watT[c][:], in_=wat[c * 128:(c + 1) * 128, :]), 16)

            @blk.sync
            def _(sync):
                env.ld.inc(sync.dma_start(
                    out=bcols[:], in_=bqkv[0:1, 0:2048].rearrange("o (a p) -> p (o a)", p=128),
                    allow_slow_non_contiguous=True), 16)
                env.ld.inc(sync.dma_start(
                    out=bvb[:], in_=bqkv[0:1, 2048:3072].unsqueeze(0).broadcast_to(
                        [128, 1, 1024]).rearrange("p o f -> p (o f)")), 16)

        for i in range(NT):
            @blk.sync
            def _(sync, i=i):
                env.vs.wait(sync)
                env.ld.inc(sync.dma_start(out=xt[:], in_=x[i * 128:(i + 1) * 128, :]), 16)

            @blk.vector
            def _(v, i=i):
                env.ld.wait(v)
            emit_ln_tile(env, xt[:], [xnb[:]], mv, bn)

            for c in range(8):
                @blk.tensor
                def _(t, i=i, c=c):
                    env.vs.wait(t)
                    env.gs.wait(t)
                    env.ts.inc(t.transpose(psT[c % 2][:, 0:128],
                                           xnb[:, c * 128:(c + 1) * 128], ident16[:]))

                @blk.vector
                def _(v, i=i, c=c):
                    env.ts.wait(v)
                    env.vs.inc(v.tensor_copy(xfm[c][:], psT[c % 2][:, 0:128]))

            for f in range(16):
                @blk.tensor
                def _(t, i=i, f=f):
                    env.vs.wait(t)
                    for c in range(8):
                        mm = t.matmul(psQ[:], watT[c][:, f * 128:(f + 1) * 128],
                                      xfm[c][:], start=(c == 0), stop=(c == 7))
                    env.ts.inc(mm)

                @blk.vector
                def _(v, i=i, f=f):
                    env.ts.wait(v)
                    env.ld.wait(v)
                    if layer == 1:
                        env.vs.inc(v.tensor_scalar(qks[:], psQ[:], bcols[:, f:f + 1],
                                                   None, ALU.add))
                    else:
                        env.vs.inc(v.tensor_copy(qks[:], psQ[:]))

                @blk.sync
                def _(sync, i=i, f=f):
                    env.vs.wait(sync)
                    dst = qo if f < 8 else ko
                    h0 = (2 * f) % 16
                    env.ld.inc(sync.dma_start(out=dst[h0, :, i * 128:(i + 1) * 128],
                                              in_=qks[0:64, :]), 16)
                    env.ld.inc(sync.dma_start(out=dst[h0 + 1, :, i * 128:(i + 1) * 128],
                                              in_=qks[64:128, :]), 16)

            @blk.tensor
            def _(t, i=i):
                env.vs.wait(t)
                for half in range(2):
                    for c in range(8):
                        mm = t.matmul(psV[half][:], xfm[c][:],
                                      watT[c][:, 2048 + half * 512:2048 + (half + 1) * 512],
                                      start=(c == 0), stop=(c == 7))
                    env.ts.inc(mm)

            @blk.vector
            def _(v, i=i):
                env.ts.wait(v)
                env.ld.wait(v)
                if layer == 1:
                    env.vs.inc(v.tensor_tensor(vs_sb[:, 0:512], psV[0][:], bvb[:, 0:512], ALU.add))
                    env.vs.inc(v.tensor_tensor(vs_sb[:, 512:1024], psV[1][:], bvb[:, 512:1024], ALU.add))
                else:
                    env.vs.inc(v.tensor_copy(vs_sb[:, 0:512], psV[0][:]))
                    env.vs.inc(v.tensor_copy(vs_sb[:, 512:1024], psV[1][:]))

            @blk.sync
            def _(sync, i=i):
                env.vs.wait(sync)
                for h in range(NH):
                    env.ld.inc(sync.dma_start(out=vo[h, i * 128:(i + 1) * 128, :],
                                              in_=vs_sb[:, h * 64:(h + 1) * 64]), 16)

        @blk.sync
        def _(sync):
            env.ld.wait(sync)
    return nc


def build_qkv0():
    return _build_qkv(0)


def build_qkv1():
    return _build_qkv(1)



def _build_attn(layer):
    causal = layer == 0
    nc = bass.Bass("TRN2", target_bir_lowering=False)
    x = nc.dram_tensor("x", [TPC, D], F32, kind="ExternalInput").ap()
    qh = nc.dram_tensor("qh", [NH, HS, TPC], BF16, kind="ExternalInput").ap()
    kh = nc.dram_tensor("kh", [NH, HS, T], BF16, kind="ExternalInput").ap()
    vh = nc.dram_tensor("vh", [NH, T, HS], BF16, kind="ExternalInput").ap()
    wp = nc.dram_tensor("wp", [D, D], F32, kind="ExternalInput").ap()
    if causal:
        cm = nc.dram_tensor("cm", [128, 512], F32, kind="ExternalInput").ap()
    else:
        bp = nc.dram_tensor("bp", [1, D], F32, kind="ExternalInput").ap()
    xo = nc.dram_tensor("xo", [TPC, D], F32, kind="ExternalOutput").ap()
    with ExitStack() as ctx:
        env = Env(nc, ctx)
        blk = env.block
        ident16 = env.const_ident(BF16)

        kt = env.sb("kt", [64, T], BF16)
        vt = env.sb("vt", [128, (T // 128) * HS], BF16)
        qsb = env.sb("qsb", [64, TPC], BF16)
        Ssb = env.sb("Ssb", [128, T], F32)
        Psb = env.sb("Psb", [128, T], BF16)
        Ptsb = env.sb("Ptsb", [128, T], BF16)
        nm = env.sb("nm", [128, 4], F32)
        y8 = [env.sb(f"y8_{c}", [128, TPC], BF16) for c in range(8)]
        wpT = [env.sb(f"wpT{c}", [128, D], BF16) for c in range(8)]
        wrow = env.sb("wrow_a", [128, 1024], BF16)
        xt = env.sb("xt_a", [128, 1024], F32)
        po = env.sb("po", [128, 1024], F32)
        cmsb = env.sb("cmsb", [128, 512], F32)
        if not causal:
            bvb = env.sb("bvb_a", [128, 1024], F32)

        psS = [env.ps("psS0", [128, 1024], F32), env.ps("psS1", [128, 1024], F32)]
        psPt = [env.ps("psPt0", [128, 1024], BF16), env.ps("psPt1", [128, 1024], BF16)]
        psY = env.ps("psY", [64, 128], F32)
        psP = env.ps("psP_a", [128, 512], F32)

        if causal:
            @blk.sync
            def _(sync):
                env.ld.inc(sync.dma_start(out=cmsb[:], in_=cm), 16)
            for rt in range(8):
                @blk.gpsimd
                def _(g, rt=rt):
                    env.vs.wait(g)
                    env.ld.inc(g.dma_start(out=wrow[:], in_=wp[rt * 128:(rt + 1) * 128, :]), 16)
                for c in range(8):
                    @blk.tensor
                    def _(t, rt=rt, c=c):
                        env.ld.wait(t)
                        env.gs.wait(t)
                        env.vs.wait(t)
                        env.ts.inc(t.transpose(psPt[c % 2][:, 0:128],
                                               wrow[:, c * 128:(c + 1) * 128], ident16[:]))

                    @blk.vector
                    def _(v, rt=rt, c=c):
                        env.ts.wait(v)
                        env.vs.inc(v.tensor_copy(wpT[c][:, rt * 128:(rt + 1) * 128],
                                                 psPt[c % 2][:, 0:128]))
        else:
            @blk.gpsimd
            def _(g):
                for c in range(8):
                    env.ld.inc(g.dma_start(out=wpT[c][:], in_=wp[c * 128:(c + 1) * 128, :]), 16)

            @blk.sync
            def _(sync):
                env.ld.inc(sync.dma_start(
                    out=bvb[:], in_=bp.unsqueeze(0).broadcast_to(
                        [128, 1, 1024]).rearrange("p o f -> p (o f)")), 16)

        for h in range(NH):
            @blk.sync
            def _(sync, h=h):
                env.ts.wait(sync)
                env.ld.inc(sync.dma_start(out=kt[:], in_=kh[h]), 16)
                env.ld.inc(sync.dma_start(
                    out=vt[:].rearrange("p (a d) -> p a d", d=HS),
                    in_=vh[h].rearrange("(a p) d -> p a d", p=128)), 16)
                env.ld.inc(sync.dma_start(out=qsb[:], in_=qh[h]), 16)

            for qt in range(NT):
                nkt = qt + 1 if causal else NT
                for kt_i in range(nkt):
                    @blk.tensor
                    def _(t, h=h, qt=qt, kt_i=kt_i):
                        env.ld.wait(t)
                        env.ss.wait(t)
                        env.ts.inc(t.matmul(psS[(kt_i // 2) % 2][:, (kt_i % 2) * 512:(kt_i % 2) * 512 + 512],
                                            qsb[:, qt * 128:(qt + 1) * 128],
                                            kt[:, kt_i * 512:(kt_i + 1) * 512],
                                            start=True, stop=True))

                    if kt_i % 2 == 1 or kt_i == nkt - 1:
                        @blk.scalar
                        def _(s, h=h, qt=qt, kt_i=kt_i):
                            env.ts.wait(s)
                            b0 = (kt_i // 2) % 2
                            lo = (kt_i // 2) * 1024
                            width = 512 * ((kt_i % 2) + 1)
                            env.ss.inc(s.activation(Ssb[:, lo:lo + width],
                                                    psS[b0][:, 0:width], AF.Copy))

                @blk.vector
                def _(v, h=h, qt=qt, nkt=nkt):
                    env.ss.wait(v)
                    if causal:
                        env.vs.inc(v.tensor_tensor(Ssb[:, (nkt - 1) * 512:nkt * 512],
                                                   Ssb[:, (nkt - 1) * 512:nkt * 512],
                                                   cmsb[:], ALU.add))
                        env.vs.wait(v)
                    env.vs.inc(v.reduce_max(nm[:, 0:1], Ssb[:, 0:nkt * 512], axis=AX.X,
                                            negate=True))
                    env.vs.wait(v)
                    env.vs.inc(v.tensor_scalar(nm[:, 1:2], nm[:, 0:1], 0.125, None, ALU.mult))

                @blk.scalar
                def _(s, h=h, qt=qt, nkt=nkt):
                    env.vs.wait(s)
                    env.ss.inc(s.activation(Psb[:, 0:nkt * 512], Ssb[:, 0:nkt * 512],
                                            AF.Exp, bias=nm[:, 1:2], scale=0.125,
                                            accum_out=nm[:, 2:3]))

                @blk.vector
                def _(v, h=h, qt=qt, nkt=nkt):
                    env.ss.wait(v)
                    env.vs.inc(v.reciprocal(nm[:, 3:4], nm[:, 2:3]))
                    env.vs.wait(v)
                    env.vs.inc(v.tensor_scalar(Psb[:, 0:nkt * 512], Psb[:, 0:nkt * 512],
                                               nm[:, 3:4], None, ALU.mult))

                for g8 in range((nkt * 4 + 7) // 8):
                    @blk.tensor
                    def _(t, h=h, qt=qt, g8=g8, nkt=nkt):
                        env.vs.wait(t)
                        nb = min(8, nkt * 4 - g8 * 8)
                        for b in range(nb):
                            kb = g8 * 8 + b
                            env.ts.inc(t.transpose(psPt[g8 % 2][:, b * 128:(b + 1) * 128],
                                                   Psb[:, kb * 128:(kb + 1) * 128], ident16[:]))

                    @blk.vector
                    def _(v, h=h, qt=qt, g8=g8, nkt=nkt):
                        env.ts.wait(v)
                        nb = min(8, nkt * 4 - g8 * 8)
                        env.vs.inc(v.tensor_copy(Ptsb[:, g8 * 1024:g8 * 1024 + nb * 128],
                                                 psPt[g8 % 2][:, 0:nb * 128]))

                @blk.tensor
                def _(t, h=h, qt=qt, nkt=nkt):
                    env.vs.wait(t)
                    for kb in range(nkt * 4):
                        mm = t.matmul(psY[:], vt[:, kb * 64:(kb + 1) * 64],
                                      Ptsb[:, kb * 128:(kb + 1) * 128],
                                      start=(kb == 0), stop=(kb == nkt * 4 - 1))
                    env.ts.inc(mm)

                @blk.vector
                def _(v, h=h, qt=qt):
                    env.ts.wait(v)
                    env.vs.inc(v.tensor_copy(
                        y8[h // 2][(h % 2) * 64:(h % 2) * 64 + 64, qt * 128:(qt + 1) * 128],
                        psY[:]))

        for m in range(NT):
            @blk.sync
            def _(sync, m=m):
                env.vs.wait(sync)
                env.ld.wait(sync)
                env.ld.inc(sync.dma_start(out=xt[:], in_=x[m * 128:(m + 1) * 128, :]), 16)

            for half in range(2):
                @blk.tensor
                def _(t, m=m, half=half):
                    env.vs.wait(t)
                    for c in range(8):
                        mm = t.matmul(psP[:], y8[c][:, m * 128:(m + 1) * 128],
                                      wpT[c][:, half * 512:(half + 1) * 512],
                                      start=(c == 0), stop=(c == 7))
                    env.ts.inc(mm)

                @blk.vector
                def _(v, m=m, half=half):
                    env.ts.wait(v)
                    env.ld.wait(v)
                    sl = slice(half * 512, (half + 1) * 512)
                    if not causal:
                        env.vs.inc(v.tensor_tensor(po[:, sl], psP[:], bvb[:, sl], ALU.add))
                        env.vs.wait(v)
                        env.vs.inc(v.tensor_tensor(xt[:, sl], xt[:, sl], po[:, sl], ALU.add))
                    else:
                        env.vs.inc(v.tensor_tensor(xt[:, sl], xt[:, sl], psP[:], ALU.add))

            @blk.sync
            def _(sync, m=m):
                env.vs.wait(sync)
                env.ld.inc(sync.dma_start(out=xo[m * 128:(m + 1) * 128, :], in_=xt[:]), 16)

        @blk.sync
        def _(sync):
            env.ld.wait(sync)
    return nc


def build_attn0():
    return _build_attn(0)


def build_attn1():
    return _build_attn(1)



_NC_CACHE = {}


def _get_nc(name):
    if name not in _NC_CACHE:
        _NC_CACHE[name] = globals()["build_" + name]()
    return _NC_CACHE[name]


def _run(name, in_maps):
    res = run_bass_kernel_spmd(_get_nc(name), in_maps, core_ids=list(range(NCORES)))
    return res.results


def run_route(x_full, wg):
    wg = np.ascontiguousarray(wg)
    return _run("route", [{"x": np.ascontiguousarray(x_full[core_rows(c)]),
                           "wg": wg} for c in range(NCORES)])


def moe_sublayer(x_full, Wg, W1, W2):
    r1 = run_route(x_full, Wg)
    meta = np.zeros((B * T, 6), np.float32)
    xn_all = np.zeros((B * T, D), BF16NP)
    tsg = np.zeros((NBLK, NE), np.float32)
    for c in range(NCORES):
        rows = core_rows(c)
        meta[rows] = r1[c]["meta"]
        xn_all[rows] = r1[c]["xn"]
        ts = r1[c]["tsum"].reshape(NT, NE)
        for i, g in enumerate(own_blocks(c)):
            tsg[g] = ts[i]
    in_maps = []
    for c in range(NCORES):
        in_maps.append({
            "xn_all": xn_all, "meta": meta, "tsg": tsg,
            "w1": np.ascontiguousarray(W1[c]),
            "w2": np.ascontiguousarray(W2[c]),
            "cid": np.full((1, 1), float(c), np.float32),
        })
    r2 = _run("moe_ffn", in_maps)
    call = np.concatenate([r2[c]["contrib"] for c in range(NCORES)]
                      + [np.zeros((1, D), np.float32)], 0)
    sall = r2[0]["sall"]
    in_maps = []
    for c in range(NCORES):
        rows = core_rows(c)
        in_maps.append({
            "x": np.ascontiguousarray(x_full[rows]),
            "call": call,
            "sown": np.ascontiguousarray(sall[rows]),
        })
    r3 = _run("moe_combine", in_maps)
    xo = np.zeros((B * T, D), np.float32)
    for c in range(NCORES):
        xo[core_rows(c)] = r3[c]["xo"]
    return xo


def ffn1_sublayer(x_full, w_fc, w_proj, u, v, fast_b, gate):
    common = {"wfc": np.ascontiguousarray(w_fc), "wpj": np.ascontiguousarray(w_proj),
              "uu": np.ascontiguousarray(u), "vv": np.ascontiguousarray(v),
              "fb": np.ascontiguousarray(fast_b),
              "gate": np.full((1, 1), np.float32(gate), np.float32)}
    in_maps = [dict(common, x=np.ascontiguousarray(x_full[core_rows(c)]))
               for c in range(NCORES)]
    r = _run("ffn1", in_maps)
    xo = np.zeros((B * T, D), np.float32)
    for c in range(NCORES):
        xo[core_rows(c)] = r[c]["xo"]
    return xo


def _causal_mask(c):
    r = c % 4
    p = np.arange(128)[:, None]
    kk = np.arange(512)[None, :]
    return np.where(kk <= r * 128 + p, 0.0, NEG).astype(np.float32)


def attn_sublayer(layer, x_full, W, bias_qkv=None, W_proj=None, bias_proj=None):
    if layer == 0:
        r1 = _run("qkv0", [{"x": np.ascontiguousarray(x_full[core_rows(c)]),
                            "wat": np.ascontiguousarray(W)} for c in range(NCORES)])
    else:
        r1 = _run("qkv1", [{"x": np.ascontiguousarray(x_full[core_rows(c)]),
                            "wat": np.ascontiguousarray(W),
                            "bqkv": np.ascontiguousarray(bias_qkv.reshape(1, -1))}
                           for c in range(NCORES)])
    K_full = np.zeros((B, NH, HS, T), BF16NP)
    V_full = np.zeros((B, NH, T, HS), BF16NP)
    for c in range(NCORES):
        b = c // 4
        pos = core_rows(c) - b * T
        K_full[b][:, :, pos] = r1[c]["ko"]
        V_full[b][:, pos, :] = r1[c]["vo"]
    in_maps = []
    for c in range(NCORES):
        b = c // 4
        m = {"x": np.ascontiguousarray(x_full[core_rows(c)]),
             "qh": r1[c]["qo"], "kh": np.ascontiguousarray(K_full[b]),
             "vh": np.ascontiguousarray(V_full[b]),
             "wp": np.ascontiguousarray(W_proj)}
        if layer == 0:
            m["cm"] = _causal_mask(c)
        else:
            m["bp"] = np.ascontiguousarray(bias_proj.reshape(1, -1))
        in_maps.append(m)
    r2 = _run("attn0" if layer == 0 else "attn1", in_maps)
    xo = np.zeros((B * T, D), np.float32)
    for c in range(NCORES):
        xo[core_rows(c)] = r2[c]["xo"]
    return xo


def kernel(L0_attn_x, L0_attn_W_attn, L0_attn_W_proj, L0_attn_pos_emb,
           L0_attn_W_fc, L0_attn_W_cproj, L0_attn_ln_w, L0_ffn_Wg, L0_ffn_W1,
           L0_ffn_W2, L1_attn_w_qkv, L1_attn_b_qkv, L1_attn_w_proj,
           L1_attn_b_proj, L1_ffn_fast_b, L1_ffn_w_fc, L1_ffn_w_proj,
           L1_ffn_u, L1_ffn_v, L1_ffn_gate):
    x = np.ascontiguousarray(np.asarray(L0_attn_x, np.float32).reshape(B * T, D))
    x = moe_sublayer(x, L0_ffn_Wg, L0_ffn_W1, L0_ffn_W2)
    x = attn_sublayer(0, x, L0_attn_W_attn, W_proj=L0_attn_W_proj)
    x = moe_sublayer(x, L0_ffn_Wg, L0_ffn_W1, L0_ffn_W2)
    x = ffn1_sublayer(x, L1_ffn_w_fc, L1_ffn_w_proj, L1_ffn_u, L1_ffn_v,
                      L1_ffn_fast_b, float(L1_ffn_gate))
    x = attn_sublayer(1, x, L1_attn_w_qkv, bias_qkv=np.asarray(L1_attn_b_qkv),
                      W_proj=L1_attn_w_proj, bias_proj=np.asarray(L1_attn_b_proj))
    x = ffn1_sublayer(x, L1_ffn_w_fc, L1_ffn_w_proj, L1_ffn_u, L1_ffn_v,
                      L1_ffn_fast_b, float(L1_ffn_gate))
    return x.reshape(B, T, D)


# version 02
# speedup vs baseline: 1.5432x; 1.5432x over previous
import numpy as np
import ml_dtypes
from contextlib import ExitStack

import concourse.bass as bass
import concourse.mybir as mybir
from concourse.bass_utils import run_bass_kernel_spmd
from concourse.bass import IndirectOffsetOnAxis

F32 = mybir.dt.float32
BF16 = mybir.dt.bfloat16
I32 = mybir.dt.int32
AF = mybir.ActivationFunctionType
ALU = mybir.AluOpType
AX = mybir.AxisListType
BF16NP = ml_dtypes.bfloat16

NCORES = 8
B, T, D = 2, 8192, 1024
TPC = 2048
NT = TPC // 128
NBLK = 128
NH, HS = 16, 64
NE, CAP = 8, 5120
FFN_H = 4096
RK = 16
BIG = 1.0e9
NEG = -1.0e30


def own_blocks(c):
    return [(c // 4) * 64 + (c % 4) + 4 * i for i in range(NT)]


def core_rows(c):
    bl = np.array(own_blocks(c))
    return (bl[:, None] * 128 + np.arange(128)[None, :]).reshape(-1)



class S:

    def __init__(self, nc, ctx, name):
        self.h = ctx.enter_context(nc.semaphore(name))
        self.n = 0

    def inc(self, instr, k=1):
        instr.then_inc(self.h, k)
        self.n += k
        return instr

    def wait(self, eng):
        if self.n:
            eng.wait_ge(self.h, self.n)


class Env:
    def __init__(self, nc, ctx):
        self.nc, self.ctx = nc, ctx
        self.block = ctx.enter_context(nc.Block())
        self.ld = S(nc, ctx, "ld")
        self.ts = S(nc, ctx, "ts")
        self.vs = S(nc, ctx, "vs")
        self.ss = S(nc, ctx, "ss")
        self.gs = S(nc, ctx, "gs")
        self.cc = S(nc, ctx, "cc")
        self._consts = {}

    def sb(self, name, shape, dt):
        return self.ctx.enter_context(self.nc.sbuf_tensor(name, shape, dt))

    def ps(self, name, shape, dt=F32):
        return self.ctx.enter_context(self.nc.psum_tensor(name, shape, dt))

    def const_ident(self, dt):
        key = ("ident", dt)
        if key not in self._consts:
            t = self.sb(f"ident{'32' if dt == F32 else '16'}", [128, 128], dt)

            @self.block.gpsimd
            def _(g):
                self.gs.inc(g.memset(t[:], 1.0))
                self.gs.wait(g)
                self.gs.inc(g.affine_select(t[:], t[:], [[1, 128]], ALU.is_equal,
                                            0.0, channel_multiplier=-1))
            self._consts[key] = t
        return self._consts[key]

    def const_ltri(self):
        if "ltri" not in self._consts:
            t = self.sb("ltri", [128, 128], F32)

            @self.block.gpsimd
            def _(g):
                self.gs.inc(g.memset(t[:], 1.0))
                self.gs.wait(g)
                self.gs.inc(g.affine_select(t[:], t[:], [[1, 128]], ALU.is_gt,
                                            0.0, channel_multiplier=-1))
            self._consts["ltri"] = t
        return self._consts["ltri"]

    def const_iota8(self):
        if "iota8" not in self._consts:
            t = self.sb("iota8", [128, 8], F32)

            @self.block.gpsimd
            def _(g):
                self.gs.inc(g.iota(t[:], [[1, 8]], channel_multiplier=0,
                                   allow_small_or_imprecise_dtypes=True))
            self._consts["iota8"] = t
        return self._consts["iota8"]

    def const_iotap(self):
        if "iotap" not in self._consts:
            t = self.sb("iotap", [128, 1], F32)

            @self.block.gpsimd
            def _(g):
                self.gs.inc(g.iota(t[:], [[0, 1]], channel_multiplier=1,
                                   allow_small_or_imprecise_dtypes=True))
            self._consts["iotap"] = t
        return self._consts["iotap"]

    def const_ones(self, n=128):
        key = ("ones", n)
        if key not in self._consts:
            t = self.sb(f"ones{n}", [n, 1], F32)

            @self.block.gpsimd
            def _(g):
                self.gs.inc(g.memset(t[:], 1.0))
            self._consts[key] = t
        return self._consts[key]


def emit_ln_tile(env, x_ap, outs, mv, bn):
    blk = env.block

    @blk.vector
    def _(v):
        env.vs.inc(v.bn_stats(bn[:, 0:6], x_ap[:, 0:512]))
        env.vs.inc(v.bn_stats(bn[:, 6:12], x_ap[:, 512:1024]))
        env.vs.wait(v)
        env.vs.inc(v.bn_aggr(mv[:, 0:2], bn[:, 0:12]))
        env.vs.wait(v)
        env.vs.inc(v.tensor_scalar(mv[:, 2:3], mv[:, 1:2], 1e-5, None, ALU.add))

    @blk.scalar
    def _(s):
        env.vs.wait(s)
        env.ss.inc(s.activation(mv[:, 2:3], mv[:, 2:3], AF.Ln))
        env.ss.wait(s)
        env.ss.inc(s.activation(mv[:, 3:4], mv[:, 2:3], AF.Exp, scale=-0.5))

    @blk.vector
    def _(v):
        env.ss.wait(v)
        for o in outs:
            env.vs.inc(v.tensor_scalar(o, x_ap, mv[:, 0:1], mv[:, 3:4],
                                       ALU.subtract, ALU.mult))



def build_route():
    nc = bass.Bass("TRN2", target_bir_lowering=False)
    x = nc.dram_tensor("x", [TPC, D], F32, kind="ExternalInput").ap()
    wg = nc.dram_tensor("wg", [NE, D], F32, kind="ExternalInput").ap()
    xn_out = nc.dram_tensor("xn", [TPC, D], BF16, kind="ExternalOutput").ap()
    meta = nc.dram_tensor("meta", [TPC, 6], F32, kind="ExternalOutput").ap()
    tsum = nc.dram_tensor("tsum", [1, NT * NE], F32, kind="ExternalOutput").ap()
    with ExitStack() as ctx:
        env = Env(nc, ctx)
        blk = env.block
        ident32 = env.const_ident(F32)
        ltri = env.const_ltri()
        iota8 = env.const_iota8()
        ones128 = env.const_ones()

        xt = env.sb("xt", [128, 1024], F32)
        xnf = env.sb("xnf", [128, 1024], F32)
        xnb = env.sb("xnb", [128, 1024], BF16)
        xc = env.sb("xc", [128, 1024], F32)
        wgt = env.sb("wgt", [NE, 1024], F32)
        wgT = env.sb("wgT", [128, 64], F32)
        logit = env.sb("logit", [128, 8], F32)
        l2 = env.sb("l2", [128, 8], F32)
        m1 = env.sb("m1", [128, 8], F32)
        m2 = env.sb("m2", [128, 8], F32)
        twoh = env.sb("twoh", [128, 8], F32)
        csum = env.sb("csum", [128, 8], F32)
        tmp8 = env.sb("tmp8", [128, 8], F32)
        idx8 = env.sb("idx8", [128, 8], mybir.dt.uint32)
        r = env.sb("r", [128, 12], F32)
        bn = env.sb("bn", [128, 12], F32)
        mv = env.sb("mv", [128, 4], F32)
        sums = env.sb("sums", [1, 128], F32)
        metat = env.sb("metat", [128, 6], F32)

        psT = [env.ps("psT0", [128, 128], F32), env.ps("psT1", [128, 128], F32)]
        psW = env.ps("psW", [128, 64], F32)
        psL = env.ps("psL", [128, 8], F32)
        psC = env.ps("psC", [128, 8], F32)
        psS = env.ps("psS", [1, 8], F32)

        @blk.sync
        def _(sync):
            env.ld.inc(sync.dma_start(out=wgt[:], in_=wg), 16)

        @blk.tensor
        def _(t):
            env.ld.wait(t)
            env.gs.wait(t)
            for c in range(8):
                env.ts.inc(t.transpose(psW[:, c * 8:(c + 1) * 8],
                                       wgt[:, c * 128:(c + 1) * 128], ident32[0:8, 0:8]))

        @blk.vector
        def _(v):
            env.ts.wait(v)
            env.vs.inc(v.tensor_copy(wgT[:], psW[:]))

        for i in range(NT):
            @blk.sync
            def _(sync, i=i):
                env.vs.wait(sync)
                env.ld.inc(sync.dma_start(out=xt[:], in_=x[i * 128:(i + 1) * 128, :]), 16)

            @blk.vector
            def _(v, i=i):
                env.ld.wait(v)
            emit_ln_tile(env, xt[:], [xnf[:], xnb[:]], mv, bn)

            for c in range(8):
                @blk.tensor
                def _(t, c=c):
                    env.vs.wait(t)
                    env.ts.inc(t.transpose(psT[c % 2][:], xnf[:, c * 128:(c + 1) * 128],
                                           ident32[:]))

                @blk.vector
                def _(v, c=c):
                    env.ts.wait(v)
                    env.vs.inc(v.tensor_copy(xc[:, c * 128:(c + 1) * 128], psT[c % 2][:]))

            @blk.tensor
            def _(t, i=i):
                env.vs.wait(t)
                for c in range(8):
                    mm = t.matmul(psL[:], xc[:, c * 128:(c + 1) * 128],
                                  wgT[:, c * 8:(c + 1) * 8],
                                  start=(c == 0), stop=(c == 7))
                env.ts.inc(mm)

            @blk.vector
            def _(v, i=i):
                env.ts.wait(v)
                env.vs.inc(v.tensor_copy(logit[:], psL[:]))
                env.vs.wait(v)
                env.vs.inc(v.max(l2[:], logit[:]))
                env.vs.wait(v)
                env.vs.inc(v.max_index(idx8[:], l2[:], logit[:]))
                env.vs.wait(v)
                env.vs.inc(v.tensor_copy(metat[:, 0:1], idx8[:, 0:1]))
                env.vs.inc(v.tensor_copy(metat[:, 1:2], idx8[:, 1:2]))
                env.vs.wait(v)
                env.vs.inc(v.tensor_scalar(m1[:], iota8[:], metat[:, 0:1], None, ALU.is_equal))
                env.vs.inc(v.tensor_scalar(m2[:], iota8[:], metat[:, 1:2], None, ALU.is_equal))
                env.vs.inc(v.tensor_sub(r[:, 2:3], l2[:, 1:2], l2[:, 0:1]))

            @blk.scalar
            def _(s, i=i):
                env.vs.wait(s)
                env.ss.inc(s.activation(r[:, 3:4], r[:, 2:3], AF.Exp))

            @blk.vector
            def _(v, i=i):
                env.ss.wait(v)
                env.vs.inc(v.tensor_scalar(r[:, 4:5], r[:, 3:4], 1.0, None, ALU.add))
                env.vs.wait(v)
                env.vs.inc(v.reciprocal(metat[:, 2:3], r[:, 4:5]))
                env.vs.wait(v)
                env.vs.inc(v.tensor_mul(metat[:, 3:4], r[:, 3:4], metat[:, 2:3]))
                env.vs.inc(v.tensor_add(twoh[:], m1[:], m2[:]))

            @blk.tensor
            def _(t, i=i):
                env.vs.wait(t)
                env.ts.inc(t.matmul(psC[:], ltri[:], twoh[:], start=True, stop=True))
                env.ts.inc(t.matmul(psS[:], ones128[:], twoh[:], start=True, stop=True))

            @blk.vector
            def _(v, i=i):
                env.ts.wait(v)
                env.vs.inc(v.tensor_copy(csum[:], psC[:]))
                env.vs.inc(v.tensor_copy(sums[0:1, i * 8:(i + 1) * 8], psS[:]))
                env.vs.wait(v)
                env.vs.inc(v.tensor_mul(tmp8[:], m1[:], csum[:]))
                env.vs.wait(v)
                env.vs.inc(v.reduce_sum(metat[:, 4:5], tmp8[:], axis=AX.X))
                env.vs.inc(v.tensor_mul(tmp8[:], m2[:], csum[:]))
                env.vs.wait(v)
                env.vs.inc(v.reduce_sum(metat[:, 5:6], tmp8[:], axis=AX.X))

            @blk.sync
            def _(sync, i=i):
                env.vs.wait(sync)
                env.ld.inc(sync.dma_start(out=xn_out[i * 128:(i + 1) * 128, :], in_=xnb[:]), 16)
                env.ld.inc(sync.dma_start(out=meta[i * 128:(i + 1) * 128, :], in_=metat[:]), 16)

        @blk.sync
        def _(sync):
            env.vs.wait(sync)
            env.ld.inc(sync.dma_start(out=tsum, in_=sums[0:1, :]), 16)
            env.ld.wait(sync)
    return nc



def build_moe_ffn():
    nc = bass.Bass("TRN2", target_bir_lowering=False)
    xn_all = nc.dram_tensor("xn_all", [B * T, D], BF16, kind="ExternalInput").ap()
    meta = nc.dram_tensor("meta", [B * T, 6], F32, kind="ExternalInput").ap()
    tsg = nc.dram_tensor("tsg", [NBLK, NE], F32, kind="ExternalInput").ap()
    w1 = nc.dram_tensor("w1", [D, 2 * 2048], F32, kind="ExternalInput").ap()
    w2 = nc.dram_tensor("w2", [2048, D], F32, kind="ExternalInput").ap()
    cid = nc.dram_tensor("cid", [1, 1], F32, kind="ExternalInput").ap()
    contrib = nc.dram_tensor("contrib", [CAP, D], F32, kind="ExternalOutput").ap()
    sall = nc.dram_tensor("sall", [B * T, 2], F32, kind="ExternalOutput").ap()

    offs_dram = nc.dram_tensor("offs_dram", [NBLK, NE], F32)
    tdram = nc.dram_tensor("tdram", [CAP + 128, 2], F32)

    with ExitStack() as ctx:
        env = Env(nc, ctx)
        blk = env.block
        ltri = env.const_ltri()
        iota8 = env.const_iota8()
        iotap = env.const_iotap()
        ident16 = env.const_ident(BF16)

        tsg_sb = env.sb("tsg_sb", [128, 8], F32)
        offs_sb = env.sb("offs_sb", [128, 8], F32)
        OFFB = env.sb("OFFB", [128, 1024], F32)
        cidb = env.sb("cidb", [128, 1], F32)
        mt = [env.sb(f"mt{k}", [128, 6], F32) for k in range(2)]
        st = [env.sb(f"st{k}", [128, 2], F32) for k in range(2)]
        r = env.sb("rm", [128, 16], F32)
        msk = env.sb("msk", [128, 8], F32)
        vals = env.sb("vals", [128, 2], F32)
        scidx = env.sb("scidx", [128, 1], I32)
        init80 = env.sb("init80", [128, 82], F32)
        tg = env.sb("tg", [128, 80], F32)
        gidx = env.sb("gidx", [128, 4], I32)
        rg = env.sb("rg", [128, 4], F32)
        gx = [env.sb(f"gx{a}", [128, 1024], BF16) for a in range(4)]
        xfm = [env.sb(f"xfm{cc}", [128, 512], BF16) for cc in range(8)]
        w1b = [env.sb(f"w1b{cc}", [128, 4096], BF16) for cc in range(8)]
        w2b = [env.sb(f"w2b{ii}", [128, 1024], BF16) for ii in range(16)]
        haf = [env.sb(f"haf{ii}", [128, 512], BF16) for ii in range(16)]
        gel = env.sb("gel", [128, 512], F32)
        ct = env.sb("ct", [128, 1024], F32)

        psO = env.ps("psO", [128, 8], F32)
        psT = [env.ps("psT0", [128, 1024], BF16), env.ps("psT1", [128, 1024], BF16)]
        psA = env.ps("psA", [128, 512], F32)
        psB = env.ps("psB", [128, 512], F32)
        psC = env.ps("psC", [128, 512], F32)
        psD = env.ps("psD", [128, 512], F32)

        @blk.sync
        def _(sync):
            env.ld.inc(sync.dma_start(out=tsg_sb[:], in_=tsg), 16)
            env.ld.inc(sync.dma_start(
                out=cidb[:], in_=cid.unsqueeze(0).broadcast_to([128, 1, 1]).rearrange("p a b -> p (a b)")), 16)

        @blk.gpsimd
        def _(g):
            for c8 in range(8):
                env.ld.inc(g.dma_start(out=w1b[c8][:], in_=w1[c8 * 128:(c8 + 1) * 128, :]), 16)
            for i16 in range(16):
                env.ld.inc(g.dma_start(out=w2b[i16][:], in_=w2[i16 * 128:(i16 + 1) * 128, :]), 16)

        @blk.tensor
        def _(t):
            env.ld.wait(t)
            env.gs.wait(t)
            env.ts.inc(t.matmul(psO[:], ltri[:], tsg_sb[:], start=True, stop=True))

        @blk.vector
        def _(v):
            env.ts.wait(v)
            env.vs.inc(v.tensor_copy(offs_sb[:], psO[:]))
            env.vs.inc(v.memset(init80[:], BIG))
            env.vs.wait(v)
            env.vs.inc(v.memset(init80[:].rearrange("p (a w) -> p a w", w=2)[:, :, 1:2], 0.0))

        @blk.sync
        def _(sync):
            env.vs.wait(sync)
            env.ld.inc(sync.dma_start(out=offs_dram.ap(), in_=offs_sb[:]), 16)
            env.ld.inc(sync.dma_start(
                out=tdram.ap().rearrange("(p a) w -> p (a w)", p=128), in_=init80[:]), 16)
            env.ld.wait(sync)
            env.ld.inc(sync.dma_start(
                out=OFFB[:].rearrange("p (j e) -> p j e", e=8),
                in_=offs_dram.ap().unsqueeze(0).broadcast_to([128, NBLK, NE])), 16)

        for j in range(NBLK):
            buf = j % 2

            @blk.sync
            def _(sync, j=j, buf=buf):
                env.vs.wait(sync)
                env.ld.inc(sync.dma_start(out=mt[buf][:], in_=meta[j * 128:(j + 1) * 128, :]), 16)

            for k in range(2):
                @blk.vector
                def _(v, j=j, k=k, buf=buf):
                    env.ld.wait(v)
                    ecol, pcol, gcol = mt[buf][:, k:k + 1], mt[buf][:, 4 + k:5 + k], mt[buf][:, 2 + k:3 + k]
                    env.vs.inc(v.tensor_scalar(msk[:], iota8[:], ecol, None, ALU.is_equal))
                    env.vs.wait(v)
                    env.vs.inc(v.tensor_mul(msk[:], msk[:], OFFB[:, j * 8:(j + 1) * 8]))
                    env.vs.wait(v)
                    env.vs.inc(v.reduce_sum(r[:, 0:1], msk[:], axis=AX.X))
                    env.vs.wait(v)
                    env.vs.inc(v.tensor_add(r[:, 1:2], r[:, 0:1], pcol))
                    env.vs.wait(v)
                    env.vs.inc(v.tensor_scalar(r[:, 2:3], r[:, 1:2], float(CAP), None, ALU.is_lt))
                    env.vs.wait(v)
                    env.vs.inc(v.tensor_mul(vals[:, 1:2], gcol, r[:, 2:3]))
                    env.vs.inc(v.tensor_scalar(r[:, 3:4], r[:, 2:3], -1.0, -BIG, ALU.add, ALU.mult))
                    env.vs.wait(v)
                    env.vs.inc(v.tensor_scalar(r[:, 4:5], ecol, float(CAP), None, ALU.mult))
                    env.vs.wait(v)
                    env.vs.inc(v.tensor_add(r[:, 5:6], r[:, 4:5], r[:, 1:2]))
                    env.vs.wait(v)
                    env.vs.inc(v.tensor_add(st[buf][:, k:k + 1], r[:, 5:6], r[:, 3:4]))
                    env.vs.inc(v.tensor_tensor(r[:, 6:7], ecol, cidb[:], ALU.not_equal))
                    env.vs.wait(v)
                    env.vs.inc(v.scalar_tensor_tensor(r[:, 7:8], r[:, 6:7], BIG, r[:, 1:2], ALU.mult, ALU.add))
                    env.vs.wait(v)
                    env.vs.inc(v.tensor_scalar(r[:, 7:8], r[:, 7:8], float(CAP), None, ALU.min))
                    env.vs.wait(v)
                    env.vs.inc(v.tensor_copy(scidx[:], r[:, 7:8]))
                    env.vs.inc(v.tensor_scalar(vals[:, 0:1], iotap[:], float(j * 128), None, ALU.add))

                @blk.gpsimd
                def _(g, j=j, k=k, buf=buf):
                    env.vs.wait(g)
                    env.ld.inc(g.indirect_dma_start(
                        out=tdram.ap(), out_offset=IndirectOffsetOnAxis(scidx[:, :], 0),
                        in_=vals[:, :], in_offset=None), 16)

            @blk.sync
            def _(sync, j=j, buf=buf):
                env.vs.wait(sync)
                env.ld.inc(sync.dma_start(out=sall[j * 128:(j + 1) * 128, :], in_=st[buf][:]), 16)

        @blk.sync
        def _(sync):
            env.ld.wait(sync)
            env.ld.inc(sync.dma_start(
                out=tg[:].rearrange("p (a w) -> p a w", w=2),
                in_=tdram.ap()[0:CAP, :].rearrange("(a p) w -> p a w", p=128)), 16)

        for G in range(10):
            @blk.vector
            def _(v, G=G):
                env.ld.wait(v)
                env.ts.wait(v)
                for a4 in range(4):
                    a = G * 4 + a4
                    env.vs.inc(v.tensor_scalar(rg[:, a4:a4 + 1], tg[:, 2 * a:2 * a + 1],
                                               float(B * T - 1), None, ALU.min))
                env.vs.wait(v)
                env.vs.inc(v.tensor_copy(gidx[:], rg[:, 0:4]))

            @blk.gpsimd
            def _(g, G=G):
                env.vs.wait(g)
                for a4 in range(4):
                    env.ld.inc(g.indirect_dma_start(
                        out=gx[a4][:, :], out_offset=None,
                        in_=xn_all, in_offset=IndirectOffsetOnAxis(gidx[:, a4:a4 + 1], 0)), 16)

            for c8 in range(8):
                @blk.tensor
                def _(t, G=G, c8=c8):
                    env.ld.wait(t)
                    env.vs.wait(t)
                    for a4 in range(4):
                        env.ts.inc(t.transpose(psT[c8 % 2][:, a4 * 128:(a4 + 1) * 128],
                                               gx[a4][:, c8 * 128:(c8 + 1) * 128], ident16[:]))

                @blk.vector
                def _(v, G=G, c8=c8):
                    env.ts.wait(v)
                    env.vs.inc(v.tensor_copy(xfm[c8][:], psT[c8 % 2][:, 0:512]))

            for fi in range(16):
                @blk.tensor
                def _(t, G=G, fi=fi):
                    env.vs.wait(t)
                    env.ss.wait(t)
                    for c8 in range(8):
                        ma = t.matmul(psA[:], w1b[c8][:, fi * 128:(fi + 1) * 128],
                                      xfm[c8][:], start=(c8 == 0), stop=(c8 == 7))
                        mb = t.matmul(psB[:], w1b[c8][:, 2048 + fi * 128:2048 + (fi + 1) * 128],
                                      xfm[c8][:], start=(c8 == 0), stop=(c8 == 7))
                    env.ts.inc(ma)
                    env.ts.inc(mb)

                @blk.scalar
                def _(s, G=G, fi=fi):
                    env.ts.wait(s)
                    env.ss.inc(s.activation(gel[:], psA[:], AF.Gelu))

                @blk.vector
                def _(v, G=G, fi=fi):
                    env.ss.wait(v)
                    env.vs.inc(v.tensor_mul(haf[fi][:], gel[:], psB[:]))

            for m in range(4):
                @blk.tensor
                def _(t, G=G, m=m):
                    env.vs.wait(t)
                    for i16 in range(16):
                        mc = t.matmul(psC[:], haf[i16][:, m * 128:(m + 1) * 128],
                                      w2b[i16][:, 0:512], start=(i16 == 0), stop=(i16 == 15))
                        md = t.matmul(psD[:], haf[i16][:, m * 128:(m + 1) * 128],
                                      w2b[i16][:, 512:1024], start=(i16 == 0), stop=(i16 == 15))
                    env.ts.inc(mc)
                    env.ts.inc(md)

                @blk.vector
                def _(v, G=G, m=m):
                    env.ts.wait(v)
                    env.ld.wait(v)
                    wcol = tg[:, 2 * (G * 4 + m) + 1:2 * (G * 4 + m) + 2]
                    env.vs.inc(v.tensor_scalar(ct[:, 0:512], psC[:], wcol, None, ALU.mult))
                    env.vs.inc(v.tensor_scalar(ct[:, 512:1024], psD[:], wcol, None, ALU.mult))

                @blk.sync
                def _(sync, G=G, m=m):
                    env.vs.wait(sync)
                    row0 = G * 512 + m * 128
                    env.ld.inc(sync.dma_start(out=contrib[row0:row0 + 128, :], in_=ct[:]), 16)

        @blk.sync
        def _(sync):
            env.ld.wait(sync)
    return nc



def build_moe_combine():
    nc = bass.Bass("TRN2", target_bir_lowering=False)
    x = nc.dram_tensor("x", [TPC, D], F32, kind="ExternalInput").ap()
    call = nc.dram_tensor("call", [NE * CAP + 1, D], F32, kind="ExternalInput").ap()
    sown = nc.dram_tensor("sown", [TPC, 2], F32, kind="ExternalInput").ap()
    xo = nc.dram_tensor("xo", [TPC, D], F32, kind="ExternalOutput").ap()
    with ExitStack() as ctx:
        env = Env(nc, ctx)
        blk = env.block
        xt = env.sb("xt", [128, 1024], F32)
        sot = env.sb("sot", [128, 2], F32)
        sidx = env.sb("sidx", [128, 2], I32)
        g1 = env.sb("g1", [128, 1024], F32)
        g2 = env.sb("g2", [128, 1024], F32)

        for i in range(NT):
            @blk.sync
            def _(sync, i=i):
                env.vs.wait(sync)
                env.ld.wait(sync)
                env.ld.inc(sync.dma_start(out=xt[:], in_=x[i * 128:(i + 1) * 128, :]), 16)
                env.ld.inc(sync.dma_start(out=sot[:], in_=sown[i * 128:(i + 1) * 128, :]), 16)

            @blk.vector
            def _(v, i=i):
                env.ld.wait(v)
                env.vs.inc(v.tensor_scalar(sot[:], sot[:], float(NE * CAP), None, ALU.min))
                env.vs.wait(v)
                env.vs.inc(v.tensor_copy(sidx[:], sot[:]))

            @blk.gpsimd
            def _(g, i=i):
                env.vs.wait(g)
                env.ld.inc(g.indirect_dma_start(
                    out=g1[:, :], out_offset=None, in_=call,
                    in_offset=IndirectOffsetOnAxis(sidx[:, 0:1], 0)), 16)
                env.ld.inc(g.indirect_dma_start(
                    out=g2[:, :], out_offset=None, in_=call,
                    in_offset=IndirectOffsetOnAxis(sidx[:, 1:2], 0)), 16)

            @blk.vector
            def _(v, i=i):
                env.ld.wait(v)
                env.vs.inc(v.scalar_tensor_tensor(xt[:], g1[:], 0.5, xt[:], ALU.mult, ALU.add))
                env.vs.wait(v)
                env.vs.inc(v.scalar_tensor_tensor(xt[:], g2[:], 0.5, xt[:], ALU.mult, ALU.add))

            @blk.sync
            def _(sync, i=i):
                env.vs.wait(sync)
                env.ld.inc(sync.dma_start(out=xo[i * 128:(i + 1) * 128, :], in_=xt[:]), 16)

        @blk.sync
        def _(sync):
            env.ld.wait(sync)
    return nc



def _emit_ternarize(env, w_ap, s_sb, si_sb, t_sb, out_ap, free):
    blk = env.block

    @blk.vector
    def _(v):
        env.ld.wait(v)
        env.vs.inc(v.reduce_sum(s_sb, w_ap, axis=AX.X, apply_absolute_value=True))
        env.vs.wait(v)
        env.vs.inc(v.tensor_scalar(s_sb, s_sb, 1.0 / free, 1e-5, ALU.mult, ALU.max))
        env.vs.wait(v)
        env.vs.inc(v.reciprocal(si_sb, s_sb))
        env.vs.wait(v)
        env.vs.inc(v.tensor_scalar(t_sb, w_ap, si_sb, None, ALU.mult))
        env.vs.wait(v)
        env.vs.inc(v.tensor_scalar(out_ap, t_sb, 0.5, None, ALU.is_gt))
        env.vs.inc(v.tensor_scalar(t_sb, t_sb, -0.5, None, ALU.is_lt))
        env.vs.wait(v)
        env.vs.inc(v.tensor_tensor(t_sb, out_ap, t_sb, ALU.subtract))
        env.vs.wait(v)
        env.vs.inc(v.tensor_scalar(out_ap, t_sb, s_sb, None, ALU.mult))


def build_ffn1():
    nc = bass.Bass("TRN2", target_bir_lowering=False)
    x = nc.dram_tensor("x", [TPC, D], F32, kind="ExternalInput").ap()
    wfc = nc.dram_tensor("wfc", [FFN_H, D], F32, kind="ExternalInput").ap()
    wpj = nc.dram_tensor("wpj", [D, FFN_H], F32, kind="ExternalInput").ap()
    uu = nc.dram_tensor("uu", [FFN_H, RK], F32, kind="ExternalInput").ap()
    vv = nc.dram_tensor("vv", [RK, D], F32, kind="ExternalInput").ap()
    fb = nc.dram_tensor("fb", [RK, RK], F32, kind="ExternalInput").ap()
    gate = nc.dram_tensor("gate", [1, 1], F32, kind="ExternalInput").ap()
    xo = nc.dram_tensor("xo", [TPC, D], F32, kind="ExternalOutput").ap()
    with ExitStack() as ctx:
        env = Env(nc, ctx)
        blk = env.block
        ident32 = env.const_ident(F32)
        ident16 = env.const_ident(BF16)

        wrow = env.sb("wrow", [128, FFN_H], F32)
        weff = env.sb("weff", [128, FFN_H], BF16)
        tsc = env.sb("tsc", [128, FFN_H], F32)
        s_sb = env.sb("s_sb", [128, 1], F32)
        si_sb = env.sb("si_sb", [128, 1], F32)
        wfct = [env.sb(f"wfct{c}", [128, FFN_H], BF16) for c in range(8)]
        wpjt = [env.sb(f"wpjt{i}", [128, D], BF16) for i in range(32)]
        vTb = env.sb("vTb", [128, 8 * RK], BF16)
        uT = env.sb("uT", [RK, FFN_H], BF16)
        fbT = env.sb("fbT", [RK, RK], BF16)
        fbsb = env.sb("fbsb", [RK, RK], F32)
        gb = env.sb("gb", [128, 1], F32)
        xt = env.sb("xt_f", [128, 1024], F32)
        xnb = env.sb("xnb_f", [128, 1024], BF16)
        bn = env.sb("bn_f", [128, 12], F32)
        mv = env.sb("mv_f", [128, 4], F32)
        xfm = [env.sb(f"xfmf{c}", [128, 128], BF16) for c in range(8)]
        t1s = env.sb("t1s", [RK, 128], BF16)
        t2s = env.sb("t2s", [RK, 128], BF16)
        hs = env.sb("hs", [128, 128], F32)
        hf = env.sb("hf", [128, 128], F32)
        rr = [env.sb(f"rr{c}", [128, 128], BF16) for c in range(32)]

        psT = [env.ps("psT0", [128, 1024], BF16), env.ps("psT1", [128, 1024], BF16)]
        psF = env.ps("psF", [128, 128], F32)
        psL = env.ps("psL", [128, 128], F32)
        ps1 = env.ps("ps1", [RK, 128], F32)
        psP = [env.ps("psP0", [128, 512], F32), env.ps("psP1", [128, 512], F32)]

        @blk.sync
        def _(sync):
            env.ld.inc(sync.dma_start(out=fbsb[:], in_=fb), 16)
            env.ld.inc(sync.dma_start(
                out=gb[:], in_=gate.unsqueeze(0).broadcast_to([128, 1, 1]).rearrange("p a b -> p (a b)")), 16)

        @blk.gpsimd
        def _(g):
            env.ld.inc(g.dma_start(out=wrow[:, 0:RK * 32].rearrange("p (a r) -> p a r", r=RK),
                                   in_=uu.rearrange("(a p) r -> p a r", p=128)), 16)
            env.ld.inc(g.dma_start(out=tsc[0:RK, 0:1024], in_=vv), 16)

        @blk.tensor
        def _(t):
            env.ld.wait(t)
            env.gs.wait(t)
            env.ts.inc(t.transpose(ps1[:, 0:RK], fbsb[:, :], ident32[0:RK, 0:RK]))

        @blk.vector
        def _(v):
            env.ts.wait(v)
            env.vs.inc(v.tensor_copy(fbT[:], ps1[:, 0:RK]))

        for a in range(32):
            @blk.tensor
            def _(t, a=a):
                env.vs.wait(t)
                env.ts.inc(t.transpose(ps1[0:RK, 0:128], wrow[:, a * RK:(a + 1) * RK], ident32[:]))

            @blk.vector
            def _(v, a=a):
                env.ts.wait(v)
                env.vs.inc(v.tensor_copy(uT[:, a * 128:(a + 1) * 128], ps1[0:RK, 0:128]))

        for c in range(8):
            @blk.tensor
            def _(t, c=c):
                env.vs.wait(t)
                env.ts.inc(t.transpose(psF[:, 0:RK], tsc[0:RK, c * 128:(c + 1) * 128],
                                       ident32[0:RK, 0:RK]))

            @blk.vector
            def _(v, c=c):
                env.ts.wait(v)
                env.vs.inc(v.tensor_copy(vTb[:, c * RK:(c + 1) * RK], psF[:, 0:RK]))

        for rt in range(32):
            @blk.sync
            def _(sync, rt=rt):
                env.vs.wait(sync)
                env.ld.inc(sync.dma_start(out=wrow[:, 0:1024], in_=wfc[rt * 128:(rt + 1) * 128, :]), 16)
            _emit_ternarize(env, wrow[:, 0:1024], s_sb[:], si_sb[:], tsc[:, 0:1024],
                            weff[:, 0:1024], 1024)
            for c in range(8):
                @blk.tensor
                def _(t, rt=rt, c=c):
                    env.vs.wait(t)
                    env.ts.inc(t.transpose(psT[c % 2][:, 0:128],
                                           weff[:, c * 128:(c + 1) * 128], ident16[:]))

                @blk.vector
                def _(v, rt=rt, c=c):
                    env.ts.wait(v)
                    env.vs.inc(v.tensor_copy(wfct[c][:, rt * 128:(rt + 1) * 128], psT[c % 2][:, 0:128]))

        for rt in range(8):
            @blk.sync
            def _(sync, rt=rt):
                env.vs.wait(sync)
                env.ld.inc(sync.dma_start(out=wrow[:], in_=wpj[rt * 128:(rt + 1) * 128, :]), 16)
            _emit_ternarize(env, wrow[:], s_sb[:], si_sb[:], tsc[:], weff[:], FFN_H)
            for c in range(32):
                @blk.tensor
                def _(t, rt=rt, c=c):
                    env.vs.wait(t)
                    env.ts.inc(t.transpose(psT[c % 2][:, 0:128],
                                           weff[:, c * 128:(c + 1) * 128], ident16[:]))

                @blk.vector
                def _(v, rt=rt, c=c):
                    env.ts.wait(v)
                    env.vs.inc(v.tensor_copy(wpjt[c][:, rt * 128:(rt + 1) * 128], psT[c % 2][:, 0:128]))

        for i in range(NT):
            @blk.sync
            def _(sync, i=i):
                env.vs.wait(sync)
                env.ld.wait(sync)
                env.ld.inc(sync.dma_start(out=xt[:], in_=x[i * 128:(i + 1) * 128, :]), 16)

            @blk.vector
            def _(v, i=i):
                env.ld.wait(v)
            emit_ln_tile(env, xt[:], [xnb[:]], mv, bn)

            for c in range(8):
                @blk.tensor
                def _(t, i=i, c=c):
                    env.vs.wait(t)
                    env.ts.inc(t.transpose(psT[c % 2][:, 0:128],
                                           xnb[:, c * 128:(c + 1) * 128], ident16[:]))

                @blk.vector
                def _(v, i=i, c=c):
                    env.ts.wait(v)
                    env.vs.inc(v.tensor_copy(xfm[c][:], psT[c % 2][:, 0:128]))

            @blk.tensor
            def _(t, i=i):
                env.vs.wait(t)
                for c in range(8):
                    mm = t.matmul(ps1[:, :], vTb[:, c * RK:(c + 1) * RK], xfm[c][:],
                                  start=(c == 0), stop=(c == 7))
                env.ts.inc(mm)

            @blk.vector
            def _(v, i=i):
                env.ts.wait(v)
                env.vs.inc(v.tensor_copy(t1s[:], ps1[:]))

            @blk.tensor
            def _(t, i=i):
                env.vs.wait(t)
                env.ts.inc(t.matmul(ps1[:], fbT[:], t1s[:], start=True, stop=True))

            @blk.vector
            def _(v, i=i):
                env.ts.wait(v)
                env.vs.inc(v.tensor_copy(t2s[:], ps1[:]))

            for fc_i in range(32):
                @blk.tensor
                def _(t, i=i, fc_i=fc_i):
                    env.vs.wait(t)
                    for c in range(8):
                        mm = t.matmul(psF[:], wfct[c][:, fc_i * 128:(fc_i + 1) * 128],
                                      xfm[c][:], start=(c == 0), stop=(c == 7))
                    env.ts.inc(mm)
                    env.ts.inc(t.matmul(psL[:], uT[:, fc_i * 128:(fc_i + 1) * 128], t2s[:],
                                        start=True, stop=True))

                @blk.vector
                def _(v, i=i, fc_i=fc_i):
                    env.ts.wait(v)
                    env.vs.inc(v.tensor_scalar(hs[:], psL[:], gb[:], None, ALU.mult))
                    env.vs.wait(v)
                    env.vs.inc(v.tensor_tensor(hf[:], psF[:], hs[:], ALU.add))
                    env.vs.wait(v)
                    env.vs.inc(v.tensor_scalar(hs[:], hf[:], 0.0, None, ALU.max))
                    env.vs.wait(v)
                    env.vs.inc(v.tensor_mul(rr[fc_i][:], hf[:], hs[:]))

            @blk.tensor
            def _(t, i=i):
                env.vs.wait(t)
                for half in range(2):
                    for c in range(32):
                        mm = t.matmul(psP[half][:], rr[c][:], wpjt[c][:, half * 512:(half + 1) * 512],
                                      start=(c == 0), stop=(c == 31))
                    env.ts.inc(mm)

            @blk.vector
            def _(v, i=i):
                env.ts.wait(v)
                env.vs.inc(v.scalar_tensor_tensor(xt[:, 0:512], psP[0][:], 0.5,
                                                  xt[:, 0:512], ALU.mult, ALU.add))
                env.vs.inc(v.scalar_tensor_tensor(xt[:, 512:1024], psP[1][:], 0.5,
                                                  xt[:, 512:1024], ALU.mult, ALU.add))

            @blk.sync
            def _(sync, i=i):
                env.vs.wait(sync)
                env.ld.inc(sync.dma_start(out=xo[i * 128:(i + 1) * 128, :], in_=xt[:]), 16)

        @blk.sync
        def _(sync):
            env.ld.wait(sync)
    return nc



def _build_qkv(layer):
    nc = bass.Bass("TRN2", target_bir_lowering=False)
    x = nc.dram_tensor("x", [TPC, D], F32, kind="ExternalInput").ap()
    if layer == 0:
        wat = nc.dram_tensor("wat", [3 * D, D], F32, kind="ExternalInput").ap()
    else:
        wat = nc.dram_tensor("wat", [D, 3 * D], F32, kind="ExternalInput").ap()
        bqkv = nc.dram_tensor("bqkv", [1, 3 * D], F32, kind="ExternalInput").ap()
    qo = nc.dram_tensor("qo", [NH, HS, TPC], BF16, kind="ExternalOutput").ap()
    ko = nc.dram_tensor("ko", [NH, HS, TPC], BF16, kind="ExternalOutput").ap()
    vo = nc.dram_tensor("vo", [NH, TPC, HS], BF16, kind="ExternalOutput").ap()
    with ExitStack() as ctx:
        env = Env(nc, ctx)
        blk = env.block
        ident16 = env.const_ident(BF16)

        watT = [env.sb(f"watT{c}", [128, 3 * D], BF16) for c in range(8)]
        wrow = env.sb("wrow_q", [128, 1024], BF16)
        xt = env.sb("xt_q", [128, 1024], F32)
        xnb = env.sb("xnb_q", [128, 1024], BF16)
        bn = env.sb("bn_q", [128, 12], F32)
        mv = env.sb("mv_q", [128, 4], F32)
        xfm = [env.sb(f"xfmq{c}", [128, 128], BF16) for c in range(8)]
        qks = env.sb("qks", [128, 128], BF16)
        vs_sb = env.sb("vs_sb", [128, 1024], BF16)
        if layer == 1:
            bcols = env.sb("bcols", [128, 16], F32)
            bvb = env.sb("bvb", [128, 1024], F32)
        psT = [env.ps("psTq0", [128, 1024], BF16), env.ps("psTq1", [128, 1024], BF16)]
        psQ = env.ps("psQ", [128, 128], F32)
        psV = [env.ps("psV0", [128, 512], F32), env.ps("psV1", [128, 512], F32)]

        if layer == 0:
            for rt in range(24):
                @blk.gpsimd
                def _(g, rt=rt):
                    env.vs.wait(g)
                    env.ld.inc(g.dma_start(out=wrow[:], in_=wat[rt * 128:(rt + 1) * 128, :]), 16)
                for c in range(8):
                    @blk.tensor
                    def _(t, rt=rt, c=c):
                        env.ld.wait(t)
                        env.gs.wait(t)
                        env.vs.wait(t)
                        env.ts.inc(t.transpose(psT[c % 2][:, 0:128],
                                               wrow[:, c * 128:(c + 1) * 128], ident16[:]))

                    @blk.vector
                    def _(v, rt=rt, c=c):
                        env.ts.wait(v)
                        env.vs.inc(v.tensor_copy(watT[c][:, rt * 128:(rt + 1) * 128],
                                                 psT[c % 2][:, 0:128]))
        else:
            @blk.gpsimd
            def _(g):
                for c in range(8):
                    env.ld.inc(g.dma_start(out=watT[c][:], in_=wat[c * 128:(c + 1) * 128, :]), 16)

            @blk.sync
            def _(sync):
                env.ld.inc(sync.dma_start(
                    out=bcols[:], in_=bqkv[0:1, 0:2048].rearrange("o (a p) -> p (o a)", p=128),
                    allow_slow_non_contiguous=True), 16)
                env.ld.inc(sync.dma_start(
                    out=bvb[:], in_=bqkv[0:1, 2048:3072].unsqueeze(0).broadcast_to(
                        [128, 1, 1024]).rearrange("p o f -> p (o f)")), 16)

        for i in range(NT):
            @blk.sync
            def _(sync, i=i):
                env.vs.wait(sync)
                env.ld.inc(sync.dma_start(out=xt[:], in_=x[i * 128:(i + 1) * 128, :]), 16)

            @blk.vector
            def _(v, i=i):
                env.ld.wait(v)
            emit_ln_tile(env, xt[:], [xnb[:]], mv, bn)

            for c in range(8):
                @blk.tensor
                def _(t, i=i, c=c):
                    env.vs.wait(t)
                    env.gs.wait(t)
                    env.ts.inc(t.transpose(psT[c % 2][:, 0:128],
                                           xnb[:, c * 128:(c + 1) * 128], ident16[:]))

                @blk.vector
                def _(v, i=i, c=c):
                    env.ts.wait(v)
                    env.vs.inc(v.tensor_copy(xfm[c][:], psT[c % 2][:, 0:128]))

            for f in range(16):
                @blk.tensor
                def _(t, i=i, f=f):
                    env.vs.wait(t)
                    for c in range(8):
                        mm = t.matmul(psQ[:], watT[c][:, f * 128:(f + 1) * 128],
                                      xfm[c][:], start=(c == 0), stop=(c == 7))
                    env.ts.inc(mm)

                @blk.vector
                def _(v, i=i, f=f):
                    env.ts.wait(v)
                    env.ld.wait(v)
                    if layer == 1:
                        env.vs.inc(v.tensor_scalar(qks[:], psQ[:], bcols[:, f:f + 1],
                                                   None, ALU.add))
                    else:
                        env.vs.inc(v.tensor_copy(qks[:], psQ[:]))

                @blk.sync
                def _(sync, i=i, f=f):
                    env.vs.wait(sync)
                    dst = qo if f < 8 else ko
                    h0 = (2 * f) % 16
                    env.ld.inc(sync.dma_start(out=dst[h0, :, i * 128:(i + 1) * 128],
                                              in_=qks[0:64, :]), 16)
                    env.ld.inc(sync.dma_start(out=dst[h0 + 1, :, i * 128:(i + 1) * 128],
                                              in_=qks[64:128, :]), 16)

            @blk.tensor
            def _(t, i=i):
                env.vs.wait(t)
                for half in range(2):
                    for c in range(8):
                        mm = t.matmul(psV[half][:], xfm[c][:],
                                      watT[c][:, 2048 + half * 512:2048 + (half + 1) * 512],
                                      start=(c == 0), stop=(c == 7))
                    env.ts.inc(mm)

            @blk.vector
            def _(v, i=i):
                env.ts.wait(v)
                env.ld.wait(v)
                if layer == 1:
                    env.vs.inc(v.tensor_tensor(vs_sb[:, 0:512], psV[0][:], bvb[:, 0:512], ALU.add))
                    env.vs.inc(v.tensor_tensor(vs_sb[:, 512:1024], psV[1][:], bvb[:, 512:1024], ALU.add))
                else:
                    env.vs.inc(v.tensor_copy(vs_sb[:, 0:512], psV[0][:]))
                    env.vs.inc(v.tensor_copy(vs_sb[:, 512:1024], psV[1][:]))

            @blk.sync
            def _(sync, i=i):
                env.vs.wait(sync)
                for h in range(NH):
                    env.ld.inc(sync.dma_start(out=vo[h, i * 128:(i + 1) * 128, :],
                                              in_=vs_sb[:, h * 64:(h + 1) * 64]), 16)

        @blk.sync
        def _(sync):
            env.ld.wait(sync)
    return nc


def build_qkv0():
    return _build_qkv(0)


def build_qkv1():
    return _build_qkv(1)



NQST = NT // 4
NKB = T // 128


class C:
    def __init__(self, nc, ctx, name):
        self.h = ctx.enter_context(nc.semaphore(name))
        self.n = 0

    def inc(self, instr, k=1):
        instr.then_inc(self.h, k)
        self.n += k
        return self.n


def _build_attn(layer):
    causal = layer == 0
    nc = bass.Bass("TRN2", target_bir_lowering=False)
    x = nc.dram_tensor("x", [TPC, D], F32, kind="ExternalInput").ap()
    qh = nc.dram_tensor("qh", [NH, HS, TPC], BF16, kind="ExternalInput").ap()
    kh = nc.dram_tensor("kh", [NH, HS, T], BF16, kind="ExternalInput").ap()
    vh = nc.dram_tensor("vh", [NH, T, HS], BF16, kind="ExternalInput").ap()
    wp = nc.dram_tensor("wp", [D, D], F32, kind="ExternalInput").ap()
    if causal:
        cm = nc.dram_tensor("cm", [16, 128, 512], BF16, kind="ExternalInput").ap()
    else:
        bp = nc.dram_tensor("bp", [1, D], F32, kind="ExternalInput").ap()
    xo = nc.dram_tensor("xo", [TPC, D], F32, kind="ExternalOutput").ap()
    with ExitStack() as ctx:
        blk = ctx.enter_context(nc.Block())
        ld = C(nc, ctx, "ld")
        lg = C(nc, ctx, "lg")
        lk = [C(nc, ctx, "lk0"), C(nc, ctx, "lk1")]
        lv = [C(nc, ctx, "lv0"), C(nc, ctx, "lv1")]
        ts = C(nc, ctx, "ts")
        ss = C(nc, ctx, "ss")
        vs = C(nc, ctx, "vs")
        gs = C(nc, ctx, "gs")

        kt = [ctx.enter_context(nc.sbuf_tensor(f"kt{b}", [65, T], BF16)) for b in range(2)]
        vt = [ctx.enter_context(nc.sbuf_tensor(f"vt{b}", [128, NKB * 65], BF16)) for b in range(2)]
        qsb = [ctx.enter_context(nc.sbuf_tensor(f"qsb{b}", [65, TPC], BF16)) for b in range(2)]
        ksq = ctx.enter_context(nc.sbuf_tensor("ksq", [64, T], BF16))
        qsq = ctx.enter_context(nc.sbuf_tensor("qsq", [64, TPC], BF16))
        krow = ctx.enter_context(nc.sbuf_tensor("krow", [1, T], F32))
        qrow = ctx.enter_context(nc.sbuf_tensor("qrow", [1, TPC], F32))
        km = ctx.enter_context(nc.sbuf_tensor("km", [1, 1], F32))
        mrow = ctx.enter_context(nc.sbuf_tensor("mrow", [1, TPC], F32))
        mne = ctx.enter_context(nc.sbuf_tensor("mne", [1, TPC], BF16))
        ones64 = ctx.enter_context(nc.sbuf_tensor("ones64", [64, 1], BF16))
        ptile = [ctx.enter_context(nc.sbuf_tensor(f"pt{i}", [128, 1024], BF16)) for i in range(3)]
        rrow = ctx.enter_context(nc.sbuf_tensor("rrow", [1, 512], F32))
        rb = ctx.enter_context(nc.sbuf_tensor("rb", [64, 512], F32))
        y8 = [ctx.enter_context(nc.sbuf_tensor(f"y8_{c}", [128, TPC], BF16)) for c in range(8)]
        ones1 = ctx.enter_context(nc.sbuf_tensor("ones1", [1, 64], F32))
        wpT = [ctx.enter_context(nc.sbuf_tensor(f"wpT{c}", [128, D], BF16)) for c in range(8)]
        xt = ctx.enter_context(nc.sbuf_tensor("xt_a", [128, 1024], F32))
        if causal:
            cmsb = ctx.enter_context(nc.sbuf_tensor("cmsb", [128, 16 * 512], BF16))
        else:
            bvb = ctx.enter_context(nc.sbuf_tensor("bvb_a", [128, 1024], F32))

        psS = [ctx.enter_context(nc.psum_tensor(f"psS{b}", [128, 1024], F32)) for b in range(2)]
        psY = ctx.enter_context(nc.psum_tensor("psY", [65, 512], F32))
        psR = ctx.enter_context(nc.psum_tensor("psR", [64, 512], F32))
        psN = [ctx.enter_context(nc.psum_tensor(f"psN{b}", [1, 512], F32)) for b in range(2)]
        psP = [psS[0][:, 0:512], psS[1][:, 0:512]]

        @blk.gpsimd
        def _(g):
            gs.inc(g.memset(ones1[:], 1.0))
            gs.inc(g.memset(ones64[:], 1.0))
            for b in range(2):
                gs.inc(g.memset(kt[b][64:65, :], 1.0))
                gs.inc(g.memset(vt[b][:].rearrange("p (a w) -> p a w", w=65)[:, :, 64:65], 1.0))
        CONST_GS = gs.n

        if causal:
            @blk.sync
            def _(sync):
                ld.inc(sync.dma_start(
                    out=cmsb[:].rearrange("p (a w) -> p a w", w=512),
                    in_=cm.rearrange("a p w -> p a w")), 16)
            CM_LD = ld.n
        else:
            CM_LD = 0

            @blk.sync
            def _(sync):
                ld.inc(sync.dma_start(
                    out=bvb[:], in_=bp.unsqueeze(0).broadcast_to(
                        [128, 1, 1024]).rearrange("p o f -> p (o f)")), 16)

        @blk.gpsimd
        def _(g):
            for c in range(8):
                lg.inc(g.dma_start(out=wpT[c][:], in_=wp[c * 128:(c + 1) * 128, :]), 16)

        st = {
            "rel_ts": [0, 0], "rel_ss": [0, 0], "rel_vs": [0, 0],
            "head_ld": [0, 0], "head_lg": [0, 0],
            "ymul_vs": 0, "evac_ss": 0,
            "nrm_mm_ts": 0, "nrm_cp_vs": 0,
        }

        def emit_loads(h):
            bf = h % 2

            @blk.sync
            def _(sync):
                sync.wait_ge(ts.h, st["rel_ts"][bf])
                sync.wait_ge(vs.h, st["rel_vs"][bf])
                lk[bf].inc(sync.dma_start(out=kt[bf][0:64, :], in_=kh[h]), 16)
                lk[bf].inc(sync.dma_start(out=qsb[bf][0:64, :], in_=qh[h]), 16)

            @blk.gpsimd
            def _(g):
                g.wait_ge(ts.h, st["rel_ts"][bf])
                g.wait_ge(gs.h, CONST_GS)
                lv[bf].inc(g.dma_start(
                    out=vt[bf][:].rearrange("p (a w) -> p a w", w=65)[:, :, 0:64],
                    in_=vh[h].rearrange("(a p) d -> p a d", p=128)), 16)
            st["head_ld"][bf] = lk[bf].n
            st["head_lg"][bf] = lv[bf].n

        def emit_head(h):
            bf = h % 2
            my_ld = st["head_ld"][bf]
            my_lg = st["head_lg"][bf]

            prev_nrm_mm = st["nrm_mm_ts"]
            prev_nrm_cp = st["nrm_cp_vs"]

            @blk.scalar
            def _(s):
                s.wait_ge(lk[bf].h, my_ld)
                s.wait_ge(ts.h, prev_nrm_mm)
                ss.inc(s.activation(ksq[:], kt[bf][0:64, :], AF.Square))
                s.wait_ge(ss.h, ss.n)
                ss.inc(s.activation(qsq[:], qsb[bf][0:64, :], AF.Square))
            sq_ss = ss.n

            nmm = {}
            ncp = {}
            for i in range(20):
                @blk.tensor
                def _(t, i=i):
                    if i == 0:
                        t.wait_ge(ss.h, sq_ss)
                        t.wait_ge(gs.h, CONST_GS)
                    if i >= 2:
                        t.wait_ge(vs.h, ncp[i - 2])
                    else:
                        t.wait_ge(vs.h, prev_nrm_cp)
                    src_ap = (ksq[:, i * 512:(i + 1) * 512] if i < 16 else
                              qsq[:, (i - 16) * 512:(i - 15) * 512])
                    nmm[i] = ts.inc(t.matmul(psN[i % 2][:], ones64[:], src_ap,
                                             start=True, stop=True))

                @blk.vector
                def _(v, i=i):
                    v.wait_ge(ts.h, nmm[i])
                    dst = (krow[0:1, i * 512:(i + 1) * 512] if i < 16 else
                           qrow[0:1, (i - 16) * 512:(i - 15) * 512])
                    ncp[i] = vs.inc(v.tensor_copy(dst, psN[i % 2][:]))
            st["nrm_mm_ts"] = ts.n
            st["nrm_cp_vs"] = vs.n

            @blk.vector
            def _(v):
                v.wait_ge(vs.h, ncp[19])
                vs.inc(v.reduce_max(km[0:1, 0:1], krow[0:1, :], axis=AX.X))
                v.wait_ge(vs.h, vs.n)
                vs.inc(v.tensor_scalar(mrow[0:1, :], qrow[0:1, :], km[0:1, 0:1],
                                       None, ALU.mult))
            m_vs = vs.n

            @blk.scalar
            def _(s):
                s.wait_ge(vs.h, m_vs)
                ss.inc(s.activation(mrow[0:1, :], mrow[0:1, :], AF.Sqrt))
            m_ss = ss.n

            @blk.vector
            def _(v):
                v.wait_ge(ss.h, m_ss)
                vs.inc(v.tensor_scalar(mne[0:1, :], mrow[0:1, :], -1.0, None,
                                       ALU.mult))
            mne_vs = vs.n

            @blk.sync
            def _(sync):
                sync.wait_ge(vs.h, mne_vs)
                lk[bf].inc(sync.dma_start(out=qsb[bf][64:65, :], in_=mne[0:1, :]), 16)
            mrow_ld = lk[bf].n

            for s_i in range(NQST):
                nch = (16 * s_i + 16) if causal else NKB
                npair = nch // 2
                smm = {}
                ex = {}
                msk = {}
                pvl = {}

                def emit_spair(p, s_i=s_i):
                    @blk.tensor
                    def _(t, p=p, s_i=s_i):
                        if s_i == 0 and p == 0:
                            t.wait_ge(lk[bf].h, mrow_ld)
                            t.wait_ge(lv[bf].h, my_lg)
                        if ("e", p - 2) in ex:
                            t.wait_ge(ss.h, ex[("e", p - 2)])
                        else:
                            t.wait_ge(ss.h, st["exp_bank"][p % 2])
                        for u in range(2):
                            c2 = 2 * p + u
                            mm = t.matmul(
                                psS[p % 2][:, u * 512:(u + 1) * 512],
                                kt[bf][:, c2 * 128:(c2 + 1) * 128],
                                qsb[bf][:, s_i * 512:(s_i + 1) * 512],
                                start=True, stop=True)
                            if u == 1:
                                smm[p] = ts.inc(mm)
                            else:
                                ts.n += 0
                                mm.then_inc(ts.h, 1)
                                ts.n += 1

                def emit_exp(p, s_i=s_i):
                    @blk.scalar
                    def _(s, p=p, s_i=s_i):
                        s.wait_ge(ts.h, smm[p])
                        ex[("e", p)] = ss.inc(s.activation(
                            ptile[p % 3][:], psS[p % 2][:], AF.Exp, scale=0.125))
                    st["exp_bank"][p % 2] = ex[("e", p)]
                    if causal and (2 * p - 16 * s_i) >= 0:
                        pb = (2 * p - 16 * s_i) // 2

                        @blk.vector
                        def _(v, p=p, pb=pb):
                            v.wait_ge(ss.h, ex[("e", p)])
                            v.wait_ge(ld.h, CM_LD)
                            msk[p] = vs.inc(v.tensor_tensor(
                                ptile[p % 3][:], ptile[p % 3][:],
                                cmsb[:, pb * 1024:(pb + 1) * 1024], ALU.mult))

                def emit_pv(p, s_i=s_i, npair=npair):
                    @blk.tensor
                    def _(t, p=p, npair=npair):
                        if p in msk:
                            t.wait_ge(vs.h, msk[p])
                        else:
                            t.wait_ge(ss.h, ex[("e", p)])
                        if p == 0:
                            t.wait_ge(vs.h, st["ymul_vs"])
                        for u in range(2):
                            c2 = 2 * p + u
                            mm = t.matmul(
                                psY[:], vt[bf][:, c2 * 65:(c2 + 1) * 65],
                                ptile[p % 3][:, u * 512:(u + 1) * 512],
                                start=(c2 == 0), stop=(c2 == 2 * npair - 1))
                            if c2 == 2 * npair - 1:
                                pvl["last"] = ts.inc(mm)

                emit_spair(0)
                emit_exp(0)
                if npair > 1:
                    emit_spair(1)
                    emit_exp(1)
                for p in range(2, npair):
                    emit_spair(p)
                    emit_exp(p)
                    emit_pv(p - 2)
                if npair > 1:
                    emit_pv(npair - 2)
                emit_pv(npair - 1)
                pv_last = pvl["last"]

                @blk.vector
                def _(v):
                    v.wait_ge(ts.h, pv_last)
                    vs.inc(v.reciprocal(rrow[0:1, :], psY[64:65, :]))
                rc_vs = vs.n

                @blk.tensor
                def _(t):
                    t.wait_ge(vs.h, rc_vs)
                    t.wait_ge(ss.h, st["evac_ss"])
                    ts.inc(t.matmul(psR[:], ones1[:], rrow[0:1, :],
                                    start=True, stop=True))
                bc_ts = ts.n

                @blk.scalar
                def _(s):
                    s.wait_ge(ts.h, bc_ts)
                    s.wait_ge(vs.h, st["ymul_vs"])
                    st["evac_ss"] = ss.inc(s.activation(rb[:], psR[:], AF.Copy))
                ev_ss = ss.n

                @blk.vector
                def _(v, s_i=s_i, h=h):
                    v.wait_ge(ss.h, ev_ss)
                    st["ymul_vs"] = vs.inc(v.tensor_tensor(
                        y8[h // 2][(h % 2) * 64:(h % 2) * 64 + 64,
                                   s_i * 512:(s_i + 1) * 512],
                        psY[0:64, :], rb[:], ALU.mult))

            st["rel_ts"][bf] = ts.n
            st["rel_ss"][bf] = ss.n
            st["rel_vs"][bf] = vs.n

        st["exp_bank"] = [0, 0]
        for h in range(NH):
            emit_loads(h)
            emit_head(h)

        ALL_Y_VS = vs.n
        LAST_EXP = max(st["exp_bank"])

        st_ld = {}
        for m in range(NT):
            @blk.sync
            def _(sync, m=m):
                if (m - 1) in st_ld:
                    sync.wait_ge(ld.h, st_ld[m - 1])
                ld.inc(sync.dma_start(out=xt[:], in_=x[m * 128:(m + 1) * 128, :]), 16)
            x_ld = ld.n

            mm_ts = {}
            psP_free = vs.n
            for half in range(2):
                @blk.tensor
                def _(t, m=m, half=half):
                    if m == 0 and half == 0:
                        t.wait_ge(vs.h, ALL_Y_VS)
                        t.wait_ge(ss.h, LAST_EXP)
                        t.wait_ge(lg.h, lg.n)
                    t.wait_ge(vs.h, psP_free)
                    for c in range(8):
                        mmi = t.matmul(psP[half][:], y8[c][:, m * 128:(m + 1) * 128],
                                       wpT[c][:, half * 512:(half + 1) * 512],
                                       start=(c == 0), stop=(c == 7))
                    mm_ts[half] = ts.inc(mmi)

            @blk.vector
            def _(v, m=m):
                v.wait_ge(ld.h, x_ld)
                for half in range(2):
                    v.wait_ge(ts.h, mm_ts[half])
                    sl = slice(half * 512, (half + 1) * 512)
                    vs.inc(v.tensor_tensor(xt[:, sl], xt[:, sl], psP[half][:],
                                           ALU.add))
                    if not causal:
                        v.wait_ge(vs.h, vs.n)
                        vs.inc(v.tensor_tensor(xt[:, sl], xt[:, sl], bvb[:, sl],
                                               ALU.add))
            add_vs = vs.n

            @blk.sync
            def _(sync, m=m):
                sync.wait_ge(vs.h, add_vs)
                st_ld[m] = ld.inc(sync.dma_start(
                    out=xo[m * 128:(m + 1) * 128, :], in_=xt[:]), 16)

        @blk.sync
        def _(sync):
            sync.wait_ge(ld.h, ld.n)
            sync.wait_ge(lg.h, lg.n)
            for b in range(2):
                sync.wait_ge(lk[b].h, lk[b].n)
                sync.wait_ge(lv[b].h, lv[b].n)
    return nc


def build_attn0():
    return _build_attn(0)


def build_attn1():
    return _build_attn(1)



_NC_CACHE = {}


def _get_nc(name):
    if name not in _NC_CACHE:
        _NC_CACHE[name] = globals()["build_" + name]()
    return _NC_CACHE[name]


def _run(name, in_maps):
    res = run_bass_kernel_spmd(_get_nc(name), in_maps, core_ids=list(range(NCORES)))
    return res.results


def run_route(x_full, wg):
    wg = np.ascontiguousarray(wg)
    return _run("route", [{"x": np.ascontiguousarray(x_full[core_rows(c)]),
                           "wg": wg} for c in range(NCORES)])


def moe_sublayer(x_full, Wg, W1, W2):
    r1 = run_route(x_full, Wg)
    meta = np.zeros((B * T, 6), np.float32)
    xn_all = np.zeros((B * T, D), BF16NP)
    tsg = np.zeros((NBLK, NE), np.float32)
    for c in range(NCORES):
        rows = core_rows(c)
        meta[rows] = r1[c]["meta"]
        xn_all[rows] = r1[c]["xn"]
        ts = r1[c]["tsum"].reshape(NT, NE)
        for i, g in enumerate(own_blocks(c)):
            tsg[g] = ts[i]
    in_maps = []
    for c in range(NCORES):
        in_maps.append({
            "xn_all": xn_all, "meta": meta, "tsg": tsg,
            "w1": np.ascontiguousarray(W1[c]),
            "w2": np.ascontiguousarray(W2[c]),
            "cid": np.full((1, 1), float(c), np.float32),
        })
    r2 = _run("moe_ffn", in_maps)
    call = np.concatenate([r2[c]["contrib"] for c in range(NCORES)]
                      + [np.zeros((1, D), np.float32)], 0)
    sall = r2[0]["sall"]
    in_maps = []
    for c in range(NCORES):
        rows = core_rows(c)
        in_maps.append({
            "x": np.ascontiguousarray(x_full[rows]),
            "call": call,
            "sown": np.ascontiguousarray(sall[rows]),
        })
    r3 = _run("moe_combine", in_maps)
    xo = np.zeros((B * T, D), np.float32)
    for c in range(NCORES):
        xo[core_rows(c)] = r3[c]["xo"]
    return xo


def ffn1_sublayer(x_full, w_fc, w_proj, u, v, fast_b, gate):
    common = {"wfc": np.ascontiguousarray(w_fc), "wpj": np.ascontiguousarray(w_proj),
              "uu": np.ascontiguousarray(u), "vv": np.ascontiguousarray(v),
              "fb": np.ascontiguousarray(fast_b),
              "gate": np.full((1, 1), np.float32(gate), np.float32)}
    in_maps = [dict(common, x=np.ascontiguousarray(x_full[core_rows(c)]))
               for c in range(NCORES)]
    r = _run("ffn1", in_maps)
    xo = np.zeros((B * T, D), np.float32)
    for c in range(NCORES):
        xo[core_rows(c)] = r[c]["xo"]
    return xo


def _causal_mask(c):
    r = c % 4
    dk = np.arange(16)[:, None, None]
    pk = np.arange(128)[None, :, None]
    f = np.arange(512)[None, None, :]
    allow = (dk - r) * 128 + pk <= (f // 128) * 512 + (f % 128)
    return allow.astype(BF16NP)


def attn_sublayer(layer, x_full, W, bias_qkv=None, W_proj=None, bias_proj=None):
    if layer == 0:
        r1 = _run("qkv0", [{"x": np.ascontiguousarray(x_full[core_rows(c)]),
                            "wat": np.ascontiguousarray(W)} for c in range(NCORES)])
    else:
        r1 = _run("qkv1", [{"x": np.ascontiguousarray(x_full[core_rows(c)]),
                            "wat": np.ascontiguousarray(W),
                            "bqkv": np.ascontiguousarray(bias_qkv.reshape(1, -1))}
                           for c in range(NCORES)])
    K_full = np.zeros((B, NH, HS, T), BF16NP)
    V_full = np.zeros((B, NH, T, HS), BF16NP)
    for c in range(NCORES):
        b = c // 4
        pos = core_rows(c) - b * T
        K_full[b][:, :, pos] = r1[c]["ko"]
        V_full[b][:, pos, :] = r1[c]["vo"]
    in_maps = []
    for c in range(NCORES):
        b = c // 4
        wp_dev = np.ascontiguousarray(W_proj.T) if layer == 0 else np.ascontiguousarray(W_proj)
        m = {"x": np.ascontiguousarray(x_full[core_rows(c)]),
             "qh": r1[c]["qo"], "kh": np.ascontiguousarray(K_full[b]),
             "vh": np.ascontiguousarray(V_full[b]),
             "wp": wp_dev}
        if layer == 0:
            m["cm"] = _causal_mask(c)
        else:
            m["bp"] = np.ascontiguousarray(bias_proj.reshape(1, -1))
        in_maps.append(m)
    r2 = _run("attn0" if layer == 0 else "attn1", in_maps)
    xo = np.zeros((B * T, D), np.float32)
    for c in range(NCORES):
        xo[core_rows(c)] = r2[c]["xo"]
    return xo


def kernel(L0_attn_x, L0_attn_W_attn, L0_attn_W_proj, L0_attn_pos_emb,
           L0_attn_W_fc, L0_attn_W_cproj, L0_attn_ln_w, L0_ffn_Wg, L0_ffn_W1,
           L0_ffn_W2, L1_attn_w_qkv, L1_attn_b_qkv, L1_attn_w_proj,
           L1_attn_b_proj, L1_ffn_fast_b, L1_ffn_w_fc, L1_ffn_w_proj,
           L1_ffn_u, L1_ffn_v, L1_ffn_gate):
    x = np.ascontiguousarray(np.asarray(L0_attn_x, np.float32).reshape(B * T, D))
    x = moe_sublayer(x, L0_ffn_Wg, L0_ffn_W1, L0_ffn_W2)
    x = attn_sublayer(0, x, L0_attn_W_attn, W_proj=L0_attn_W_proj)
    x = moe_sublayer(x, L0_ffn_Wg, L0_ffn_W1, L0_ffn_W2)
    x = ffn1_sublayer(x, L1_ffn_w_fc, L1_ffn_w_proj, L1_ffn_u, L1_ffn_v,
                      L1_ffn_fast_b, float(L1_ffn_gate))
    x = attn_sublayer(1, x, L1_attn_w_qkv, bias_qkv=np.asarray(L1_attn_b_qkv),
                      W_proj=L1_attn_w_proj, bias_proj=np.asarray(L1_attn_b_proj))
    x = ffn1_sublayer(x, L1_ffn_w_fc, L1_ffn_w_proj, L1_ffn_u, L1_ffn_v,
                      L1_ffn_fast_b, float(L1_ffn_gate))
    return x.reshape(B, T, D)


# version 03 (final)
# speedup vs baseline: 2.3923x; 1.5502x over previous
import numpy as np
import ml_dtypes
from contextlib import ExitStack

import concourse.bass as bass
import concourse.mybir as mybir
from concourse.bass_utils import run_bass_kernel_spmd
from concourse.bass import IndirectOffsetOnAxis

F32 = mybir.dt.float32
BF16 = mybir.dt.bfloat16
I32 = mybir.dt.int32
AF = mybir.ActivationFunctionType
ALU = mybir.AluOpType
AX = mybir.AxisListType
BF16NP = ml_dtypes.bfloat16

NCORES = 8
B, T, D = 2, 8192, 1024
TPC = 2048
NT = TPC // 128
NBLK = 128
NH, HS = 16, 64
NE, CAP = 8, 5120
FFN_H = 4096
RK = 16
BIG = 1.0e9
NEG = -1.0e30


def own_blocks(c):
    return [(c // 4) * 64 + (c % 4) + 4 * i for i in range(NT)]


def core_rows(c):
    bl = np.array(own_blocks(c))
    return (bl[:, None] * 128 + np.arange(128)[None, :]).reshape(-1)



class S:

    def __init__(self, nc, ctx, name):
        self.h = ctx.enter_context(nc.semaphore(name))
        self.n = 0

    def inc(self, instr, k=1):
        instr.then_inc(self.h, k)
        self.n += k
        return instr

    def wait(self, eng):
        if self.n:
            eng.wait_ge(self.h, self.n)


class Env:
    def __init__(self, nc, ctx):
        self.nc, self.ctx = nc, ctx
        self.block = ctx.enter_context(nc.Block())
        self.ld = S(nc, ctx, "ld")
        self.ts = S(nc, ctx, "ts")
        self.vs = S(nc, ctx, "vs")
        self.ss = S(nc, ctx, "ss")
        self.gs = S(nc, ctx, "gs")
        self.cc = S(nc, ctx, "cc")
        self._consts = {}

    def sb(self, name, shape, dt):
        return self.ctx.enter_context(self.nc.sbuf_tensor(name, shape, dt))

    def ps(self, name, shape, dt=F32):
        return self.ctx.enter_context(self.nc.psum_tensor(name, shape, dt))

    def const_ident(self, dt):
        key = ("ident", dt)
        if key not in self._consts:
            t = self.sb(f"ident{'32' if dt == F32 else '16'}", [128, 128], dt)

            @self.block.gpsimd
            def _(g):
                self.gs.inc(g.memset(t[:], 1.0))
                self.gs.wait(g)
                self.gs.inc(g.affine_select(t[:], t[:], [[1, 128]], ALU.is_equal,
                                            0.0, channel_multiplier=-1))
            self._consts[key] = t
        return self._consts[key]

    def const_ltri(self):
        if "ltri" not in self._consts:
            t = self.sb("ltri", [128, 128], F32)

            @self.block.gpsimd
            def _(g):
                self.gs.inc(g.memset(t[:], 1.0))
                self.gs.wait(g)
                self.gs.inc(g.affine_select(t[:], t[:], [[1, 128]], ALU.is_gt,
                                            0.0, channel_multiplier=-1))
            self._consts["ltri"] = t
        return self._consts["ltri"]

    def const_iota8(self):
        if "iota8" not in self._consts:
            t = self.sb("iota8", [128, 8], F32)

            @self.block.gpsimd
            def _(g):
                self.gs.inc(g.iota(t[:], [[1, 8]], channel_multiplier=0,
                                   allow_small_or_imprecise_dtypes=True))
            self._consts["iota8"] = t
        return self._consts["iota8"]

    def const_iotap(self):
        if "iotap" not in self._consts:
            t = self.sb("iotap", [128, 1], F32)

            @self.block.gpsimd
            def _(g):
                self.gs.inc(g.iota(t[:], [[0, 1]], channel_multiplier=1,
                                   allow_small_or_imprecise_dtypes=True))
            self._consts["iotap"] = t
        return self._consts["iotap"]

    def const_ones(self, n=128):
        key = ("ones", n)
        if key not in self._consts:
            t = self.sb(f"ones{n}", [n, 1], F32)

            @self.block.gpsimd
            def _(g):
                self.gs.inc(g.memset(t[:], 1.0))
            self._consts[key] = t
        return self._consts[key]


def emit_ln_tile(env, x_ap, outs, mv, bn):
    blk = env.block

    @blk.vector
    def _(v):
        env.vs.inc(v.bn_stats(bn[:, 0:6], x_ap[:, 0:512]))
        env.vs.inc(v.bn_stats(bn[:, 6:12], x_ap[:, 512:1024]))
        env.vs.wait(v)
        env.vs.inc(v.bn_aggr(mv[:, 0:2], bn[:, 0:12]))
        env.vs.wait(v)
        env.vs.inc(v.tensor_scalar(mv[:, 2:3], mv[:, 1:2], 1e-5, None, ALU.add))

    @blk.scalar
    def _(s):
        env.vs.wait(s)
        env.ss.inc(s.activation(mv[:, 2:3], mv[:, 2:3], AF.Ln))
        env.ss.wait(s)
        env.ss.inc(s.activation(mv[:, 3:4], mv[:, 2:3], AF.Exp, scale=-0.5))

    @blk.vector
    def _(v):
        env.ss.wait(v)
        for o in outs:
            env.vs.inc(v.tensor_scalar(o, x_ap, mv[:, 0:1], mv[:, 3:4],
                                       ALU.subtract, ALU.mult))



def build_route():
    nc = bass.Bass("TRN2", target_bir_lowering=False)
    x = nc.dram_tensor("x", [TPC, D], F32, kind="ExternalInput").ap()
    wg = nc.dram_tensor("wg", [NE, D], F32, kind="ExternalInput").ap()
    xn_out = nc.dram_tensor("xn", [TPC, D], BF16, kind="ExternalOutput").ap()
    meta = nc.dram_tensor("meta", [TPC, 6], F32, kind="ExternalOutput").ap()
    tsum = nc.dram_tensor("tsum", [1, NT * NE], F32, kind="ExternalOutput").ap()
    with ExitStack() as ctx:
        env = Env(nc, ctx)
        blk = env.block
        ident32 = env.const_ident(F32)
        ltri = env.const_ltri()
        iota8 = env.const_iota8()
        ones128 = env.const_ones()

        xt = env.sb("xt", [128, 1024], F32)
        xnf = env.sb("xnf", [128, 1024], F32)
        xnb = env.sb("xnb", [128, 1024], BF16)
        xc = env.sb("xc", [128, 1024], F32)
        wgt = env.sb("wgt", [NE, 1024], F32)
        wgT = env.sb("wgT", [128, 64], F32)
        logit = env.sb("logit", [128, 8], F32)
        l2 = env.sb("l2", [128, 8], F32)
        m1 = env.sb("m1", [128, 8], F32)
        m2 = env.sb("m2", [128, 8], F32)
        twoh = env.sb("twoh", [128, 8], F32)
        csum = env.sb("csum", [128, 8], F32)
        tmp8 = env.sb("tmp8", [128, 8], F32)
        idx8 = env.sb("idx8", [128, 8], mybir.dt.uint32)
        r = env.sb("r", [128, 12], F32)
        bn = env.sb("bn", [128, 12], F32)
        mv = env.sb("mv", [128, 4], F32)
        sums = env.sb("sums", [1, 128], F32)
        metat = env.sb("metat", [128, 6], F32)

        psT = [env.ps("psT0", [128, 128], F32), env.ps("psT1", [128, 128], F32)]
        psW = env.ps("psW", [128, 64], F32)
        psL = env.ps("psL", [128, 8], F32)
        psC = env.ps("psC", [128, 8], F32)
        psS = env.ps("psS", [1, 8], F32)

        @blk.sync
        def _(sync):
            env.ld.inc(sync.dma_start(out=wgt[:], in_=wg), 16)

        @blk.tensor
        def _(t):
            env.ld.wait(t)
            env.gs.wait(t)
            for c in range(8):
                env.ts.inc(t.transpose(psW[:, c * 8:(c + 1) * 8],
                                       wgt[:, c * 128:(c + 1) * 128], ident32[0:8, 0:8]))

        @blk.vector
        def _(v):
            env.ts.wait(v)
            env.vs.inc(v.tensor_copy(wgT[:], psW[:]))

        for i in range(NT):
            @blk.sync
            def _(sync, i=i):
                env.vs.wait(sync)
                env.ld.inc(sync.dma_start(out=xt[:], in_=x[i * 128:(i + 1) * 128, :]), 16)

            @blk.vector
            def _(v, i=i):
                env.ld.wait(v)
            emit_ln_tile(env, xt[:], [xnf[:], xnb[:]], mv, bn)

            for c in range(8):
                @blk.tensor
                def _(t, c=c):
                    env.vs.wait(t)
                    env.ts.inc(t.transpose(psT[c % 2][:], xnf[:, c * 128:(c + 1) * 128],
                                           ident32[:]))

                @blk.vector
                def _(v, c=c):
                    env.ts.wait(v)
                    env.vs.inc(v.tensor_copy(xc[:, c * 128:(c + 1) * 128], psT[c % 2][:]))

            @blk.tensor
            def _(t, i=i):
                env.vs.wait(t)
                for c in range(8):
                    mm = t.matmul(psL[:], xc[:, c * 128:(c + 1) * 128],
                                  wgT[:, c * 8:(c + 1) * 8],
                                  start=(c == 0), stop=(c == 7))
                env.ts.inc(mm)

            @blk.vector
            def _(v, i=i):
                env.ts.wait(v)
                env.vs.inc(v.tensor_copy(logit[:], psL[:]))
                env.vs.wait(v)
                env.vs.inc(v.max(l2[:], logit[:]))
                env.vs.wait(v)
                env.vs.inc(v.max_index(idx8[:], l2[:], logit[:]))
                env.vs.wait(v)
                env.vs.inc(v.tensor_copy(metat[:, 0:1], idx8[:, 0:1]))
                env.vs.inc(v.tensor_copy(metat[:, 1:2], idx8[:, 1:2]))
                env.vs.wait(v)
                env.vs.inc(v.tensor_scalar(m1[:], iota8[:], metat[:, 0:1], None, ALU.is_equal))
                env.vs.inc(v.tensor_scalar(m2[:], iota8[:], metat[:, 1:2], None, ALU.is_equal))
                env.vs.inc(v.tensor_sub(r[:, 2:3], l2[:, 1:2], l2[:, 0:1]))

            @blk.scalar
            def _(s, i=i):
                env.vs.wait(s)
                env.ss.inc(s.activation(r[:, 3:4], r[:, 2:3], AF.Exp))

            @blk.vector
            def _(v, i=i):
                env.ss.wait(v)
                env.vs.inc(v.tensor_scalar(r[:, 4:5], r[:, 3:4], 1.0, None, ALU.add))
                env.vs.wait(v)
                env.vs.inc(v.reciprocal(metat[:, 2:3], r[:, 4:5]))
                env.vs.wait(v)
                env.vs.inc(v.tensor_mul(metat[:, 3:4], r[:, 3:4], metat[:, 2:3]))
                env.vs.inc(v.tensor_add(twoh[:], m1[:], m2[:]))

            @blk.tensor
            def _(t, i=i):
                env.vs.wait(t)
                env.ts.inc(t.matmul(psC[:], ltri[:], twoh[:], start=True, stop=True))
                env.ts.inc(t.matmul(psS[:], ones128[:], twoh[:], start=True, stop=True))

            @blk.vector
            def _(v, i=i):
                env.ts.wait(v)
                env.vs.inc(v.tensor_copy(csum[:], psC[:]))
                env.vs.inc(v.tensor_copy(sums[0:1, i * 8:(i + 1) * 8], psS[:]))
                env.vs.wait(v)
                env.vs.inc(v.tensor_mul(tmp8[:], m1[:], csum[:]))
                env.vs.wait(v)
                env.vs.inc(v.reduce_sum(metat[:, 4:5], tmp8[:], axis=AX.X))
                env.vs.inc(v.tensor_mul(tmp8[:], m2[:], csum[:]))
                env.vs.wait(v)
                env.vs.inc(v.reduce_sum(metat[:, 5:6], tmp8[:], axis=AX.X))

            @blk.sync
            def _(sync, i=i):
                env.vs.wait(sync)
                env.ld.inc(sync.dma_start(out=xn_out[i * 128:(i + 1) * 128, :], in_=xnb[:]), 16)
                env.ld.inc(sync.dma_start(out=meta[i * 128:(i + 1) * 128, :], in_=metat[:]), 16)

        @blk.sync
        def _(sync):
            env.vs.wait(sync)
            env.ld.inc(sync.dma_start(out=tsum, in_=sums[0:1, :]), 16)
            env.ld.wait(sync)
    return nc



def build_moe_ffn():
    nc = bass.Bass("TRN2", target_bir_lowering=False)
    xn_all = nc.dram_tensor("xn_all", [B * T, D], BF16, kind="ExternalInput").ap()
    meta = nc.dram_tensor("meta", [B * T, 6], F32, kind="ExternalInput").ap()
    tsg = nc.dram_tensor("tsg", [NBLK, NE], F32, kind="ExternalInput").ap()
    w1 = nc.dram_tensor("w1", [D, 2 * 2048], F32, kind="ExternalInput").ap()
    w2 = nc.dram_tensor("w2", [2048, D], F32, kind="ExternalInput").ap()
    cid = nc.dram_tensor("cid", [1, 1], F32, kind="ExternalInput").ap()
    contrib = nc.dram_tensor("contrib", [CAP, D], F32, kind="ExternalOutput").ap()
    sall = nc.dram_tensor("sall", [128, NBLK * 2], F32, kind="ExternalOutput").ap()

    tdram = nc.dram_tensor("tdram", [CAP, 2], F32)

    with ExitStack() as ctx:
        blk = ctx.enter_context(nc.Block())
        lw = C(nc, ctx, "lw")
        lts = C(nc, ctx, "lts")
        lob = C(nc, ctx, "lob")
        lm = [C(nc, ctx, "lm0"), C(nc, ctx, "lm1")]
        lsc = C(nc, ctx, "lsc")
        lsa = C(nc, ctx, "lsa")
        lt = C(nc, ctx, "lt")
        lgx = C(nc, ctx, "lgx")
        lo = [C(nc, ctx, "lo0"), C(nc, ctx, "lo1")]
        ts = C(nc, ctx, "ts")
        ss = C(nc, ctx, "ss")
        vs = C(nc, ctx, "vs")
        gs = C(nc, ctx, "gs")

        ltri = ctx.enter_context(nc.sbuf_tensor("ltri", [128, 128], F32))
        iota8 = ctx.enter_context(nc.sbuf_tensor("iota8", [128, 8], F32))
        iotap2 = ctx.enter_context(nc.sbuf_tensor("iotap2", [128, 2], F32))
        ident16 = ctx.enter_context(nc.sbuf_tensor("ident16", [128, 128], BF16))

        tsg_sb = ctx.enter_context(nc.sbuf_tensor("tsg_sb", [128, 8], F32))
        offs_sb = ctx.enter_context(nc.sbuf_tensor("offs_sb", [128, 8], F32))
        OFFB = ctx.enter_context(nc.sbuf_tensor("OFFB", [128, 1024], F32))
        cidb = ctx.enter_context(nc.sbuf_tensor("cidb", [128, 1], F32))
        mtg = [ctx.enter_context(nc.sbuf_tensor(f"mtg{b}", [128, 48], F32)) for b in range(2)]
        init80 = ctx.enter_context(nc.sbuf_tensor("init80", [128, 80], F32))
        scr = ctx.enter_context(nc.sbuf_tensor("scr", [128, 8 * 24], F32))
        vals_all = ctx.enter_context(nc.sbuf_tensor("vals_all", [128, NBLK * 4], F32))
        scidx_all = ctx.enter_context(nc.sbuf_tensor("scidx_all", [128, NBLK * 2], I32))
        stall = ctx.enter_context(nc.sbuf_tensor("stall", [128, NBLK * 2], F32))

        tg = ctx.enter_context(nc.sbuf_tensor("tg", [128, 80], F32))
        gidx = ctx.enter_context(nc.sbuf_tensor("gidx", [128, 4], I32))
        rg = ctx.enter_context(nc.sbuf_tensor("rg", [128, 4], F32))
        gx = [ctx.enter_context(nc.sbuf_tensor(f"gx{a}", [128, 1024], BF16)) for a in range(4)]
        xfm = [ctx.enter_context(nc.sbuf_tensor(f"xfm{cc}", [128, 512], BF16)) for cc in range(8)]
        w1b = [ctx.enter_context(nc.sbuf_tensor(f"w1b{cc}", [128, 4096], BF16)) for cc in range(8)]
        w2b = [ctx.enter_context(nc.sbuf_tensor(f"w2b{ii}", [128, 1024], BF16)) for ii in range(16)]
        haf = [ctx.enter_context(nc.sbuf_tensor(f"haf{ii}", [128, 512], BF16)) for ii in range(16)]
        gel = [ctx.enter_context(nc.sbuf_tensor(f"gel{b}", [128, 512], F32)) for b in range(2)]
        ct = [ctx.enter_context(nc.sbuf_tensor(f"ct{b}", [128, 1024], F32)) for b in range(2)]

        psO = ctx.enter_context(nc.psum_tensor("psO", [128, 8], F32))
        psT = [ctx.enter_context(nc.psum_tensor(f"psT{b}", [128, 1024], BF16)) for b in range(2)]
        psA = ctx.enter_context(nc.psum_tensor("psA", [128, 512], F32))
        psB = ctx.enter_context(nc.psum_tensor("psB", [128, 512], F32))
        psC = ctx.enter_context(nc.psum_tensor("psC", [128, 512], F32))
        psD = ctx.enter_context(nc.psum_tensor("psD", [128, 512], F32))

        @blk.gpsimd
        def _(g):
            gs.inc(g.memset(ltri[:], 1.0))
            g.wait_ge(gs.h, gs.n)
            gs.inc(g.affine_select(ltri[:], ltri[:], [[1, 128]], ALU.is_gt,
                                   0.0, channel_multiplier=-1))
            gs.inc(g.iota(iota8[:], [[1, 8]], channel_multiplier=0,
                          allow_small_or_imprecise_dtypes=True))
            gs.inc(g.iota(iotap2[:], [[0, 2]], channel_multiplier=1,
                          allow_small_or_imprecise_dtypes=True))
            gs.inc(g.memset(ident16[:], 1.0))
            g.wait_ge(gs.h, gs.n)
            gs.inc(g.affine_select(ident16[:], ident16[:], [[1, 128]], ALU.is_equal,
                                   0.0, channel_multiplier=-1))
        CONST_GS = gs.n

        @blk.sync
        def _(sync):
            lts.inc(sync.dma_start(out=tsg_sb[:], in_=tsg), 16)
            lts.inc(sync.dma_start(
                out=cidb[:], in_=cid.unsqueeze(0).broadcast_to([128, 1, 1]).rearrange("p a b -> p (a b)")), 16)

        @blk.gpsimd
        def _(g):
            for c8 in range(8):
                lw.inc(g.dma_start(out=w1b[c8][:], in_=w1[c8 * 128:(c8 + 1) * 128, :]), 16)
            for i16 in range(16):
                lw.inc(g.dma_start(out=w2b[i16][:], in_=w2[i16 * 128:(i16 + 1) * 128, :]), 16)
        W_LW = lw.n

        @blk.tensor
        def _(t):
            t.wait_ge(lts.h, 32)
            t.wait_ge(gs.h, CONST_GS)
            ts.inc(t.matmul(psO[:], ltri[:], tsg_sb[:], start=True, stop=True))

        @blk.vector
        def _(v):
            v.wait_ge(ts.h, ts.n)
            vs.inc(v.tensor_copy(offs_sb[:], psO[:]))
            vs.inc(v.memset(init80[:], BIG))
            v.wait_ge(vs.h, vs.n)
            vs.inc(v.memset(init80[:].rearrange("p (a w) -> p a w", w=2)[:, :, 1:2], 0.0))
        OFF_VS = vs.n

        offs_dram = nc.dram_tensor("offs_dram", [NBLK, NE], F32)

        @blk.sync
        def _(sync):
            sync.wait_ge(vs.h, OFF_VS)
            lob.inc(sync.dma_start(out=offs_dram.ap(), in_=offs_sb[:]), 16)
            lob.inc(sync.dma_start(
                out=tdram.ap().rearrange("(p a) w -> p (a w)", p=128), in_=init80[:]), 16)
            sync.wait_ge(lob.h, lob.n)
            lob.inc(sync.dma_start(
                out=OFFB[:].rearrange("p (j e) -> p j e", e=8),
                in_=offs_dram.ap().unsqueeze(0).broadcast_to([128, NBLK, NE])), 16)
        OFFB_LW = lob.n

        bc_reg = [None]

        @blk.gpsimd
        def _(g):
            bc_reg[0] = g.to_reg(CAP - 1)

        grp_done_vs = {}
        chain_vs = {}
        sc_emitted = {}

        def s_off(jj, a, b):
            return scr[:, jj * 24 + a:jj * 24 + b]

        for jg in range(NBLK // 8):
            bufm = jg % 2

            @blk.sync
            def _(sync, jg=jg, bufm=bufm):
                if (jg - 2) in grp_done_vs:
                    sync.wait_ge(vs.h, grp_done_vs[jg - 2])
                lm[bufm].inc(sync.dma_start(
                    out=mtg[bufm][:].rearrange("p (j m) -> p j m", m=6),
                    in_=meta[jg * 1024:(jg + 1) * 1024, :].rearrange(
                        "(j p) m -> p j m", p=128)), 16)
            my_lm = lm[bufm].n

            def MT(jj, c0, c1):
                return mtg[bufm][:, jj * 6 + c0:jj * 6 + c1]

            for stage in range(10):
                for jj in range(8):
                    j = jg * 8 + jj

                    @blk.vector
                    def _(v, jj=jj, j=j, stage=stage, bufm=bufm, my_lm=my_lm):
                        if stage == 0:
                            v.wait_ge(lm[bufm].h, my_lm)
                            v.wait_ge(lob.h, OFFB_LW)
                            v.wait_ge(gs.h, CONST_GS)
                            if (jg - 1) in grp_done_vs:
                                v.wait_ge(vs.h, grp_done_vs[jg - 1])
                            vs.inc(v.tensor_scalar(s_off(jj, 0, 8), iota8[:],
                                                   MT(jj, 0, 1), None, ALU.is_equal))
                            chain_vs[(j, 0)] = vs.inc(v.tensor_scalar(
                                s_off(jj, 8, 16), iota8[:], MT(jj, 1, 2), None,
                                ALU.is_equal))
                        elif stage == 1:
                            v.wait_ge(vs.h, chain_vs[(j, 0)])
                            vs.inc(v.tensor_mul(s_off(jj, 0, 8), s_off(jj, 0, 8),
                                                OFFB[:, j * 8:(j + 1) * 8]))
                            chain_vs[(j, 1)] = vs.inc(v.tensor_mul(
                                s_off(jj, 8, 16), s_off(jj, 8, 16),
                                OFFB[:, j * 8:(j + 1) * 8]))
                        elif stage == 2:
                            v.wait_ge(vs.h, chain_vs[(j, 1)])
                            vs.inc(v.reduce_sum(s_off(jj, 16, 17), s_off(jj, 0, 8),
                                                axis=AX.X))
                            chain_vs[(j, 2)] = vs.inc(v.reduce_sum(
                                s_off(jj, 17, 18), s_off(jj, 8, 16), axis=AX.X))
                        elif stage == 3:
                            v.wait_ge(vs.h, chain_vs[(j, 2)])
                            chain_vs[(j, 3)] = vs.inc(v.tensor_add(
                                s_off(jj, 18, 20), s_off(jj, 16, 18), MT(jj, 4, 6)))
                        elif stage == 4:
                            v.wait_ge(vs.h, chain_vs[(j, 3)])
                            vs.inc(v.tensor_scalar(s_off(jj, 20, 22), s_off(jj, 18, 20),
                                                   float(CAP), None, ALU.is_lt))
                            v.wait_ge(vs.h, vs.n)
                            chain_vs[(j, 4)] = vs.inc(v.tensor_mul(
                                vals_all[:, j * 4 + 1:j * 4 + 4:2], MT(jj, 2, 4),
                                s_off(jj, 20, 22)))
                        elif stage == 5:
                            chain_vs[(j, 5)] = vs.inc(v.tensor_scalar(
                                vals_all[:, j * 4 + 0:j * 4 + 3:2], iotap2[:],
                                float(j * 128), None, ALU.add))
                        elif stage == 6:
                            v.wait_ge(vs.h, chain_vs[(j, 4)])
                            vs.inc(v.tensor_scalar(s_off(jj, 22, 24), s_off(jj, 20, 22),
                                                   -1.0, -BIG, ALU.add, ALU.mult))
                            vs.inc(v.tensor_scalar(s_off(jj, 0, 2), MT(jj, 0, 2),
                                                   float(CAP), None, ALU.mult))
                            v.wait_ge(vs.h, vs.n)
                            vs.inc(v.tensor_add(s_off(jj, 2, 4), s_off(jj, 0, 2),
                                                s_off(jj, 18, 20)))
                            v.wait_ge(vs.h, vs.n)
                            chain_vs[(j, 6)] = vs.inc(v.tensor_add(
                                stall[:, j * 2:j * 2 + 2], s_off(jj, 2, 4),
                                s_off(jj, 22, 24)))
                        elif stage == 7:
                            chain_vs[(j, 7)] = vs.inc(v.tensor_scalar(
                                s_off(jj, 4, 6), MT(jj, 0, 2), cidb[:, 0:1], None,
                                ALU.not_equal))
                        elif stage == 8:
                            v.wait_ge(vs.h, chain_vs[(j, 7)])
                            chain_vs[(j, 8)] = vs.inc(v.scalar_tensor_tensor(
                                s_off(jj, 6, 8), s_off(jj, 4, 6), BIG,
                                s_off(jj, 18, 20), ALU.mult, ALU.add))
                        elif stage == 9:
                            v.wait_ge(vs.h, chain_vs[(j, 8)])
                            chain_vs[(j, 9)] = vs.inc(v.tensor_copy(
                                scidx_all[:, j * 2:j * 2 + 2], s_off(jj, 6, 8)))

                    if stage == 9:
                        for k in range(2):
                            @blk.gpsimd
                            def _(e, j=j, k=k):
                                e.wait_ge(vs.h, chain_vs[(j, 9)])
                                lsc.inc(e.indirect_dma_start(
                                    out=tdram.ap(),
                                    out_offset=IndirectOffsetOnAxis(
                                        scidx_all[:, j * 2 + k:j * 2 + k + 1], 0),
                                    in_=vals_all[:, j * 4 + 2 * k:j * 4 + 2 * k + 2],
                                    in_offset=None, bounds_check=bc_reg[0],
                                    oob_is_err=False), 16)
            grp_done_vs[jg] = chain_vs[(jg * 8 + 7, 9)]

        ALL_CHAIN_VS = vs.n

        @blk.sync
        def _(sync):
            sync.wait_ge(vs.h, ALL_CHAIN_VS)
            lsa.inc(sync.dma_start(out=sall, in_=stall[:]), 16)

        @blk.sync
        def _(sync):
            sync.wait_ge(lsc.h, lsc.n)
            lt.inc(sync.dma_start(
                out=tg[:].rearrange("p (a w) -> p a w", w=2),
                in_=tdram.ap()[0:CAP, :].rearrange("(a p) w -> p a w", p=128)), 16)
        TG_LT = lt.n

        gx_lt = {}
        gel_ss = {}
        mul_vs = {-1: 0, -2: 0}
        tr_ts = {}
        cp_vs = {}
        w1_ts = {}
        w2_ts = {}
        ct_vs = {-1: 0, -2: 0}
        ct_lo = {-1: 0, -2: 0}
        gidx_vs = {}

        for G in range(10):
            @blk.vector
            def _(v, G=G):
                v.wait_ge(lt.h, TG_LT)
                if (G - 1) in gx_lt:
                    v.wait_ge(lgx.h, gx_lt[G - 1])
                for a4 in range(4):
                    a = G * 4 + a4
                    v.wait_ge(vs.h, vs.n if a4 == 0 else 0)
                    vs.inc(v.tensor_scalar(rg[:, a4:a4 + 1], tg[:, 2 * a:2 * a + 1],
                                           float(B * T - 1), None, ALU.min))
                v.wait_ge(vs.h, vs.n)
                gidx_vs[G] = vs.inc(v.tensor_copy(gidx[:], rg[:, 0:4]))

            @blk.gpsimd
            def _(g, G=G):
                g.wait_ge(vs.h, gidx_vs[G])
                if G >= 1:
                    g.wait_ge(ts.h, tr_ts.get((G - 1, 7), 0))
                for a4 in range(4):
                    lgx.inc(g.indirect_dma_start(
                        out=gx[a4][:, :], out_offset=None,
                        in_=xn_all, in_offset=IndirectOffsetOnAxis(gidx[:, a4:a4 + 1], 0)), 16)
            gx_lt[G] = my_gx = lgx.n

            for c8 in range(8):
                @blk.tensor
                def _(t, G=G, c8=c8):
                    if c8 == 0:
                        t.wait_ge(lgx.h, my_gx)
                        t.wait_ge(gs.h, CONST_GS)
                    if G >= 1 and c8 < 2:
                        t.wait_ge(ss.h, gel_ss.get((G - 1, 15), 0))
                        t.wait_ge(ts.h, w2_ts.get((G - 1, 3), 0))
                    t.wait_ge(vs.h, cp_vs.get((G, c8 - 2), cp_vs.get((G - 1, c8 + 6), 0)))
                    for a4 in range(4):
                        mm = t.transpose(psT[c8 % 2][:, a4 * 128:(a4 + 1) * 128],
                                         gx[a4][:, c8 * 128:(c8 + 1) * 128], ident16[:])
                    tr_ts[(G, c8)] = ts.inc(mm)

                @blk.vector
                def _(v, G=G, c8=c8):
                    v.wait_ge(ts.h, tr_ts[(G, c8)])
                    cp_vs[(G, c8)] = vs.inc(v.tensor_copy(xfm[c8][:], psT[c8 % 2][:, 0:512]))

            for fi in range(16):
                pp, pq = (psA, psB) if fi % 2 == 0 else (psC, psD)

                @blk.tensor
                def _(t, G=G, fi=fi, pp=pp, pq=pq):
                    if fi == 0:
                        t.wait_ge(vs.h, cp_vs[(G, 7)])
                        t.wait_ge(lw.h, W_LW)
                    if fi >= 2:
                        t.wait_ge(ss.h, gel_ss[(G, fi - 2)])
                        t.wait_ge(vs.h, mul_vs[fi - 2])
                    elif G >= 1:
                        t.wait_ge(ss.h, gel_ss[(G - 1, 14 + fi)])
                        t.wait_ge(vs.h, mul_vs[14 + fi - 16])
                    for c8 in range(8):
                        ma = t.matmul(pp[:], w1b[c8][:, fi * 128:(fi + 1) * 128],
                                      xfm[c8][:], start=(c8 == 0), stop=(c8 == 7))
                        mb = t.matmul(pq[:], w1b[c8][:, 2048 + fi * 128:2048 + (fi + 1) * 128],
                                      xfm[c8][:], start=(c8 == 0), stop=(c8 == 7))
                    w1_ts[(G, fi)] = ts.inc(mb)

                @blk.scalar
                def _(s, G=G, fi=fi, pp=pp):
                    s.wait_ge(ts.h, w1_ts[(G, fi)])
                    s.wait_ge(vs.h, mul_vs[fi - 2])
                    gel_ss[(G, fi)] = ss.inc(s.activation(gel[fi % 2][:], pp[:], AF.Gelu))

                @blk.vector
                def _(v, G=G, fi=fi, pq=pq):
                    v.wait_ge(ss.h, gel_ss[(G, fi)])
                    mul_vs[fi] = vs.inc(v.tensor_mul(haf[fi][:], gel[fi % 2][:], pq[:]))
            mul_vs[-1] = mul_vs[15]
            mul_vs[-2] = mul_vs[14]

            for m in range(4):
                pp, pq = (psA, psB) if m % 2 == 0 else (psC, psD)

                @blk.tensor
                def _(t, G=G, m=m, pp=pp, pq=pq):
                    if m < 2:
                        t.wait_ge(ss.h, gel_ss[(G, 14 + m)])
                        t.wait_ge(vs.h, mul_vs[14 + m - 16 + 16])
                    else:
                        t.wait_ge(vs.h, ct_vs[m - 2])
                    if m == 0:
                        t.wait_ge(vs.h, mul_vs[15])
                    for i16 in range(16):
                        mc = t.matmul(pp[:], haf[i16][:, m * 128:(m + 1) * 128],
                                      w2b[i16][:, 0:512], start=(i16 == 0), stop=(i16 == 15))
                        md = t.matmul(pq[:], haf[i16][:, m * 128:(m + 1) * 128],
                                      w2b[i16][:, 512:1024], start=(i16 == 0), stop=(i16 == 15))
                    w2_ts[(G, m)] = ts.inc(md)

                @blk.vector
                def _(v, G=G, m=m, pp=pp, pq=pq):
                    v.wait_ge(ts.h, w2_ts[(G, m)])
                    v.wait_ge(lo[m % 2].h, ct_lo[m - 2])
                    wcol = tg[:, 2 * (G * 4 + m) + 1:2 * (G * 4 + m) + 2]
                    vs.inc(v.tensor_scalar(ct[m % 2][:, 0:512], pp[:], wcol, None, ALU.mult))
                    v.wait_ge(vs.h, vs.n)
                    ct_vs[m] = vs.inc(v.tensor_scalar(ct[m % 2][:, 512:1024], pq[:],
                                                      wcol, None, ALU.mult))

                @blk.sync
                def _(sync, G=G, m=m):
                    sync.wait_ge(vs.h, ct_vs[m])
                    row0 = G * 512 + m * 128
                    ct_lo[m % 2] = lo[m % 2].inc(sync.dma_start(
                        out=contrib[row0:row0 + 128, :], in_=ct[m % 2][:]), 16)
            ct_vs[-1] = ct_vs[3]
            ct_vs[-2] = ct_vs[2]
            ct_lo[-1] = ct_lo[1]
            ct_lo[-2] = ct_lo[0]

        @blk.sync
        def _(sync):
            for b in range(2):
                sync.wait_ge(lo[b].h, lo[b].n)
                sync.wait_ge(lm[b].h, lm[b].n)
            sync.wait_ge(lsc.h, lsc.n)
            sync.wait_ge(lsa.h, lsa.n)
            sync.wait_ge(lt.h, lt.n)
            sync.wait_ge(lgx.h, lgx.n)
            sync.wait_ge(lw.h, lw.n)
            sync.wait_ge(lts.h, lts.n)
            sync.wait_ge(lob.h, lob.n)
    return nc



def build_moe_combine():
    nc = bass.Bass("TRN2", target_bir_lowering=False)
    x = nc.dram_tensor("x", [TPC, D], F32, kind="ExternalInput").ap()
    call = nc.dram_tensor("call", [NE * CAP + 1, D], F32, kind="ExternalInput").ap()
    sown = nc.dram_tensor("sown", [TPC, 2], F32, kind="ExternalInput").ap()
    xo = nc.dram_tensor("xo", [TPC, D], F32, kind="ExternalOutput").ap()
    with ExitStack() as ctx:
        env = Env(nc, ctx)
        blk = env.block
        xt = env.sb("xt", [128, 1024], F32)
        sot = env.sb("sot", [128, 2], F32)
        sidx = env.sb("sidx", [128, 2], I32)
        g1 = env.sb("g1", [128, 1024], F32)
        g2 = env.sb("g2", [128, 1024], F32)

        for i in range(NT):
            @blk.sync
            def _(sync, i=i):
                env.vs.wait(sync)
                env.ld.wait(sync)
                env.ld.inc(sync.dma_start(out=xt[:], in_=x[i * 128:(i + 1) * 128, :]), 16)
                env.ld.inc(sync.dma_start(out=sot[:], in_=sown[i * 128:(i + 1) * 128, :]), 16)

            @blk.vector
            def _(v, i=i):
                env.ld.wait(v)
                env.vs.inc(v.tensor_scalar(sot[:], sot[:], float(NE * CAP), None, ALU.min))
                env.vs.wait(v)
                env.vs.inc(v.tensor_copy(sidx[:], sot[:]))

            @blk.gpsimd
            def _(g, i=i):
                env.vs.wait(g)
                env.ld.inc(g.indirect_dma_start(
                    out=g1[:, :], out_offset=None, in_=call,
                    in_offset=IndirectOffsetOnAxis(sidx[:, 0:1], 0)), 16)
                env.ld.inc(g.indirect_dma_start(
                    out=g2[:, :], out_offset=None, in_=call,
                    in_offset=IndirectOffsetOnAxis(sidx[:, 1:2], 0)), 16)

            @blk.vector
            def _(v, i=i):
                env.ld.wait(v)
                env.vs.inc(v.scalar_tensor_tensor(xt[:], g1[:], 0.5, xt[:], ALU.mult, ALU.add))
                env.vs.wait(v)
                env.vs.inc(v.scalar_tensor_tensor(xt[:], g2[:], 0.5, xt[:], ALU.mult, ALU.add))

            @blk.sync
            def _(sync, i=i):
                env.vs.wait(sync)
                env.ld.inc(sync.dma_start(out=xo[i * 128:(i + 1) * 128, :], in_=xt[:]), 16)

        @blk.sync
        def _(sync):
            env.ld.wait(sync)
    return nc



def _emit_ternarize(env, w_ap, s_sb, si_sb, t_sb, out_ap, free):
    blk = env.block

    @blk.vector
    def _(v):
        env.ld.wait(v)
        env.vs.inc(v.reduce_sum(s_sb, w_ap, axis=AX.X, apply_absolute_value=True))
        env.vs.wait(v)
        env.vs.inc(v.tensor_scalar(s_sb, s_sb, 1.0 / free, 1e-5, ALU.mult, ALU.max))
        env.vs.wait(v)
        env.vs.inc(v.reciprocal(si_sb, s_sb))
        env.vs.wait(v)
        env.vs.inc(v.tensor_scalar(t_sb, w_ap, si_sb, None, ALU.mult))
        env.vs.wait(v)
        env.vs.inc(v.tensor_scalar(out_ap, t_sb, 0.5, None, ALU.is_gt))
        env.vs.inc(v.tensor_scalar(t_sb, t_sb, -0.5, None, ALU.is_lt))
        env.vs.wait(v)
        env.vs.inc(v.tensor_tensor(t_sb, out_ap, t_sb, ALU.subtract))
        env.vs.wait(v)
        env.vs.inc(v.tensor_scalar(out_ap, t_sb, s_sb, None, ALU.mult))


def build_ffn1():
    nc = bass.Bass("TRN2", target_bir_lowering=False)
    x = nc.dram_tensor("x", [TPC, D], F32, kind="ExternalInput").ap()
    wfc = nc.dram_tensor("wfc", [FFN_H, D], F32, kind="ExternalInput").ap()
    wpj = nc.dram_tensor("wpj", [D, FFN_H], F32, kind="ExternalInput").ap()
    uu = nc.dram_tensor("uu", [FFN_H, RK], F32, kind="ExternalInput").ap()
    vv = nc.dram_tensor("vv", [RK, D], F32, kind="ExternalInput").ap()
    fb = nc.dram_tensor("fb", [RK, RK], F32, kind="ExternalInput").ap()
    gate = nc.dram_tensor("gate", [1, 1], F32, kind="ExternalInput").ap()
    xo = nc.dram_tensor("xo", [TPC, D], F32, kind="ExternalOutput").ap()
    with ExitStack() as ctx:
        env = Env(nc, ctx)
        blk = env.block
        ident32 = env.const_ident(F32)
        ident16 = env.const_ident(BF16)

        wrow = env.sb("wrow", [128, FFN_H], F32)
        weff = env.sb("weff", [128, FFN_H], BF16)
        tsc = env.sb("tsc", [128, FFN_H], F32)
        s_sb = env.sb("s_sb", [128, 1], F32)
        si_sb = env.sb("si_sb", [128, 1], F32)
        wfct = [env.sb(f"wfct{c}", [128, FFN_H], BF16) for c in range(8)]
        wpjt = [env.sb(f"wpjt{i}", [128, D], BF16) for i in range(32)]
        vTb = env.sb("vTb", [128, 8 * RK], BF16)
        uT = env.sb("uT", [RK, FFN_H], BF16)
        fbT = env.sb("fbT", [RK, RK], BF16)
        fbsb = env.sb("fbsb", [RK, RK], F32)
        gb = env.sb("gb", [128, 1], F32)
        xt = env.sb("xt_f", [128, 1024], F32)
        xnb = env.sb("xnb_f", [128, 1024], BF16)
        bn = env.sb("bn_f", [128, 12], F32)
        mv = env.sb("mv_f", [128, 4], F32)
        xfm = [env.sb(f"xfmf{c}", [128, 128], BF16) for c in range(8)]
        t1s = env.sb("t1s", [RK, 128], BF16)
        t2s = env.sb("t2s", [RK, 128], BF16)
        hs = env.sb("hs", [128, 128], F32)
        hf = env.sb("hf", [128, 128], F32)
        rr = [env.sb(f"rr{c}", [128, 128], BF16) for c in range(32)]

        psT = [env.ps("psT0", [128, 1024], BF16), env.ps("psT1", [128, 1024], BF16)]
        psF = env.ps("psF", [128, 128], F32)
        psL = env.ps("psL", [128, 128], F32)
        ps1 = env.ps("ps1", [RK, 128], F32)
        psP = [env.ps("psP0", [128, 512], F32), env.ps("psP1", [128, 512], F32)]

        @blk.sync
        def _(sync):
            env.ld.inc(sync.dma_start(out=fbsb[:], in_=fb), 16)
            env.ld.inc(sync.dma_start(
                out=gb[:], in_=gate.unsqueeze(0).broadcast_to([128, 1, 1]).rearrange("p a b -> p (a b)")), 16)

        @blk.gpsimd
        def _(g):
            env.ld.inc(g.dma_start(out=wrow[:, 0:RK * 32].rearrange("p (a r) -> p a r", r=RK),
                                   in_=uu.rearrange("(a p) r -> p a r", p=128)), 16)
            env.ld.inc(g.dma_start(out=tsc[0:RK, 0:1024], in_=vv), 16)

        @blk.tensor
        def _(t):
            env.ld.wait(t)
            env.gs.wait(t)
            env.ts.inc(t.transpose(ps1[:, 0:RK], fbsb[:, :], ident32[0:RK, 0:RK]))

        @blk.vector
        def _(v):
            env.ts.wait(v)
            env.vs.inc(v.tensor_copy(fbT[:], ps1[:, 0:RK]))

        for a in range(32):
            @blk.tensor
            def _(t, a=a):
                env.vs.wait(t)
                env.ts.inc(t.transpose(ps1[0:RK, 0:128], wrow[:, a * RK:(a + 1) * RK], ident32[:]))

            @blk.vector
            def _(v, a=a):
                env.ts.wait(v)
                env.vs.inc(v.tensor_copy(uT[:, a * 128:(a + 1) * 128], ps1[0:RK, 0:128]))

        for c in range(8):
            @blk.tensor
            def _(t, c=c):
                env.vs.wait(t)
                env.ts.inc(t.transpose(psF[:, 0:RK], tsc[0:RK, c * 128:(c + 1) * 128],
                                       ident32[0:RK, 0:RK]))

            @blk.vector
            def _(v, c=c):
                env.ts.wait(v)
                env.vs.inc(v.tensor_copy(vTb[:, c * RK:(c + 1) * RK], psF[:, 0:RK]))

        for rt in range(32):
            @blk.sync
            def _(sync, rt=rt):
                env.vs.wait(sync)
                env.ld.inc(sync.dma_start(out=wrow[:, 0:1024], in_=wfc[rt * 128:(rt + 1) * 128, :]), 16)
            _emit_ternarize(env, wrow[:, 0:1024], s_sb[:], si_sb[:], tsc[:, 0:1024],
                            weff[:, 0:1024], 1024)
            for c in range(8):
                @blk.tensor
                def _(t, rt=rt, c=c):
                    env.vs.wait(t)
                    env.ts.inc(t.transpose(psT[c % 2][:, 0:128],
                                           weff[:, c * 128:(c + 1) * 128], ident16[:]))

                @blk.vector
                def _(v, rt=rt, c=c):
                    env.ts.wait(v)
                    env.vs.inc(v.tensor_copy(wfct[c][:, rt * 128:(rt + 1) * 128], psT[c % 2][:, 0:128]))

        for rt in range(8):
            @blk.sync
            def _(sync, rt=rt):
                env.vs.wait(sync)
                env.ld.inc(sync.dma_start(out=wrow[:], in_=wpj[rt * 128:(rt + 1) * 128, :]), 16)
            _emit_ternarize(env, wrow[:], s_sb[:], si_sb[:], tsc[:], weff[:], FFN_H)
            for c in range(32):
                @blk.tensor
                def _(t, rt=rt, c=c):
                    env.vs.wait(t)
                    env.ts.inc(t.transpose(psT[c % 2][:, 0:128],
                                           weff[:, c * 128:(c + 1) * 128], ident16[:]))

                @blk.vector
                def _(v, rt=rt, c=c):
                    env.ts.wait(v)
                    env.vs.inc(v.tensor_copy(wpjt[c][:, rt * 128:(rt + 1) * 128], psT[c % 2][:, 0:128]))

        for i in range(NT):
            @blk.sync
            def _(sync, i=i):
                env.vs.wait(sync)
                env.ld.wait(sync)
                env.ld.inc(sync.dma_start(out=xt[:], in_=x[i * 128:(i + 1) * 128, :]), 16)

            @blk.vector
            def _(v, i=i):
                env.ld.wait(v)
            emit_ln_tile(env, xt[:], [xnb[:]], mv, bn)

            for c in range(8):
                @blk.tensor
                def _(t, i=i, c=c):
                    env.vs.wait(t)
                    env.ts.inc(t.transpose(psT[c % 2][:, 0:128],
                                           xnb[:, c * 128:(c + 1) * 128], ident16[:]))

                @blk.vector
                def _(v, i=i, c=c):
                    env.ts.wait(v)
                    env.vs.inc(v.tensor_copy(xfm[c][:], psT[c % 2][:, 0:128]))

            @blk.tensor
            def _(t, i=i):
                env.vs.wait(t)
                for c in range(8):
                    mm = t.matmul(ps1[:, :], vTb[:, c * RK:(c + 1) * RK], xfm[c][:],
                                  start=(c == 0), stop=(c == 7))
                env.ts.inc(mm)

            @blk.vector
            def _(v, i=i):
                env.ts.wait(v)
                env.vs.inc(v.tensor_copy(t1s[:], ps1[:]))

            @blk.tensor
            def _(t, i=i):
                env.vs.wait(t)
                env.ts.inc(t.matmul(ps1[:], fbT[:], t1s[:], start=True, stop=True))

            @blk.vector
            def _(v, i=i):
                env.ts.wait(v)
                env.vs.inc(v.tensor_copy(t2s[:], ps1[:]))

            for fc_i in range(32):
                @blk.tensor
                def _(t, i=i, fc_i=fc_i):
                    env.vs.wait(t)
                    for c in range(8):
                        mm = t.matmul(psF[:], wfct[c][:, fc_i * 128:(fc_i + 1) * 128],
                                      xfm[c][:], start=(c == 0), stop=(c == 7))
                    env.ts.inc(mm)
                    env.ts.inc(t.matmul(psL[:], uT[:, fc_i * 128:(fc_i + 1) * 128], t2s[:],
                                        start=True, stop=True))

                @blk.vector
                def _(v, i=i, fc_i=fc_i):
                    env.ts.wait(v)
                    env.vs.inc(v.tensor_scalar(hs[:], psL[:], gb[:], None, ALU.mult))
                    env.vs.wait(v)
                    env.vs.inc(v.tensor_tensor(hf[:], psF[:], hs[:], ALU.add))
                    env.vs.wait(v)
                    env.vs.inc(v.tensor_scalar(hs[:], hf[:], 0.0, None, ALU.max))
                    env.vs.wait(v)
                    env.vs.inc(v.tensor_mul(rr[fc_i][:], hf[:], hs[:]))

            @blk.tensor
            def _(t, i=i):
                env.vs.wait(t)
                for half in range(2):
                    for c in range(32):
                        mm = t.matmul(psP[half][:], rr[c][:], wpjt[c][:, half * 512:(half + 1) * 512],
                                      start=(c == 0), stop=(c == 31))
                    env.ts.inc(mm)

            @blk.vector
            def _(v, i=i):
                env.ts.wait(v)
                env.vs.inc(v.scalar_tensor_tensor(xt[:, 0:512], psP[0][:], 0.5,
                                                  xt[:, 0:512], ALU.mult, ALU.add))
                env.vs.inc(v.scalar_tensor_tensor(xt[:, 512:1024], psP[1][:], 0.5,
                                                  xt[:, 512:1024], ALU.mult, ALU.add))

            @blk.sync
            def _(sync, i=i):
                env.vs.wait(sync)
                env.ld.inc(sync.dma_start(out=xo[i * 128:(i + 1) * 128, :], in_=xt[:]), 16)

        @blk.sync
        def _(sync):
            env.ld.wait(sync)
    return nc



def _build_qkv(layer):
    nc = bass.Bass("TRN2", target_bir_lowering=False)
    x = nc.dram_tensor("x", [TPC, D], F32, kind="ExternalInput").ap()
    if layer == 0:
        wat = nc.dram_tensor("wat", [3 * D, D], F32, kind="ExternalInput").ap()
    else:
        wat = nc.dram_tensor("wat", [D, 3 * D], F32, kind="ExternalInput").ap()
        bqkv = nc.dram_tensor("bqkv", [1, 3 * D], F32, kind="ExternalInput").ap()
    qo = nc.dram_tensor("qo", [NH, HS, TPC], BF16, kind="ExternalOutput").ap()
    ko = nc.dram_tensor("ko", [NH, HS, TPC], BF16, kind="ExternalOutput").ap()
    vo = nc.dram_tensor("vo", [NH, TPC, HS], BF16, kind="ExternalOutput").ap()
    with ExitStack() as ctx:
        env = Env(nc, ctx)
        blk = env.block
        ident16 = env.const_ident(BF16)

        watT = [env.sb(f"watT{c}", [128, 3 * D], BF16) for c in range(8)]
        wrow = env.sb("wrow_q", [128, 1024], BF16)
        xt = env.sb("xt_q", [128, 1024], F32)
        xnb = env.sb("xnb_q", [128, 1024], BF16)
        bn = env.sb("bn_q", [128, 12], F32)
        mv = env.sb("mv_q", [128, 4], F32)
        xfm = [env.sb(f"xfmq{c}", [128, 128], BF16) for c in range(8)]
        qks = env.sb("qks", [128, 128], BF16)
        vs_sb = env.sb("vs_sb", [128, 1024], BF16)
        if layer == 1:
            bcols = env.sb("bcols", [128, 16], F32)
            bvb = env.sb("bvb", [128, 1024], F32)
        psT = [env.ps("psTq0", [128, 1024], BF16), env.ps("psTq1", [128, 1024], BF16)]
        psQ = env.ps("psQ", [128, 128], F32)
        psV = [env.ps("psV0", [128, 512], F32), env.ps("psV1", [128, 512], F32)]

        if layer == 0:
            for rt in range(24):
                @blk.gpsimd
                def _(g, rt=rt):
                    env.vs.wait(g)
                    env.ld.inc(g.dma_start(out=wrow[:], in_=wat[rt * 128:(rt + 1) * 128, :]), 16)
                for c in range(8):
                    @blk.tensor
                    def _(t, rt=rt, c=c):
                        env.ld.wait(t)
                        env.gs.wait(t)
                        env.vs.wait(t)
                        env.ts.inc(t.transpose(psT[c % 2][:, 0:128],
                                               wrow[:, c * 128:(c + 1) * 128], ident16[:]))

                    @blk.vector
                    def _(v, rt=rt, c=c):
                        env.ts.wait(v)
                        env.vs.inc(v.tensor_copy(watT[c][:, rt * 128:(rt + 1) * 128],
                                                 psT[c % 2][:, 0:128]))
        else:
            @blk.gpsimd
            def _(g):
                for c in range(8):
                    env.ld.inc(g.dma_start(out=watT[c][:], in_=wat[c * 128:(c + 1) * 128, :]), 16)

            @blk.sync
            def _(sync):
                env.ld.inc(sync.dma_start(
                    out=bcols[:], in_=bqkv[0:1, 0:2048].rearrange("o (a p) -> p (o a)", p=128),
                    allow_slow_non_contiguous=True), 16)
                env.ld.inc(sync.dma_start(
                    out=bvb[:], in_=bqkv[0:1, 2048:3072].unsqueeze(0).broadcast_to(
                        [128, 1, 1024]).rearrange("p o f -> p (o f)")), 16)

        for i in range(NT):
            @blk.sync
            def _(sync, i=i):
                env.vs.wait(sync)
                env.ld.inc(sync.dma_start(out=xt[:], in_=x[i * 128:(i + 1) * 128, :]), 16)

            @blk.vector
            def _(v, i=i):
                env.ld.wait(v)
            emit_ln_tile(env, xt[:], [xnb[:]], mv, bn)

            for c in range(8):
                @blk.tensor
                def _(t, i=i, c=c):
                    env.vs.wait(t)
                    env.gs.wait(t)
                    env.ts.inc(t.transpose(psT[c % 2][:, 0:128],
                                           xnb[:, c * 128:(c + 1) * 128], ident16[:]))

                @blk.vector
                def _(v, i=i, c=c):
                    env.ts.wait(v)
                    env.vs.inc(v.tensor_copy(xfm[c][:], psT[c % 2][:, 0:128]))

            for f in range(16):
                @blk.tensor
                def _(t, i=i, f=f):
                    env.vs.wait(t)
                    for c in range(8):
                        mm = t.matmul(psQ[:], watT[c][:, f * 128:(f + 1) * 128],
                                      xfm[c][:], start=(c == 0), stop=(c == 7))
                    env.ts.inc(mm)

                @blk.vector
                def _(v, i=i, f=f):
                    env.ts.wait(v)
                    env.ld.wait(v)
                    if layer == 1:
                        env.vs.inc(v.tensor_scalar(qks[:], psQ[:], bcols[:, f:f + 1],
                                                   None, ALU.add))
                    else:
                        env.vs.inc(v.tensor_copy(qks[:], psQ[:]))

                @blk.sync
                def _(sync, i=i, f=f):
                    env.vs.wait(sync)
                    dst = qo if f < 8 else ko
                    h0 = (2 * f) % 16
                    env.ld.inc(sync.dma_start(out=dst[h0, :, i * 128:(i + 1) * 128],
                                              in_=qks[0:64, :]), 16)
                    env.ld.inc(sync.dma_start(out=dst[h0 + 1, :, i * 128:(i + 1) * 128],
                                              in_=qks[64:128, :]), 16)

            @blk.tensor
            def _(t, i=i):
                env.vs.wait(t)
                for half in range(2):
                    for c in range(8):
                        mm = t.matmul(psV[half][:], xfm[c][:],
                                      watT[c][:, 2048 + half * 512:2048 + (half + 1) * 512],
                                      start=(c == 0), stop=(c == 7))
                    env.ts.inc(mm)

            @blk.vector
            def _(v, i=i):
                env.ts.wait(v)
                env.ld.wait(v)
                if layer == 1:
                    env.vs.inc(v.tensor_tensor(vs_sb[:, 0:512], psV[0][:], bvb[:, 0:512], ALU.add))
                    env.vs.inc(v.tensor_tensor(vs_sb[:, 512:1024], psV[1][:], bvb[:, 512:1024], ALU.add))
                else:
                    env.vs.inc(v.tensor_copy(vs_sb[:, 0:512], psV[0][:]))
                    env.vs.inc(v.tensor_copy(vs_sb[:, 512:1024], psV[1][:]))

            @blk.sync
            def _(sync, i=i):
                env.vs.wait(sync)
                for h in range(NH):
                    env.ld.inc(sync.dma_start(out=vo[h, i * 128:(i + 1) * 128, :],
                                              in_=vs_sb[:, h * 64:(h + 1) * 64]), 16)

        @blk.sync
        def _(sync):
            env.ld.wait(sync)
    return nc


def build_qkv0():
    return _build_qkv(0)


def build_qkv1():
    return _build_qkv(1)



NQST = NT // 4
NKB = T // 128


class C:
    def __init__(self, nc, ctx, name):
        self.h = ctx.enter_context(nc.semaphore(name))
        self.n = 0

    def inc(self, instr, k=1):
        instr.then_inc(self.h, k)
        self.n += k
        return self.n


def _build_attn(layer):
    causal = layer == 0
    nc = bass.Bass("TRN2", target_bir_lowering=False)
    x = nc.dram_tensor("x", [TPC, D], F32, kind="ExternalInput").ap()
    qh = nc.dram_tensor("qh", [NH, HS, TPC], BF16, kind="ExternalInput").ap()
    kh = nc.dram_tensor("kh", [NH, HS, T], BF16, kind="ExternalInput").ap()
    vh = nc.dram_tensor("vh", [NH, T, HS], BF16, kind="ExternalInput").ap()
    wp = nc.dram_tensor("wp", [D, D], F32, kind="ExternalInput").ap()
    if causal:
        cm = nc.dram_tensor("cm", [16, 128, 512], BF16, kind="ExternalInput").ap()
    else:
        bp = nc.dram_tensor("bp", [1, D], F32, kind="ExternalInput").ap()
    xo = nc.dram_tensor("xo", [TPC, D], F32, kind="ExternalOutput").ap()
    with ExitStack() as ctx:
        blk = ctx.enter_context(nc.Block())
        ld = C(nc, ctx, "ld")
        lg = C(nc, ctx, "lg")
        lk = [C(nc, ctx, "lk0"), C(nc, ctx, "lk1")]
        lv = [C(nc, ctx, "lv0"), C(nc, ctx, "lv1")]
        ts = C(nc, ctx, "ts")
        ss = C(nc, ctx, "ss")
        vs = C(nc, ctx, "vs")
        gs = C(nc, ctx, "gs")

        kt = [ctx.enter_context(nc.sbuf_tensor(f"kt{b}", [65, T], BF16)) for b in range(2)]
        vt = [ctx.enter_context(nc.sbuf_tensor(f"vt{b}", [128, NKB * 65], BF16)) for b in range(2)]
        qsb = [ctx.enter_context(nc.sbuf_tensor(f"qsb{b}", [65, TPC], BF16)) for b in range(2)]
        ksq = ctx.enter_context(nc.sbuf_tensor("ksq", [64, T], BF16))
        qsq = ctx.enter_context(nc.sbuf_tensor("qsq", [64, TPC], BF16))
        krow = ctx.enter_context(nc.sbuf_tensor("krow", [1, T], F32))
        qrow = ctx.enter_context(nc.sbuf_tensor("qrow", [1, TPC], F32))
        km = ctx.enter_context(nc.sbuf_tensor("km", [1, 1], F32))
        mrow = ctx.enter_context(nc.sbuf_tensor("mrow", [1, TPC], F32))
        mne = ctx.enter_context(nc.sbuf_tensor("mne", [1, TPC], BF16))
        ones64 = ctx.enter_context(nc.sbuf_tensor("ones64", [64, 1], BF16))
        ptile = [ctx.enter_context(nc.sbuf_tensor(f"pt{i}", [128, 1024], BF16)) for i in range(3)]
        rrow = ctx.enter_context(nc.sbuf_tensor("rrow", [1, 512], F32))
        rb = ctx.enter_context(nc.sbuf_tensor("rb", [64, 512], F32))
        y8 = [ctx.enter_context(nc.sbuf_tensor(f"y8_{c}", [128, TPC], BF16)) for c in range(8)]
        ones1 = ctx.enter_context(nc.sbuf_tensor("ones1", [1, 64], F32))
        wpT = [ctx.enter_context(nc.sbuf_tensor(f"wpT{c}", [128, D], BF16)) for c in range(8)]
        xt = ctx.enter_context(nc.sbuf_tensor("xt_a", [128, 1024], F32))
        if causal:
            cmsb = ctx.enter_context(nc.sbuf_tensor("cmsb", [128, 16 * 512], BF16))
        else:
            bvb = ctx.enter_context(nc.sbuf_tensor("bvb_a", [128, 1024], F32))

        psS = [ctx.enter_context(nc.psum_tensor(f"psS{b}", [128, 1024], F32)) for b in range(2)]
        psY = ctx.enter_context(nc.psum_tensor("psY", [65, 512], F32))
        psR = ctx.enter_context(nc.psum_tensor("psR", [64, 512], F32))
        psN = [ctx.enter_context(nc.psum_tensor(f"psN{b}", [1, 512], F32)) for b in range(2)]
        psP = [psS[0][:, 0:512], psS[1][:, 0:512]]

        @blk.gpsimd
        def _(g):
            gs.inc(g.memset(ones1[:], 1.0))
            gs.inc(g.memset(ones64[:], 1.0))
            for b in range(2):
                gs.inc(g.memset(kt[b][64:65, :], 1.0))
                gs.inc(g.memset(vt[b][:].rearrange("p (a w) -> p a w", w=65)[:, :, 64:65], 1.0))
        CONST_GS = gs.n

        if causal:
            @blk.sync
            def _(sync):
                ld.inc(sync.dma_start(
                    out=cmsb[:].rearrange("p (a w) -> p a w", w=512),
                    in_=cm.rearrange("a p w -> p a w")), 16)
            CM_LD = ld.n
        else:
            CM_LD = 0

            @blk.sync
            def _(sync):
                ld.inc(sync.dma_start(
                    out=bvb[:], in_=bp.unsqueeze(0).broadcast_to(
                        [128, 1, 1024]).rearrange("p o f -> p (o f)")), 16)

        @blk.gpsimd
        def _(g):
            for c in range(8):
                lg.inc(g.dma_start(out=wpT[c][:], in_=wp[c * 128:(c + 1) * 128, :]), 16)

        st = {
            "rel_ts": [0, 0], "rel_ss": [0, 0], "rel_vs": [0, 0],
            "head_ld": [0, 0], "head_lg": [0, 0],
            "ymul_vs": 0, "evac_ss": 0,
            "nrm_mm_ts": 0, "nrm_cp_vs": 0,
        }

        def emit_loads(h):
            bf = h % 2

            @blk.sync
            def _(sync):
                sync.wait_ge(ts.h, st["rel_ts"][bf])
                sync.wait_ge(vs.h, st["rel_vs"][bf])
                lk[bf].inc(sync.dma_start(out=kt[bf][0:64, :], in_=kh[h]), 16)
                lk[bf].inc(sync.dma_start(out=qsb[bf][0:64, :], in_=qh[h]), 16)

            @blk.gpsimd
            def _(g):
                g.wait_ge(ts.h, st["rel_ts"][bf])
                g.wait_ge(gs.h, CONST_GS)
                lv[bf].inc(g.dma_start(
                    out=vt[bf][:].rearrange("p (a w) -> p a w", w=65)[:, :, 0:64],
                    in_=vh[h].rearrange("(a p) d -> p a d", p=128)), 16)
            st["head_ld"][bf] = lk[bf].n
            st["head_lg"][bf] = lv[bf].n

        def emit_head(h):
            bf = h % 2
            my_ld = st["head_ld"][bf]
            my_lg = st["head_lg"][bf]

            prev_nrm_mm = st["nrm_mm_ts"]
            prev_nrm_cp = st["nrm_cp_vs"]

            @blk.scalar
            def _(s):
                s.wait_ge(lk[bf].h, my_ld)
                s.wait_ge(ts.h, prev_nrm_mm)
                ss.inc(s.activation(ksq[:], kt[bf][0:64, :], AF.Square))
                s.wait_ge(ss.h, ss.n)
                ss.inc(s.activation(qsq[:], qsb[bf][0:64, :], AF.Square))
            sq_ss = ss.n

            nmm = {}
            ncp = {}
            for i in range(20):
                @blk.tensor
                def _(t, i=i):
                    if i == 0:
                        t.wait_ge(ss.h, sq_ss)
                        t.wait_ge(gs.h, CONST_GS)
                    if i >= 2:
                        t.wait_ge(vs.h, ncp[i - 2])
                    else:
                        t.wait_ge(vs.h, prev_nrm_cp)
                    src_ap = (ksq[:, i * 512:(i + 1) * 512] if i < 16 else
                              qsq[:, (i - 16) * 512:(i - 15) * 512])
                    nmm[i] = ts.inc(t.matmul(psN[i % 2][:], ones64[:], src_ap,
                                             start=True, stop=True))

                @blk.vector
                def _(v, i=i):
                    v.wait_ge(ts.h, nmm[i])
                    dst = (krow[0:1, i * 512:(i + 1) * 512] if i < 16 else
                           qrow[0:1, (i - 16) * 512:(i - 15) * 512])
                    ncp[i] = vs.inc(v.tensor_copy(dst, psN[i % 2][:]))
            st["nrm_mm_ts"] = ts.n
            st["nrm_cp_vs"] = vs.n

            @blk.vector
            def _(v):
                v.wait_ge(vs.h, ncp[19])
                vs.inc(v.reduce_max(km[0:1, 0:1], krow[0:1, :], axis=AX.X))
                v.wait_ge(vs.h, vs.n)
                vs.inc(v.tensor_scalar(mrow[0:1, :], qrow[0:1, :], km[0:1, 0:1],
                                       None, ALU.mult))
            m_vs = vs.n

            @blk.scalar
            def _(s):
                s.wait_ge(vs.h, m_vs)
                ss.inc(s.activation(mrow[0:1, :], mrow[0:1, :], AF.Sqrt))
            m_ss = ss.n

            @blk.vector
            def _(v):
                v.wait_ge(ss.h, m_ss)
                vs.inc(v.tensor_scalar(mne[0:1, :], mrow[0:1, :], -1.0, None,
                                       ALU.mult))
            mne_vs = vs.n

            @blk.sync
            def _(sync):
                sync.wait_ge(vs.h, mne_vs)
                lk[bf].inc(sync.dma_start(out=qsb[bf][64:65, :], in_=mne[0:1, :]), 16)
            mrow_ld = lk[bf].n

            for s_i in range(NQST):
                nch = (16 * s_i + 16) if causal else NKB
                npair = nch // 2
                smm = {}
                ex = {}
                msk = {}
                pvl = {}

                def emit_spair(p, s_i=s_i):
                    @blk.tensor
                    def _(t, p=p, s_i=s_i):
                        if s_i == 0 and p == 0:
                            t.wait_ge(lk[bf].h, mrow_ld)
                            t.wait_ge(lv[bf].h, my_lg)
                        if ("e", p - 2) in ex:
                            t.wait_ge(ss.h, ex[("e", p - 2)])
                        else:
                            t.wait_ge(ss.h, st["exp_bank"][p % 2])
                        for u in range(2):
                            c2 = 2 * p + u
                            mm = t.matmul(
                                psS[p % 2][:, u * 512:(u + 1) * 512],
                                kt[bf][:, c2 * 128:(c2 + 1) * 128],
                                qsb[bf][:, s_i * 512:(s_i + 1) * 512],
                                start=True, stop=True)
                            if u == 1:
                                smm[p] = ts.inc(mm)
                            else:
                                ts.n += 0
                                mm.then_inc(ts.h, 1)
                                ts.n += 1

                def emit_exp(p, s_i=s_i):
                    @blk.scalar
                    def _(s, p=p, s_i=s_i):
                        s.wait_ge(ts.h, smm[p])
                        ex[("e", p)] = ss.inc(s.activation(
                            ptile[p % 3][:], psS[p % 2][:], AF.Exp, scale=0.125))
                    st["exp_bank"][p % 2] = ex[("e", p)]
                    if causal and (2 * p - 16 * s_i) >= 0:
                        pb = (2 * p - 16 * s_i) // 2

                        @blk.vector
                        def _(v, p=p, pb=pb):
                            v.wait_ge(ss.h, ex[("e", p)])
                            v.wait_ge(ld.h, CM_LD)
                            msk[p] = vs.inc(v.tensor_tensor(
                                ptile[p % 3][:], ptile[p % 3][:],
                                cmsb[:, pb * 1024:(pb + 1) * 1024], ALU.mult))

                def emit_pv(p, s_i=s_i, npair=npair):
                    @blk.tensor
                    def _(t, p=p, npair=npair):
                        if p in msk:
                            t.wait_ge(vs.h, msk[p])
                        else:
                            t.wait_ge(ss.h, ex[("e", p)])
                        if p == 0:
                            t.wait_ge(vs.h, st["ymul_vs"])
                        for u in range(2):
                            c2 = 2 * p + u
                            mm = t.matmul(
                                psY[:], vt[bf][:, c2 * 65:(c2 + 1) * 65],
                                ptile[p % 3][:, u * 512:(u + 1) * 512],
                                start=(c2 == 0), stop=(c2 == 2 * npair - 1))
                            if c2 == 2 * npair - 1:
                                pvl["last"] = ts.inc(mm)

                emit_spair(0)
                emit_exp(0)
                if npair > 1:
                    emit_spair(1)
                    emit_exp(1)
                for p in range(2, npair):
                    emit_spair(p)
                    emit_exp(p)
                    emit_pv(p - 2)
                if npair > 1:
                    emit_pv(npair - 2)
                emit_pv(npair - 1)
                pv_last = pvl["last"]

                @blk.vector
                def _(v):
                    v.wait_ge(ts.h, pv_last)
                    vs.inc(v.reciprocal(rrow[0:1, :], psY[64:65, :]))
                rc_vs = vs.n

                @blk.tensor
                def _(t):
                    t.wait_ge(vs.h, rc_vs)
                    t.wait_ge(ss.h, st["evac_ss"])
                    ts.inc(t.matmul(psR[:], ones1[:], rrow[0:1, :],
                                    start=True, stop=True))
                bc_ts = ts.n

                @blk.scalar
                def _(s):
                    s.wait_ge(ts.h, bc_ts)
                    s.wait_ge(vs.h, st["ymul_vs"])
                    st["evac_ss"] = ss.inc(s.activation(rb[:], psR[:], AF.Copy))
                ev_ss = ss.n

                @blk.vector
                def _(v, s_i=s_i, h=h):
                    v.wait_ge(ss.h, ev_ss)
                    st["ymul_vs"] = vs.inc(v.tensor_tensor(
                        y8[h // 2][(h % 2) * 64:(h % 2) * 64 + 64,
                                   s_i * 512:(s_i + 1) * 512],
                        psY[0:64, :], rb[:], ALU.mult))

            st["rel_ts"][bf] = ts.n
            st["rel_ss"][bf] = ss.n
            st["rel_vs"][bf] = vs.n

        st["exp_bank"] = [0, 0]
        for h in range(NH):
            emit_loads(h)
            emit_head(h)

        ALL_Y_VS = vs.n
        LAST_EXP = max(st["exp_bank"])

        st_ld = {}
        for m in range(NT):
            @blk.sync
            def _(sync, m=m):
                if (m - 1) in st_ld:
                    sync.wait_ge(ld.h, st_ld[m - 1])
                ld.inc(sync.dma_start(out=xt[:], in_=x[m * 128:(m + 1) * 128, :]), 16)
            x_ld = ld.n

            mm_ts = {}
            psP_free = vs.n
            for half in range(2):
                @blk.tensor
                def _(t, m=m, half=half):
                    if m == 0 and half == 0:
                        t.wait_ge(vs.h, ALL_Y_VS)
                        t.wait_ge(ss.h, LAST_EXP)
                        t.wait_ge(lg.h, lg.n)
                    t.wait_ge(vs.h, psP_free)
                    for c in range(8):
                        mmi = t.matmul(psP[half][:], y8[c][:, m * 128:(m + 1) * 128],
                                       wpT[c][:, half * 512:(half + 1) * 512],
                                       start=(c == 0), stop=(c == 7))
                    mm_ts[half] = ts.inc(mmi)

            @blk.vector
            def _(v, m=m):
                v.wait_ge(ld.h, x_ld)
                for half in range(2):
                    v.wait_ge(ts.h, mm_ts[half])
                    sl = slice(half * 512, (half + 1) * 512)
                    vs.inc(v.tensor_tensor(xt[:, sl], xt[:, sl], psP[half][:],
                                           ALU.add))
                    if not causal:
                        v.wait_ge(vs.h, vs.n)
                        vs.inc(v.tensor_tensor(xt[:, sl], xt[:, sl], bvb[:, sl],
                                               ALU.add))
            add_vs = vs.n

            @blk.sync
            def _(sync, m=m):
                sync.wait_ge(vs.h, add_vs)
                st_ld[m] = ld.inc(sync.dma_start(
                    out=xo[m * 128:(m + 1) * 128, :], in_=xt[:]), 16)

        @blk.sync
        def _(sync):
            sync.wait_ge(ld.h, ld.n)
            sync.wait_ge(lg.h, lg.n)
            for b in range(2):
                sync.wait_ge(lk[b].h, lk[b].n)
                sync.wait_ge(lv[b].h, lv[b].n)
    return nc


def build_attn0():
    return _build_attn(0)


def build_attn1():
    return _build_attn(1)



_NC_CACHE = {}


def _get_nc(name):
    if name not in _NC_CACHE:
        _NC_CACHE[name] = globals()["build_" + name]()
    return _NC_CACHE[name]


def _run(name, in_maps):
    res = run_bass_kernel_spmd(_get_nc(name), in_maps, core_ids=list(range(NCORES)))
    return res.results


def run_route(x_full, wg):
    wg = np.ascontiguousarray(wg)
    return _run("route", [{"x": np.ascontiguousarray(x_full[core_rows(c)]),
                           "wg": wg} for c in range(NCORES)])


def moe_sublayer(x_full, Wg, W1, W2):
    r1 = run_route(x_full, Wg)
    meta = np.zeros((B * T, 6), np.float32)
    xn_all = np.zeros((B * T, D), BF16NP)
    tsg = np.zeros((NBLK, NE), np.float32)
    for c in range(NCORES):
        rows = core_rows(c)
        meta[rows] = r1[c]["meta"]
        xn_all[rows] = r1[c]["xn"]
        ts = r1[c]["tsum"].reshape(NT, NE)
        for i, g in enumerate(own_blocks(c)):
            tsg[g] = ts[i]
    in_maps = []
    for c in range(NCORES):
        in_maps.append({
            "xn_all": xn_all, "meta": meta, "tsg": tsg,
            "w1": np.ascontiguousarray(W1[c]),
            "w2": np.ascontiguousarray(W2[c]),
            "cid": np.full((1, 1), float(c), np.float32),
        })
    r2 = _run("moe_ffn", in_maps)
    call = np.concatenate([r2[c]["contrib"] for c in range(NCORES)]
                      + [np.zeros((1, D), np.float32)], 0)
    sall = r2[0]["sall"].reshape(128, NBLK, 2).transpose(1, 0, 2).reshape(B * T, 2)
    in_maps = []
    for c in range(NCORES):
        rows = core_rows(c)
        in_maps.append({
            "x": np.ascontiguousarray(x_full[rows]),
            "call": call,
            "sown": np.ascontiguousarray(sall[rows]),
        })
    r3 = _run("moe_combine", in_maps)
    xo = np.zeros((B * T, D), np.float32)
    for c in range(NCORES):
        xo[core_rows(c)] = r3[c]["xo"]
    return xo


def ffn1_sublayer(x_full, w_fc, w_proj, u, v, fast_b, gate):
    common = {"wfc": np.ascontiguousarray(w_fc), "wpj": np.ascontiguousarray(w_proj),
              "uu": np.ascontiguousarray(u), "vv": np.ascontiguousarray(v),
              "fb": np.ascontiguousarray(fast_b),
              "gate": np.full((1, 1), np.float32(gate), np.float32)}
    in_maps = [dict(common, x=np.ascontiguousarray(x_full[core_rows(c)]))
               for c in range(NCORES)]
    r = _run("ffn1", in_maps)
    xo = np.zeros((B * T, D), np.float32)
    for c in range(NCORES):
        xo[core_rows(c)] = r[c]["xo"]
    return xo


def _causal_mask(c):
    r = c % 4
    dk = np.arange(16)[:, None, None]
    pk = np.arange(128)[None, :, None]
    f = np.arange(512)[None, None, :]
    allow = (dk - r) * 128 + pk <= (f // 128) * 512 + (f % 128)
    return allow.astype(BF16NP)


def attn_sublayer(layer, x_full, W, bias_qkv=None, W_proj=None, bias_proj=None):
    if layer == 0:
        r1 = _run("qkv0", [{"x": np.ascontiguousarray(x_full[core_rows(c)]),
                            "wat": np.ascontiguousarray(W)} for c in range(NCORES)])
    else:
        r1 = _run("qkv1", [{"x": np.ascontiguousarray(x_full[core_rows(c)]),
                            "wat": np.ascontiguousarray(W),
                            "bqkv": np.ascontiguousarray(bias_qkv.reshape(1, -1))}
                           for c in range(NCORES)])
    K_full = np.zeros((B, NH, HS, T), BF16NP)
    V_full = np.zeros((B, NH, T, HS), BF16NP)
    for c in range(NCORES):
        b = c // 4
        pos = core_rows(c) - b * T
        K_full[b][:, :, pos] = r1[c]["ko"]
        V_full[b][:, pos, :] = r1[c]["vo"]
    in_maps = []
    for c in range(NCORES):
        b = c // 4
        wp_dev = np.ascontiguousarray(W_proj.T) if layer == 0 else np.ascontiguousarray(W_proj)
        m = {"x": np.ascontiguousarray(x_full[core_rows(c)]),
             "qh": r1[c]["qo"], "kh": np.ascontiguousarray(K_full[b]),
             "vh": np.ascontiguousarray(V_full[b]),
             "wp": wp_dev}
        if layer == 0:
            m["cm"] = _causal_mask(c)
        else:
            m["bp"] = np.ascontiguousarray(bias_proj.reshape(1, -1))
        in_maps.append(m)
    r2 = _run("attn0" if layer == 0 else "attn1", in_maps)
    xo = np.zeros((B * T, D), np.float32)
    for c in range(NCORES):
        xo[core_rows(c)] = r2[c]["xo"]
    return xo


def kernel(L0_attn_x, L0_attn_W_attn, L0_attn_W_proj, L0_attn_pos_emb,
           L0_attn_W_fc, L0_attn_W_cproj, L0_attn_ln_w, L0_ffn_Wg, L0_ffn_W1,
           L0_ffn_W2, L1_attn_w_qkv, L1_attn_b_qkv, L1_attn_w_proj,
           L1_attn_b_proj, L1_ffn_fast_b, L1_ffn_w_fc, L1_ffn_w_proj,
           L1_ffn_u, L1_ffn_v, L1_ffn_gate):
    x = np.ascontiguousarray(np.asarray(L0_attn_x, np.float32).reshape(B * T, D))
    x = moe_sublayer(x, L0_ffn_Wg, L0_ffn_W1, L0_ffn_W2)
    x = attn_sublayer(0, x, L0_attn_W_attn, W_proj=L0_attn_W_proj)
    x = moe_sublayer(x, L0_ffn_Wg, L0_ffn_W1, L0_ffn_W2)
    x = ffn1_sublayer(x, L1_ffn_w_fc, L1_ffn_w_proj, L1_ffn_u, L1_ffn_v,
                      L1_ffn_fast_b, float(L1_ffn_gate))
    x = attn_sublayer(1, x, L1_attn_w_qkv, bias_qkv=np.asarray(L1_attn_b_qkv),
                      W_proj=L1_attn_w_proj, bias_proj=np.asarray(L1_attn_b_proj))
    x = ffn1_sublayer(x, L1_ffn_w_fc, L1_ffn_w_proj, L1_ffn_u, L1_ffn_v,
                      L1_ffn_fast_b, float(L1_ffn_gate))
    return x.reshape(B, T, D)
